# version 25
# baseline (speedup 1.0000x reference)
import sys
import time
import numpy as np
from contextlib import ExitStack

import concourse.bass as bass
import concourse.mybir as mybir
from concourse.bass import AP
from concourse.bass_utils import run_bass_kernel_spmd

F32 = mybir.dt.float32
BF16 = mybir.dt.bfloat16
AF = mybir.ActivationFunctionType
ALU = mybir.AluOpType
AX = mybir.AxisListType

N = 2048
C = 1024
NH = 16
NT = N // 128
NC = C // 128

FH = 8
FD = 64
W = 512
HID = 2730
HIDP = 2816
NJ = HIDP // 128
NH_RUN = 16


def host_tables():
    import jax
    import jax.numpy as jnp
    cpu = jax.local_devices(backend="cpu")[0]
    with jax.default_device(cpu):
        inv = 1.0 / 10000.0 ** (jnp.arange(0, 128, 2, dtype=jnp.float32) / 128)
        freqs = jnp.arange(N, dtype=jnp.float32)[:, None] * inv[None, :]
        acos = np.asarray(jnp.cos(freqs), np.float32)
        asin = np.asarray(jnp.sin(freqs), np.float32)

        half_idx = jnp.arange(0, FD, 2, dtype=jnp.float32)
        inv2 = 1.0 / 10000.0 ** (half_idx / FD)
        t = jnp.arange(N, dtype=jnp.float32)
        f2 = t[:, None] * inv2[None, :]
        base = (half_idx + 0.4 * FD) / (1.4 * FD)
        power = (t - N // 2) / float(W // 2)
        scale = base[None, :] ** power[:, None]
        cos2 = np.asarray(jnp.cos(f2), np.float64)
        sin2 = np.asarray(jnp.sin(f2), np.float64)
        sc64 = np.asarray(scale, np.float64)
    csq = (cos2 * sc64).astype(np.float32)
    ssq = (sin2 * sc64).astype(np.float32)
    csk = (cos2 / sc64).astype(np.float32)
    ssk = (sin2 / sc64).astype(np.float32)
    return acos, asin, csq, ssq, csk, ssk


def host_masks():
    MNEG = np.float32(-1.0e30)
    t_idx = np.arange(4)[:, None, None]
    p_idx = np.arange(128)[None, :, None]
    kap = np.arange(1024)[None, None, :]
    rel = 128 * t_idx + p_idx
    allowed = (kap >= rel) & (kap <= rel + 512)
    import ml_dtypes
    maskw = np.where(allowed, np.float32(0), MNEG).astype(ml_dtypes.bfloat16)
    return maskw, None


def prep_shared(inp):
    d = {}
    Wqkv = np.asarray(inp["L0_attn_W_qkv"])
    Wt = np.ascontiguousarray(Wqkv.T)
    vcols = np.concatenate([np.concatenate([4 * 1024 + h * 64 + np.arange(64),
                                            5 * 1024 + h * 64 + np.arange(64)])
                            for h in range(NH)])
    order = np.concatenate([np.arange(0, 2048), np.arange(2048, 4096), vcols])
    Wre = Wt[:, order]
    Wdev = np.empty((3, 2, 8, 128, 1024), np.float32)
    for p in range(3):
        for hf in range(2):
            blk = Wre[:, p * 2048 + hf * 1024: p * 2048 + (hf + 1) * 1024]
            Wdev[p, hf] = blk.reshape(8, 128, 1024)
    d["wqkv"] = Wdev
    d["wproj"] = np.ascontiguousarray(
        np.asarray(inp["L0_attn_W_proj"]).T.reshape(NH, 128, 1024))
    d["lamq1"] = np.asarray(inp["L0_attn_lambda_q1"]).reshape(1, 64)
    d["lamq2"] = np.asarray(inp["L0_attn_lambda_q2"]).reshape(1, 64)
    d["lamk1"] = np.asarray(inp["L0_attn_lambda_k1"]).reshape(1, 64)
    d["lamk2"] = np.asarray(inp["L0_attn_lambda_k2"]).reshape(1, 64)

    d["fwqkv"] = np.ascontiguousarray(np.asarray(inp["L0_ffn_wqkv"]))
    d["fwo"] = np.ascontiguousarray(np.asarray(inp["L0_ffn_wo"]))
    w1 = np.asarray(inp["L0_ffn_w1"])
    w1a = np.zeros((2, 1024, HIDP), np.float32); w1a[:, :, :HID] = w1[:, :, :HID]
    w1g = np.zeros((2, 1024, HIDP), np.float32); w1g[:, :, :HID] = w1[:, :, HID:]
    W1dev = np.empty((2, NJ, 2, 8, 128, 128), np.float32)
    for l in range(2):
        for jj in range(NJ):
            for k in range(8):
                W1dev[l, jj, 0, k] = w1a[l, k * 128:(k + 1) * 128, jj * 128:(jj + 1) * 128]
                W1dev[l, jj, 1, k] = w1g[l, k * 128:(k + 1) * 128, jj * 128:(jj + 1) * 128]
    d["fw1"] = W1dev
    b1 = np.asarray(inp["L0_ffn_b1"])
    b1a = np.zeros((2, HIDP), np.float32); b1a[:, :HID] = b1[:, :HID]
    b1g = np.zeros((2, HIDP), np.float32); b1g[:, :HID] = b1[:, HID:]
    d["fb1a"] = np.ascontiguousarray(b1a.reshape(2, NJ, 128).transpose(0, 2, 1))
    d["fb1g"] = np.ascontiguousarray(b1g.reshape(2, NJ, 128).transpose(0, 2, 1))
    w2 = np.asarray(inp["L0_ffn_w2"])
    w2p = np.zeros((2, HIDP, 1024), np.float32); w2p[:, :HID] = w2
    W2dev = np.empty((2, 8, NJ, 128, 128), np.float32)
    for l in range(2):
        for m in range(8):
            for jj in range(NJ):
                W2dev[l, m, jj] = w2p[l, jj * 128:(jj + 1) * 128, m * 128:(m + 1) * 128]
    d["fw2"] = W2dev
    d["fb2"] = np.ascontiguousarray(
        np.asarray(inp["L0_ffn_b2"]).reshape(2, 8, 128).transpose(0, 2, 1))
    d["flng"] = np.ascontiguousarray(np.broadcast_to(
        np.asarray(inp["L0_ffn_ln_g"])[:, None, :], (2, 128, 1024)))
    d["flnb"] = np.ascontiguousarray(np.broadcast_to(
        np.asarray(inp["L0_ffn_ln_b"])[:, None, :], (2, 128, 1024)))
    d["fqs"] = np.ascontiguousarray(np.broadcast_to(
        np.asarray(inp["L0_ffn_q_scale"])[:, None, :], (2, 128, 64)))
    d["fks"] = np.ascontiguousarray(np.broadcast_to(
        np.asarray(inp["L0_ffn_k_scale"])[:, None, :], (2, 128, 64)))

    acos, asin, csq, ssq, csk, ssk = host_tables()
    d["acos"], d["asin"] = acos, asin
    d["csq"], d["ssq"], d["csk"], d["ssk"] = csq, ssq, csk, ssk
    maskw, _ = host_masks()
    d["maskw"] = maskw
    d["ident"] = np.eye(128, dtype=np.float32)
    d["onesr"] = np.ones((1, 128), np.float32)
    d["onesc"] = np.ones((128, 1), np.float32)
    cvals = np.zeros(8, np.float64)
    cvals[0] = 1e-5
    cvals[1] = 1e-12
    cvals[2] = 0.8 - 0.6 * np.exp(-0.3 * 0)
    cvals[3] = 1.0 / 1024.0
    d["consts"] = np.broadcast_to(cvals.astype(np.float32), (128, 8)).copy()
    return d


class Sems:
    def __init__(self, nc, es, names):
        self.h = {}
        self.c = {}
        for n_ in names:
            self.h[n_] = es.enter_context(nc.semaphore(n_))
            self.c[n_] = 0

    def inc(self, ins, name, n=1):
        self.c[name] += n
        ins.then_inc(self.h[name], n)
        return self.c[name]

    def wait(self, eng, name, val=None):
        v = self.c[name] if val is None else val
        if v > 0:
            eng.wait_ge(self.h[name], v)
        return v


SEM_NAMES = ["WL", "SP", "PE", "ACT", "DVE", "GP"]


def build():
    nc = bass.Bass("TRN2", target_bir_lowering=False)
    t0 = time.time()

    din = {}
    def dri(name, shape, dtype=F32):
        din[name] = nc.dram_tensor(name, list(shape), dtype, kind="ExternalInput").ap()
        return din[name]

    dri("xT", (C, N))
    wqkv = dri("wqkv", (3, 2, 8, 128, 1024))
    dri("wproj", (NH, 128, 1024))
    for nm in ["lamq1", "lamq2", "lamk1", "lamk2"]:
        dri(nm, (1, 64))
    dri("acos", (N, 64)); dri("asin", (N, 64))
    dri("csq", (N, 32)); dri("ssq", (N, 32)); dri("csk", (N, 32)); dri("ssk", (N, 32))
    dri("maskw", (4, 128, 1024), BF16)
    dri("ident", (128, 128)); dri("onesr", (1, 128)); dri("onesc", (128, 1))
    dri("consts", (128, 8))
    dri("fwqkv", (2, C, 1536)); dri("fwo", (2, 512, C))
    dri("fw1", (2, NJ, 2, 8, 128, 128)); dri("fb1a", (2, 128, NJ)); dri("fb1g", (2, 128, NJ))
    dri("fw2", (2, 8, NJ, 128, 128)); dri("fb2", (2, 128, 8))
    dri("flng", (2, 128, C)); dri("flnb", (2, 128, C))
    dri("fqs", (2, 128, 64)); dri("fks", (2, 128, 64))

    out = nc.dram_tensor("out", [N, C], F32, kind="ExternalOutput").ap()

    qT_sp = nc.dram_tensor("qT_sp", [NH, 128, N], F32, kind="Internal").ap()
    kT_sp = nc.dram_tensor("kT_sp", [NH, 128, N], F32, kind="Internal").ap()
    v_sp = nc.dram_tensor("v_sp", [NH, N, 128], F32, kind="Internal").ap()
    xin_sp = nc.dram_tensor("xin_sp", [C, N], F32, kind="Internal").ap()

    es = ExitStack()
    sb = lambda n_, shp, dt_=F32: es.enter_context(nc.sbuf_tensor(n_, shp, dt_))
    S = Sems(nc, es, SEM_NAMES)

    xs = sb("xs", [128, NC * N])
    ident = sb("identt", [128, 128])
    onesr = sb("onesrt", [1, 128])
    onesc = sb("onesct", [128, 1])
    consts = sb("constst", [128, 8])
    lam_b = sb("lam_b", [128, 1])

    with es, nc.Block() as block:
        lam_in = {nm: sb("t_" + nm, [1, 64]) for nm in
                  ["lamq1", "lamk1", "lamq2", "lamk2"]}
        tmp1 = sb("tmp1", [1, 64])
        tmp2 = sb("tmp2", [1, 4])
        ph_s = es.enter_context(nc.psum_tensor("ph_s", [128, 512], F32))

        def ph0(sync):
            ins = sync.dma_start(out=ident[:], in_=din["ident"][:, :]); S.inc(ins, "WL", 16)
            ins = sync.dma_start(out=onesr[:], in_=din["onesr"][:, :]); S.inc(ins, "WL", 16)
            ins = sync.dma_start(out=onesc[:], in_=din["onesc"][:, :]); S.inc(ins, "WL", 16)
            ins = sync.dma_start(out=consts[:], in_=din["consts"][:, :]); S.inc(ins, "WL", 16)
            ins = sync.dma_start(
                out=xs[:].rearrange("p (c n) -> p c n", n=N),
                in_=din["xT"][:, :].rearrange("(c p) n -> p c n", p=128))
            S.inc(ins, "WL", 16)
            for nm in ["lamq1", "lamk1", "lamq2", "lamk2"]:
                ins = sync.dma_start(out=lam_in[nm][:, :], in_=din[nm][:, :])
                S.inc(ins, "WL", 16)
        block.sync(ph0)
        wl_pro = S.c["WL"]

        def lam_dve(vector):
            vector.wait_ge(S.h["WL"], wl_pro)
            vector.tensor_tensor(out=tmp1[:, :], in0=lam_in["lamq1"][:, :],
                                 in1=lam_in["lamk1"][:, :], op=ALU.mult)
            vector.tensor_reduce(out=tmp2[:, 0:1], in_=tmp1[:, :], axis=AX.X, op=ALU.add)
            vector.tensor_tensor(out=tmp1[:, :], in0=lam_in["lamq2"][:, :],
                                 in1=lam_in["lamk2"][:, :], op=ALU.mult)
            ins = vector.tensor_reduce(out=tmp2[:, 1:2], in_=tmp1[:, :], axis=AX.X, op=ALU.add)
            S.inc(ins, "DVE")
        block.vector(lam_dve)

        def lam_act(scalar):
            S.wait(scalar, "DVE")
            scalar.activation(out=tmp2[:, 2:3], in_=tmp2[:, 0:1], func=AF.Exp)
            ins = scalar.activation(out=tmp2[:, 3:4], in_=tmp2[:, 1:2], func=AF.Exp)
            S.inc(ins, "ACT")
        block.scalar(lam_act)

        def lam_dve2(vector):
            S.wait(vector, "ACT")
            ins = vector.tensor_tensor(out=tmp2[:, 0:1], in0=tmp2[:, 2:3], in1=tmp2[:, 3:4],
                                       op=ALU.subtract)
            v = S.inc(ins, "DVE")
            vector.wait_ge(S.h["DVE"], v)
            ins = vector.tensor_tensor(out=tmp2[:, 0:1], in0=tmp2[:, 0:1],
                                       in1=consts[0:1, 2:3], op=ALU.add)
            S.inc(ins, "DVE")
        block.vector(lam_dve2)

        def lam_pe(tensor):
            S.wait(tensor, "DVE")
            ins = tensor.matmul(ph_s[:, 0:1], onesr[:, :], tmp2[0:1, 0:1],
                                start=True, stop=True)
            S.inc(ins, "PE")
        block.tensor(lam_pe)

        def lam_act2(scalar):
            S.wait(scalar, "PE")
            ins = scalar.copy(lam_b[:], ph_s[:, 0:1])
            S.inc(ins, "ACT")
        block.scalar(lam_act2)

        esA = ExitStack()
        sbA = lambda n_, shp, dt_=F32: esA.enter_context(nc.sbuf_tensor(n_, shp, dt_))
        tacos = sbA("tacos", [128, NT * 64])
        tasin = sbA("tasin", [128, NT * 64])
        wstrip = [sbA(f"wstrip{i}", [128, 8 * 1024]) for i in range(2)]
        stg = sbA("stg", [128, 2048])
        rst = sbA("rst", [128, 2048])
        rtm = sbA("rtm", [128, 1024])
        spT = sbA("spT", [128, 512])
        ph_a = es.enter_context(nc.psum_tensor("ph_a", [128, 2048], F32))
        ph_t = es.enter_context(nc.psum_tensor("ph_t", [128, 512], F32))

        def phA_tables(sync):
            for dst, src in [(tacos, din["acos"]), (tasin, din["asin"])]:
                ins = sync.dma_start(
                    out=dst[:].rearrange("p (t e) -> p t e", e=64),
                    in_=src[:, :].rearrange("(t p) e -> p t e", p=128))
                S.inc(ins, "WL", 16)
        block.sync(phA_tables)

        strip_wl = {}
        strip_pe_done = {}
        for i0 in range(2):
            def load_strip(sync, i=i0):
                p, hf = divmod(i, 2)
                ins = sync.dma_start(
                    out=wstrip[i % 2][:].rearrange("p (k x) -> p k x", x=1024),
                    in_=wqkv[p, hf].transpose([1, 0, 2]))
                strip_wl[i] = S.inc(ins, "WL", 16)
            block.sync(load_strip)

        last_evac = {"q0": 0, "q1": 0}
        last_rot = 0
        last_stg_user = 0
        last_tevac = 0
        last_spill = 0
        last_vspill = 0

        for pair in range(3):
            for nt in range(NT):
                def mm_step(tensor, pair=pair, nt=nt,
                            w0=last_evac["q0"], w1=last_evac["q1"],
                            lrot=last_rot):
                    i = pair * 2
                    tensor.wait_ge(S.h["WL"], max(strip_wl[i], strip_wl[i + 1]))
                    if w0:
                        tensor.wait_ge(S.h["ACT"], max(w0, w1))
                    for hf in range(2):
                        buf = wstrip[(pair * 2 + hf) % 2]
                        for cc in range(8):
                            for q5 in range(2):
                                ins = tensor.matmul(
                                    ph_a[:, hf * 1024 + q5 * 512: hf * 1024 + (q5 + 1) * 512],
                                    xs[:, cc * N + nt * 128: cc * N + (nt + 1) * 128],
                                    buf[:, cc * 1024 + q5 * 512: cc * 1024 + (q5 + 1) * 512],
                                    start=(cc == 0), stop=(cc == 7))
                    S.inc(ins, "PE")
                block.tensor(mm_step)
                mm_pe = S.c["PE"]
                if nt == NT - 1:
                    strip_pe_done[pair * 2] = mm_pe
                    strip_pe_done[pair * 2 + 1] = mm_pe
                    if pair < 2:
                        for hf2 in range(2):
                            def load_next(sync, pair=pair, hf2=hf2, gate=mm_pe):
                                i = (pair + 1) * 2 + hf2
                                sync.wait_ge(S.h["PE"], strip_pe_done[i - 2])
                                p2, h2 = divmod(i, 2)
                                ins = sync.dma_start(
                                    out=wstrip[i % 2][:].rearrange("p (k x) -> p k x", x=1024),
                                    in_=wqkv[p2, h2].transpose([1, 0, 2]))
                                strip_wl[i] = S.inc(ins, "WL", 16)
                            block.sync(load_next)

                def evac_step(scalar, pair=pair, nt=nt, gate=mm_pe,
                              lrot=last_rot, lvs=last_vspill):
                    scalar.wait_ge(S.h["PE"], gate)
                    if pair < 2:
                        if lrot:
                            scalar.wait_ge(S.h["DVE"], lrot)
                        dst = stg
                    else:
                        if lvs:
                            scalar.wait_ge(S.h["SP"], lvs)
                        dst = rst
                    scalar.activation(out=dst[:, 0:1024], in_=ph_a[:, 0:1024], func=AF.Copy)
                    ins = scalar.activation(out=dst[:, 1024:2048], in_=ph_a[:, 1024:2048],
                                            func=AF.Copy)
                    S.inc(ins, "ACT")
                block.scalar(evac_step)
                ev_act = S.c["ACT"]
                last_evac["q0"] = ev_act
                last_evac["q1"] = ev_act

                if pair < 2:
                    def rot_step(vector, nt=nt, gate=ev_act, ltev=last_tevac):
                        vector.wait_ge(S.h["ACT"], gate)
                        if ltev:
                            pass
                        cb = tacos[:, nt * 64:(nt + 1) * 64].unsqueeze(1).broadcast_to([128, 16, 64])
                        sbb = tasin[:, nt * 64:(nt + 1) * 64].unsqueeze(1).broadcast_to([128, 16, 64])
                        qa = stg[:, 0:1024].rearrange("p (h e) -> p h e", e=64)
                        qb = stg[:, 1024:2048].rearrange("p (h e) -> p h e", e=64)
                        r3 = rst[:].rearrange("p (h d) -> p h d", d=128)
                        t3 = rtm[:].rearrange("p (h e) -> p h e", e=64)
                        vector.tensor_tensor(out=t3, in0=qa, in1=cb, op=ALU.mult)
                        vector.tensor_tensor(out=r3[:, :, 0:64], in0=qb, in1=sbb, op=ALU.mult)
                        vector.tensor_tensor(out=r3[:, :, 0:64], in0=t3, in1=r3[:, :, 0:64],
                                             op=ALU.subtract)
                        vector.tensor_tensor(out=t3, in0=qb, in1=cb, op=ALU.mult)
                        vector.tensor_tensor(out=r3[:, :, 64:128], in0=qa, in1=sbb, op=ALU.mult)
                        ins = vector.tensor_tensor(out=r3[:, :, 64:128], in0=t3,
                                                   in1=r3[:, :, 64:128], op=ALU.add)
                        S.inc(ins, "DVE")
                    rot_pe_gate = S.c["PE"]

                    def rot_step2(vector, gate_pe=rot_pe_gate):
                        vector.wait_ge(S.h["PE"], gate_pe)
                    block.vector(rot_step2)
                    block.vector(rot_step)
                    last_rot = S.c["DVE"]

                    for g in range(4):
                        def t_step(tensor, g=g, rgate=last_rot, egate=last_tevac):
                            tensor.wait_ge(S.h["DVE"], rgate)
                            if egate:
                                tensor.wait_ge(S.h["ACT"], egate)
                            for j in range(4):
                                hh = g * 4 + j
                                ins = tensor.matmul(
                                    ph_t[:, j * 128:(j + 1) * 128],
                                    rst[:, hh * 128:(hh + 1) * 128],
                                    ident[:, :],
                                    start=(j == 0), stop=(j == 3), is_transpose=True)
                            S.inc(ins, "PE")
                        block.tensor(t_step)
                        t_pe = S.c["PE"]

                        def tevac_step(scalar, gate=t_pe, sgate=last_spill):
                            scalar.wait_ge(S.h["PE"], gate)
                            if sgate:
                                scalar.wait_ge(S.h["SP"], sgate)
                            ins = scalar.activation(out=spT[:], in_=ph_t[:], func=AF.Copy)
                            S.inc(ins, "ACT")
                        block.scalar(tevac_step)
                        last_tevac = S.c["ACT"]

                        def spill_step(gpsimd, pair=pair, nt=nt, g=g, gate=last_tevac):
                            gpsimd.wait_ge(S.h["ACT"], gate)
                            dst_all = qT_sp if pair == 0 else kT_sp
                            dst = dst_all[4 * g:4 * g + 4, :, nt * 128:(nt + 1) * 128]
                            ins = gpsimd.dma_start(
                                out=dst.transpose([1, 0, 2]),
                                in_=spT[:].rearrange("p (j n) -> p j n", n=128))
                            S.inc(ins, "SP", 16)
                        block.gpsimd(spill_step)
                        last_spill = S.c["SP"]
                else:
                    def vspill_step(gpsimd, nt=nt, gate=ev_act):
                        gpsimd.wait_ge(S.h["ACT"], gate)
                        dst = v_sp[:, nt * 128:(nt + 1) * 128, :]
                        ins = gpsimd.dma_start(
                            out=dst.transpose([1, 0, 2]),
                            in_=rst[:].rearrange("p (h d) -> p h d", d=128))
                        S.inc(ins, "SP", 16)
                    block.gpsimd(vspill_step)
                    last_vspill = S.c["SP"]


        phA_end = {k: S.c[k] for k in SEM_NAMES}
        esA.close()

        ph_pv = es.enter_context(nc.psum_tensor("ph_pv", [128, 512], F32))
        ph_pj = es.enter_context(nc.psum_tensor("ph_pj", [128, 512], F32))
        esB = ExitStack()
        sbB = lambda n_, shp, dt_=F32: esB.enter_context(nc.sbuf_tensor(n_, shp, dt_))
        qkb = [sbB(f"qkb{i}", [128, 2048]) for i in range(2)]
        kkb = [sbB(f"kkb{i}", [128, 2048]) for i in range(2)]
        vkb = [sbB(f"vkb{i}", [128, 2048]) for i in range(2)]
        wpb = [sbB("wpb0", [128, 1024])]
        wpb.append(wpb[0])
        P1u = sbB("P1u", [128, 2048])
        P2u = sbB("P2u", [128, 2048])
        Pc = [sbB(f"Pc_{i}", [128, 2048]) for i in range(2)]
        PT = sbB("PT", [128, 8192])
        outT = sbB("outT", [128, 2048])
        sqb = sbB("sqb", [128, 512])
        negc = [sbB(f"negc{i}", [128, 1]) for i in range(2)]
        acc1r = [sbB(f"acc1_{i}", [128, 4]) for i in range(2)]
        acc2r = [sbB(f"acc2_{i}", [128, 4]) for i in range(2)]
        s1t = sbB("s1t_", [128, 1]); s2t = sbB("s2t_", [128, 1])
        inv1 = sbB("inv1", [128, 1]); inv2 = sbB("inv2", [128, 1])
        m2n = sbB("m2n", [128, 1])
        nrm = sbB("nrm", [1, 10])

        head_load_wl = {}
        wpb_wl = {}
        wpb_gate = {}
        wpb_pending = {}
        head_pe_end = {}
        exp_done = {}
        t_pe_done = {}
        comb_done = {}
        tevac_done = {}
        pv_mm_done = {}
        pv_evac_done = {}
        pjadd_done = {}
        negc_done = {}
        cchain = {}

        def emit_head_loads(h):
            def loads(sync, h=h):
                if h >= 2:
                    sync.wait_ge(S.h["PE"], head_pe_end[h - 2])
                else:
                    for k in ["SP", "PE", "ACT", "DVE"]:
                        if phA_end[k]:
                            sync.wait_ge(S.h[k], phA_end[k])
                b = h % 2
                ins = sync.dma_start(out=qkb[b][:], in_=qT_sp[h, :, :]); S.inc(ins, "WL", 16)
                ins = sync.dma_start(out=kkb[b][:], in_=kT_sp[h, :, :]); S.inc(ins, "WL", 16)
                ins = sync.dma_start(
                    out=vkb[b][:].rearrange("p (kc d) -> p kc d", d=128),
                    in_=v_sp[h].rearrange("(kc p) d -> p kc d", p=128))
                S.inc(ins, "WL", 16)
                head_load_wl[h] = S.c["WL"]
            block.sync(loads)

            def wpb_load(sync, h=h, gate=None):
                if h >= 1:
                    sync.wait_ge(S.h["PE"], wpb_gate[h])
                ins = sync.dma_start(
                    out=wpb[0][:].rearrange("p (cc x) -> p cc x", x=128),
                    in_=din["wproj"][h].rearrange("p (cc x) -> p cc x", x=128))
                wpb_wl[h] = S.inc(ins, "WL", 16)
            if h == 0:
                wpb_gate[0] = 0
                block.sync(wpb_load)
            else:
                wpb_pending[h] = wpb_load

        def cchain_steps(h):
            b = h % 2
            last_sq_pe = {"v": 0}

            for side in range(2):
                for g in range(4):
                    def nsq(vector, h=h, b=b, side=side, g=g):
                        if side == 0 and g == 0:
                            vector.wait_ge(S.h["WL"], head_load_wl[h])
                        S.wait(vector, "PE")
                        buf = qkb[b] if side == 0 else kkb[b]
                        ins = vector.tensor_tensor(out=sqb[:],
                                                   in0=buf[:, g * 512:(g + 1) * 512],
                                                   in1=buf[:, g * 512:(g + 1) * 512],
                                                   op=ALU.mult)
                        S.inc(ins, "DVE")
                    yield ("vector", nsq)
                    def nmm(tensor, g=g):
                        S.wait(tensor, "DVE")
                        ins = tensor.matmul(ph_s[0:1, :], onesc[:, :], sqb[:],
                                            start=True, stop=True)
                        S.inc(ins, "PE")
                    yield ("tensor", nmm)
                    def nred(vector, side=side, g=g):
                        S.wait(vector, "PE")
                        ins = vector.reduce_max(out=nrm[:, side * 4 + g:side * 4 + g + 1],
                                                in_=ph_s[0:1, :], axis=AX.X)
                        S.inc(ins, "DVE")
                    yield ("vector", nred)

            def fin(vector):
                vector.reduce_max(out=nrm[:, 8:9], in_=nrm[:, 0:4], axis=AX.X)
                ins = vector.reduce_max(out=nrm[:, 9:10], in_=nrm[:, 4:8], axis=AX.X)
                v = S.inc(ins, "DVE")
                vector.wait_ge(S.h["DVE"], v)
                ins = vector.tensor_tensor(out=nrm[:, 8:9], in0=nrm[:, 8:9],
                                           in1=nrm[:, 9:10], op=ALU.mult)
                S.inc(ins, "DVE")
            yield ("vector", fin)

            def fin2(scalar):
                S.wait(scalar, "DVE")
                ins = scalar.activation(out=nrm[:, 9:10], in_=nrm[:, 8:9], func=AF.Ln)
                v = S.inc(ins, "ACT")
                scalar.wait_ge(S.h["ACT"], v)
                ins = scalar.activation(out=nrm[:, 8:9], in_=nrm[:, 9:10], func=AF.Exp,
                                        scale=0.5)
                S.inc(ins, "ACT")
            yield ("scalar", fin2)

            def fin3(vector):
                S.wait(vector, "ACT")
                ins = vector.tensor_scalar(out=nrm[:, 8:9], in0=nrm[:, 8:9],
                                           scalar1=-0.125, scalar2=None, op0=ALU.mult)
                S.inc(ins, "DVE")
            yield ("vector", fin3)

            def bca(tensor):
                S.wait(tensor, "DVE")
                ins = tensor.matmul(ph_s[:, 0:1], onesr[:, :], nrm[0:1, 8:9],
                                    start=True, stop=True)
                S.inc(ins, "PE")
            yield ("tensor", bca)

            def bcc(vector, h=h):
                S.wait(vector, "PE")
                ins = vector.tensor_copy(out=negc[h % 2][:], in_=ph_s[:, 0:1])
                negc_done[h] = S.inc(ins, "DVE")
            yield ("vector", bcc)

        def run_steps(gen, nsteps):
            n = 0
            for kind, f in gen:
                getattr(block, kind)(f)
                n += 1
                if n >= nsteps:
                    break

        emit_head_loads(0)
        g0 = cchain_steps(0)
        run_steps(g0, 100)
        for h in range(NH_RUN):
            b = h % 2
            if h + 1 < NH_RUN:
                emit_head_loads(h + 1)
                gnext = cchain_steps(h + 1)
            else:
                gnext = None

            def emit_Tpv(qt, h=h, b=b):
                    qg, qr = divmod(qt, 4)
                    for g in range(4):
                        def tmm(tensor, h=h, qt=qt, g=g, qg=qg):
                            tensor.wait_ge(S.h["DVE"], comb_done[(h, qt)])
                            pk = (h, qt, g - 1) if g > 0 else \
                                 ((h, qt - 1, 3) if qt > 0 else
                                  ((h - 1, NT - 1, 3) if h > 0 else None))
                            if pk is not None and pk in tevac_done:
                                tensor.wait_ge(S.h["DVE"], tevac_done[pk])
                            pvk = (h, qg - 1) if qg > 0 else ((h - 1, 3) if h > 0 else None)
                            if pvk is not None and pvk in pv_mm_done and g == 0:
                                tensor.wait_ge(S.h["PE"], pv_mm_done[pvk])
                            pq = qt % 2
                            for j in range(4):
                                kc = g * 4 + j
                                ins = tensor.matmul(ph_t[:, j * 128:(j + 1) * 128],
                                                    Pc[pq][:, kc * 128:(kc + 1) * 128],
                                                    ident[:, :],
                                                    start=(j == 0), stop=(j == 3),
                                                    is_transpose=True)
                            S.inc(ins, "PE")
                        block.tensor(tmm)
                        t_pe = S.c["PE"]
                        t_pe_done[(h, qt, g)] = t_pe

                        def tev(vector, h=h, qt=qt, g=g, qg=qg, qr=qr, gate=t_pe):
                            vector.wait_ge(S.h["PE"], gate)
                            ins = vector.tensor_copy(
                                out=PT[:].rearrange("p (kc q) -> p kc q", q=512)
                                    [:, 4 * g:4 * g + 4, qr * 128:(qr + 1) * 128],
                                in_=ph_t[:].rearrange("p (j n) -> p j n", n=128))
                            tevac_done[(h, qt, g)] = S.inc(ins, "DVE")
                        block.vector(tev)

                    if qr == 3:
                        def pv(tensor, h=h, b=b, qg=qg):
                            for qq in range(4):
                                tensor.wait_ge(S.h["DVE"], tevac_done[(h, qg * 4 + qq, 3)])
                            pk = (h, qg - 1) if qg > 0 else ((h - 1, 3) if h > 0 else None)
                            if pk is not None and pk in pv_evac_done:
                                tensor.wait_ge(S.h["DVE"], pv_evac_done[pk])
                            if pk is not None and (pk[0], pk[1], 7) in pjadd_done:
                                tensor.wait_ge(S.h["DVE"], pjadd_done[(pk[0], pk[1], 7)])
                            for kc in range(16):
                                ins = tensor.matmul(ph_pv[:, :],
                                                    vkb[b][:, kc * 128:(kc + 1) * 128],
                                                    PT[:, kc * 512:(kc + 1) * 512],
                                                    start=(kc == 0), stop=(kc == 15))
                            pv_mm_done[(h, qg)] = S.inc(ins, "PE")
                        block.tensor(pv)
                        pv_pe = S.c["PE"]

                        def pvev(vector, h=h, qg=qg, gate=pv_pe):
                            vector.wait_ge(S.h["PE"], gate)
                            ins = vector.tensor_copy(out=outT[:, qg * 512:(qg + 1) * 512],
                                                     in_=ph_pv[:, :])
                            pv_evac_done[(h, qg)] = S.inc(ins, "DVE")
                        block.vector(pvev)

                        for cc in range(8):
                            def pj(tensor, h=h, b=b, qg=qg, cc=cc):
                                if cc == 0:
                                    tensor.wait_ge(S.h["DVE"], pv_evac_done[(h, qg)])
                                bank = ph_pj if cc % 2 == 0 else ph_pv
                                if cc >= 2:
                                    tensor.wait_ge(S.h["DVE"], pjadd_done[(h, qg, cc - 2)])
                                elif cc == 0:
                                    pk2 = (h, qg - 1) if qg > 0 else ((h - 1, 3) if h > 0 else None)
                                    if pk2 is not None and (pk2[0], pk2[1], 6) in pjadd_done:
                                        tensor.wait_ge(S.h["DVE"], pjadd_done[(pk2[0], pk2[1], 6)])
                                if cc == 0 and qg == 0:
                                    tensor.wait_ge(S.h["WL"], wpb_wl[h])
                                ins = tensor.matmul(bank[:, :],
                                                    wpb[0][:, cc * 128:(cc + 1) * 128],
                                                    outT[:, qg * 512:(qg + 1) * 512],
                                                    start=True, stop=True)
                                S.inc(ins, "PE")
                            block.tensor(pj)
                            pj_pe = S.c["PE"]

                            def pjadd(vector, h=h, qg=qg, cc=cc, gate=pj_pe):
                                vector.wait_ge(S.h["PE"], gate)
                                bank = ph_pj if cc % 2 == 0 else ph_pv
                                dst = xs[:, cc * N + qg * 512: cc * N + (qg + 1) * 512]
                                ins = vector.scalar_tensor_tensor(
                                    out=dst, in0=bank[:, :], scalar=1.0, in1=dst,
                                    op0=ALU.mult, op1=ALU.add)
                                pjadd_done[(h, qg, cc)] = S.inc(ins, "DVE")
                            block.vector(pjadd)

            for qt in range(NT):
                act_base = S.c["ACT"]
                exp_pred = {ks: act_base + ks + 1 for ks in range(4)}
                s_pe_ks = {}
                for ks in range(4):
                    bp = ks % 2
                    def smm(tensor, h=h, b=b, qt=qt, ks=ks, bp=bp, ep=exp_pred):
                        if qt == 0 and ks == 0:
                            tensor.wait_ge(S.h["WL"], head_load_wl[h])
                        key = (h, qt - 1, ks + 2) if qt > 0 else \
                              ((h - 1, NT - 1, ks + 2) if h > 0 else None)
                        if ks >= 2:
                            tensor.wait_ge(S.h["ACT"], ep[ks - 2])
                        elif key is not None and key in exp_done:
                            tensor.wait_ge(S.h["ACT"], exp_done[key])
                        lo = bp * 1024
                        tensor.matmul(ph_a[:, lo:lo + 512],
                                      qkb[b][0:64, qt * 128:(qt + 1) * 128],
                                      kkb[b][0:64, ks * 512:(ks + 1) * 512],
                                      start=True, stop=True)
                        ins = tensor.matmul(ph_a[:, lo + 512:lo + 1024],
                                            qkb[b][64:128, qt * 128:(qt + 1) * 128],
                                            kkb[b][64:128, ks * 512:(ks + 1) * 512],
                                            start=True, stop=True,
                                            tile_position=(64, 0))
                        S.inc(ins, "PE")
                    block.tensor(smm)
                    s_pe_ks[ks] = S.c["PE"]

                assert S.c["ACT"] == act_base, "no ACT emissions between smm and exp body"

                def sexp(scalar, h=h, qt=qt, gates=dict(s_pe_ks)):
                    ckey = (h, qt - 1) if qt >= 1 else \
                        ((h - 1, NT - 1) if h > 0 else None)
                    if ckey is not None and ckey in comb_done:
                        scalar.wait_ge(S.h["DVE"], comb_done[ckey])
                    if qt == 0 and h in negc_done:
                        scalar.wait_ge(S.h["DVE"], negc_done[h])
                    pq = qt % 2
                    for ks in range(4):
                        bp = ks % 2
                        lo = bp * 1024
                        scalar.wait_ge(S.h["PE"], gates[ks])
                        scalar.activation(out=P1u[:, ks * 512:(ks + 1) * 512],
                                          in_=ph_a[:, lo:lo + 512], func=AF.Exp,
                                          bias=negc[h % 2][:], scale=0.125,
                                          accum_out=acc1r[pq][:, ks:ks + 1])
                        ins = scalar.activation(out=P2u[:, ks * 512:(ks + 1) * 512],
                                                in_=ph_a[:, lo + 512:lo + 1024], func=AF.Exp,
                                                bias=negc[h % 2][:], scale=0.125,
                                                accum_out=acc2r[pq][:, ks:ks + 1])
                        exp_done[(h, qt, ks)] = S.inc(ins, "ACT")
                block.scalar(sexp)
                assert exp_done[(h, qt, 3)] == exp_pred[3]

                def comb_a(vector, h=h, qt=qt):
                    pq = qt % 2
                    vector.wait_ge(S.h["ACT"], exp_done[(h, qt, 3)])
                    vector.tensor_reduce(out=s1t[:], in_=acc1r[pq][:], axis=AX.X, op=ALU.add)
                    ins = vector.tensor_reduce(out=s2t[:], in_=acc2r[pq][:], axis=AX.X, op=ALU.add)
                    v = S.inc(ins, "DVE")
                    vector.wait_ge(S.h["DVE"], v)
                    vector.reciprocal(out=inv1[:], in_=s1t[:])
                    ins = vector.reciprocal(out=inv2[:], in_=s2t[:])
                    v = S.inc(ins, "DVE")
                    vector.wait_ge(S.h["DVE"], v)
                    ins = vector.tensor_scalar(out=m2n[:], in0=inv2[:], scalar1=lam_b[:, 0:1],
                                               scalar2=-1.0, op0=ALU.mult, op1=ALU.mult)
                    S.inc(ins, "DVE")
                block.vector(comb_a)
                m2_dve = S.c["DVE"]

                def sscale(scalar, gate=m2_dve):
                    scalar.wait_ge(S.h["DVE"], gate)
                    scalar.activation(out=P1u[:], in_=P1u[:], func=AF.Identity,
                                      scale=inv1[:, 0:1], bias=0.0)
                    ins = scalar.activation(out=P2u[:], in_=P2u[:], func=AF.Identity,
                                            scale=m2n[:, 0:1], bias=0.0)
                    S.inc(ins, "ACT")
                block.scalar(sscale)
                sc_act = S.c["ACT"]

                def comb(vector, h=h, qt=qt, gate=sc_act):
                    pq = qt % 2
                    vector.wait_ge(S.h["ACT"], gate)
                    tk = (h, qt - 2, 3) if qt >= 2 else ((h - 1, NT - 2 + qt, 3) if h > 0 else None)
                    if tk is not None and tk in t_pe_done:
                        vector.wait_ge(S.h["PE"], t_pe_done[tk])
                    ins = vector.tensor_tensor(out=Pc[pq][:], in0=P2u[:], in1=P1u[:],
                                               op=ALU.add)
                    comb_done[(h, qt)] = S.inc(ins, "DVE")
                block.vector(comb)

                if qt >= 1:
                    emit_Tpv(qt - 1)

            emit_Tpv(NT - 1)
            if gnext is not None:
                run_steps(gnext, 100)
            head_pe_end[h] = S.c["PE"]
            if (h + 1) in wpb_pending:
                wpb_gate[h + 1] = S.c["PE"]
                block.sync(wpb_pending.pop(h + 1))




        phB_end = {k: S.c[k] for k in SEM_NAMES}
        esB.close()
        ln_y = sb("ln_y", [128, 1024])
        ln_sq = sb("ln_sq", [128, 1024])
        ln_s = sb("ln_s", [128, 6])
        ln_state = {"prev_back_act": 0, "prev_apply_pe": 0, "first_gate": None}

        def emit_ln_tile(nt, gamma_beta=None, write_back=True, out_rows=None,
                         src_fn=None, back_dst_fn=None):
            def fwd(tensor, nt=nt):
                if ln_state["first_gate"] is not None:
                    g = ln_state["first_gate"]
                    ln_state["first_gate"] = None
                    tensor.wait_ge(S.h["DVE"], g[0])
                    tensor.wait_ge(S.h["ACT"], g[1])
                if ln_state["prev_back_act"]:
                    tensor.wait_ge(S.h["ACT"], ln_state["prev_back_act"])
                if ln_state.get("prev_fwd_evac", 0):
                    tensor.wait_ge(S.h["ACT"], ln_state["prev_fwd_evac"])
                for cc in range(8):
                    src = src_fn(cc) if src_fn is not None else \
                        xs[:, cc * N + nt * 128: cc * N + (nt + 1) * 128]
                    ins = tensor.matmul(ph_a[:, cc * 128:(cc + 1) * 128], src,
                                        ident[:, :],
                                        start=(cc % 4 == 0), stop=(cc % 4 == 3),
                                        is_transpose=True)
                S.inc(ins, "PE")
            block.tensor(fwd)
            fwd_pe = S.c["PE"]

            def evac(scalar, gate=fwd_pe, osp=ln_state.get("out_sp", 0)):
                scalar.wait_ge(S.h["PE"], gate)
                if osp:
                    scalar.wait_ge(S.h["SP"], osp)
                scalar.activation(out=ln_y[:], in_=ph_a[:, 0:1024], func=AF.Copy,
                                  accum_out=ln_s[:, 0:1])
                ins = scalar.activation(out=ln_sq[:], in_=ln_y[:], func=AF.Square,
                                        accum_out=ln_s[:, 1:2])
                S.inc(ins, "ACT")
            block.scalar(evac)
            ev_act = S.c["ACT"]
            ln_state["prev_fwd_evac"] = ev_act

            def stats(vector, gate=ev_act):
                vector.wait_ge(S.h["ACT"], gate)
                vector.tensor_scalar(out=ln_s[:, 2:3], in0=ln_s[:, 0:1],
                                     scalar1=1.0 / 1024.0, scalar2=None, op0=ALU.mult)
                ins = vector.tensor_scalar(out=ln_s[:, 3:4], in0=ln_s[:, 1:2],
                                           scalar1=1.0 / 1024.0, scalar2=None, op0=ALU.mult)
                v = S.inc(ins, "DVE")
                vector.wait_ge(S.h["DVE"], v)
                ins = vector.tensor_tensor(out=ln_s[:, 5:6], in0=ln_s[:, 2:3],
                                           in1=ln_s[:, 2:3], op=ALU.mult)
                v = S.inc(ins, "DVE")
                vector.wait_ge(S.h["DVE"], v)
                ins = vector.tensor_tensor(out=ln_s[:, 3:4], in0=ln_s[:, 3:4],
                                           in1=ln_s[:, 5:6], op=ALU.subtract)
                S.inc(ins, "DVE")
            block.vector(stats)
            st_dve = S.c["DVE"]

            def sqrt_step(scalar, gate=st_dve):
                scalar.wait_ge(S.h["DVE"], gate)
                ins = scalar.activation(out=ln_s[:, 3:4], in_=ln_s[:, 3:4], func=AF.Sqrt,
                                        bias=consts[:, 0:1])
                S.inc(ins, "ACT")
            block.scalar(sqrt_step)
            sq_act = S.c["ACT"]

            def rstep(vector, gate=sq_act):
                vector.wait_ge(S.h["ACT"], gate)
                ins = vector.reciprocal(out=ln_s[:, 4:5], in_=ln_s[:, 3:4])
                v = S.inc(ins, "DVE")
                vector.wait_ge(S.h["DVE"], v)
                ins = vector.tensor_scalar(out=ln_s[:, 5:6], in0=ln_s[:, 2:3],
                                           scalar1=ln_s[:, 4:5], scalar2=-1.0,
                                           op0=ALU.mult, op1=ALU.mult)
                S.inc(ins, "DVE")
            block.vector(rstep)
            r_dve = S.c["DVE"]

            def apply(scalar, gate=r_dve, pgate=ln_state["prev_apply_pe"]):
                scalar.wait_ge(S.h["DVE"], gate)
                if pgate:
                    scalar.wait_ge(S.h["PE"], pgate)
                ins = scalar.activation(out=ln_y[:], in_=ln_y[:], func=AF.Identity,
                                        scale=ln_s[:, 4:5], bias=ln_s[:, 5:6])
                S.inc(ins, "ACT")
            block.scalar(apply)
            if gamma_beta is not None:
                gmt, bt = gamma_beta
                def gb(vector):
                    S.wait(vector, "ACT")
                    vector.tensor_tensor(out=ln_y[:], in0=ln_y[:], in1=gmt, op=ALU.mult)
                    ins = vector.tensor_tensor(out=ln_y[:], in0=ln_y[:], in1=bt, op=ALU.add)
                    S.inc(ins, "DVE")
                block.vector(gb)
            ap_act, ap_dve = S.c["ACT"], S.c["DVE"]

            if write_back:
                def back(tensor, nt=nt, ga=ap_act, gd=ap_dve, gb_=gamma_beta):
                    tensor.wait_ge(S.h["ACT"], ga)
                    if gb_ is not None:
                        tensor.wait_ge(S.h["DVE"], gd)
                    for cc in range(8):
                        ins = tensor.matmul(ph_a[:, 1024 + (cc % 4) * 128:1024 + (cc % 4) * 128 + 128],
                                            ln_y[:, cc * 128:(cc + 1) * 128],
                                            ident[:, :],
                                            start=True, stop=True,
                                            is_transpose=True)
                        S.inc(ins, "PE")
                        g_pe = S.c["PE"]
                        def bevac(scalar, cc=cc, nt=nt, gate=g_pe):
                            scalar.wait_ge(S.h["PE"], gate)
                            ins = scalar.activation(
                                out=xs[:, cc * N + nt * 128: cc * N + (nt + 1) * 128],
                                in_=ph_a[:, 1024 + (cc % 4) * 128:1024 + (cc % 4) * 128 + 128],
                                func=AF.Copy)
                            S.inc(ins, "ACT")
                        return ins, bevac
                bevac_acts = {}
                for cc in range(8):
                    def bt_step(tensor, nt=nt, cc=cc, ga=ap_act, gd=ap_dve, gb_=gamma_beta):
                        if cc == 0:
                            tensor.wait_ge(S.h["ACT"], ga)
                            if gb_ is not None:
                                tensor.wait_ge(S.h["DVE"], gd)
                        if cc >= 4:
                            tensor.wait_ge(S.h["ACT"], bevac_acts[cc - 4])
                        ins = tensor.matmul(
                            ph_a[:, 1024 + (cc % 4) * 128:1024 + (cc % 4) * 128 + 128],
                            ln_y[:, cc * 128:(cc + 1) * 128], ident[:, :],
                            start=True, stop=True, is_transpose=True)
                        S.inc(ins, "PE")
                    block.tensor(bt_step)
                    g_pe = S.c["PE"]
                    def bevac(scalar, cc=cc, nt=nt, gate=g_pe, bdf=back_dst_fn):
                        scalar.wait_ge(S.h["PE"], gate)
                        dst = bdf(cc) if bdf is not None else \
                            xs[:, cc * N + nt * 128: cc * N + (nt + 1) * 128]
                        ins = scalar.activation(
                            out=dst,
                            in_=ph_a[:, 1024 + (cc % 4) * 128:1024 + (cc % 4) * 128 + 128],
                            func=AF.Copy)
                        S.inc(ins, "ACT")
                    block.scalar(bevac)
                    bevac_acts[cc] = S.c["ACT"]
                ln_state["prev_back_act"] = S.c["ACT"]
                ln_state["prev_apply_pe"] = S.c["PE"]
            if out_rows is not None:
                def odma(gpsimd, nt=nt, ga=ap_act, gd=ap_dve, gb_=gamma_beta):
                    gpsimd.wait_ge(S.h["ACT"], ga)
                    if gb_ is not None:
                        gpsimd.wait_ge(S.h["DVE"], gd)
                    ins = gpsimd.dma_start(out=out_rows, in_=ln_y[:])
                    S.inc(ins, "SP", 16)
                block.gpsimd(odma)
                ln_state["prev_apply_pe"] = 0
                ln_state["prev_back_act"] = 0
                ln_state["out_sp"] = S.c["SP"]

        ln_state["first_gate"] = (phB_end["DVE"], phB_end["ACT"])
        for nt in range(NT):
            emit_ln_tile(nt)

        def xin_dma(sync):
            S.wait(sync, "ACT")
            ins = sync.dma_start(
                out=xin_sp[:, :].rearrange("(c p) n -> p c n", p=128),
                in_=xs[:].rearrange("p (c n) -> p c n", n=N))
            S.inc(ins, "WL", 16)
        block.sync(xin_dma)





        def phase_barrier(tag):
            cur = {k: S.c[k] for k in SEM_NAMES}
            def bsync(sync):
                for k in ["PE", "ACT", "DVE", "SP", "GP"]:
                    if cur[k]:
                        sync.wait_ge(S.h[k], cur[k])
            block.sync(bsync)
            def bact(scalar):
                for k in ["PE", "DVE", "WL", "SP"]:
                    if cur[k]:
                        scalar.wait_ge(S.h[k], cur[k])
            block.scalar(bact)
            def bdve(vector):
                for k in ["PE", "ACT", "WL", "SP"]:
                    if cur[k]:
                        vector.wait_ge(S.h[k], cur[k])
            block.vector(bdve)
            def bpe(tensor):
                for k in ["ACT", "DVE", "WL", "SP"]:
                    if cur[k]:
                        tensor.wait_ge(S.h[k], cur[k])
            block.tensor(bpe)
            def bgp(gpsimd):
                for k in ["PE", "ACT", "DVE", "WL"]:
                    if cur[k]:
                        gpsimd.wait_ge(S.h[k], cur[k])
            block.gpsimd(bgp)
            return cur

        for layer in range(2):
            phase_barrier(f"ffa{layer}")
            esFa = ExitStack()
            sbF = lambda n_, shp, dt_=F32: esFa.enter_context(
                nc.sbuf_tensor(n_ + f"_l{layer}", shp, dt_))
            wstrips = [sbF(f"fws{i}", [128, 4096]) for i in range(2)]
            maskw_t = sbF("maskw_t", [128, 4096], BF16)
            tcsq = sbF("tcsq", [128, 512]); tssq = sbF("tssq", [128, 512])
            tcsk = sbF("tcsk", [128, 512]); tssk = sbF("tssk", [128, 512])
            fqs_t = sbF("fqs_t", [128, 64]); fks_t = sbF("fks_t", [128, 64])
            fwo_t = sbF("fwo_t", [128, 4096])
            qTr = sbF("qTr", [128, 2048])
            kTr = sbF("kTr", [128, 4096])
            vr = sbF("vr", [128, 4096])
            qtk = sbF("qtk", [128, 512])
            sql2 = sbF("sql2", [128, 512])
            rtmp = sbF("rtmp", [128, 512])
            ssq8 = sbF("ssq8", [128, 8]); invq = sbF("invq", [128, 8])
            Pf = sbF("Pf", [128, 1024])
            PTf = sbF("PTf", [128, 1024])
            oTf = sbF("oTf", [128, 2048])
            fmx = sbF("fmx", [128, 1]); frb = sbF("frb", [128, 1])
            fsum = sbF("fsum", [128, 1]); finv = sbF("finv", [128, 1])
            h0t = sbF("h0t", [128, 512])
            stash = sbF("stash", [128, 16])

            def ffa_loads(sync, layer=layer):
                ins = sync.dma_start(
                    out=maskw_t[:].rearrange("p (q x) -> p q x", x=1024),
                    in_=din["maskw"][:, :, :].transpose([1, 0, 2]))
                S.inc(ins, "WL", 16)
                for dst, nm in [(tcsq, "csq"), (tssq, "ssq"), (tcsk, "csk"), (tssk, "ssk")]:
                    ins = sync.dma_start(
                        out=dst[:].rearrange("p (t e) -> p t e", e=32),
                        in_=din[nm][:, :].rearrange("(t p) e -> p t e", p=128))
                    S.inc(ins, "WL", 16)
                ins = sync.dma_start(out=fqs_t[:], in_=din["fqs"][layer]); S.inc(ins, "WL", 16)
                ins = sync.dma_start(out=fks_t[:], in_=din["fks"][layer]); S.inc(ins, "WL", 16)
                ins = sync.dma_start(
                    out=fwo_t[:].rearrange("p (pr c) -> p pr c", c=1024),
                    in_=din["fwo"][layer].rearrange("(pr p) c -> p pr c", p=128))
                S.inc(ins, "WL", 16)
                ins = sync.dma_start(
                    out=wstrips[0][:].rearrange("p (cc x) -> p cc x", x=512),
                    in_=din["fwqkv"][layer][:, 0:512].rearrange("(cc p) x -> p cc x", p=128))
                S.inc(ins, "WL", 16)
            block.sync(ffa_loads)
            strip_wl_f = {0: S.c["WL"]}

            def ffa_prep(vector, layer=layer):
                S.wait(vector, "WL")
                vector.memset(h0t[:], 0.0)
                for i, cc in enumerate(range(4, 8)):
                    vector.tensor_copy(out=h0t[:, i * 128 + 1:(i + 1) * 128],
                                       in_=xs[:, cc * N: cc * N + 127])
                for w_ in range(1, 4):
                    for i, cc in enumerate(range(4, 8)):
                        ins = vector.tensor_copy(
                            out=stash[:, (w_ - 1) * 4 + i:(w_ - 1) * 4 + i + 1],
                            in_=xs[:, cc * N + w_ * 512 - 1: cc * N + w_ * 512])
                S.inc(ins, "DVE")
            block.vector(ffa_prep)
            prep_dve = S.c["DVE"]

            def shifted_lhs(cc, gt):
                if cc < 4:
                    return xs[:, cc * N + gt * 128: cc * N + (gt + 1) * 128]
                if gt == 0:
                    return h0t[:, (cc - 4) * 128:(cc - 4) * 128 + 128]
                return xs[:, cc * N + gt * 128 - 1: cc * N + (gt + 1) * 128 - 1]

            wtile = sbF("wtile", [128, 512])

            gate = {"strip_pe": {}, "exp_act": {}, "tev_act": 0, "pvv_act": 0,
                    "qkv_evac": 0, "wo_dve": 0, "sc_exp": {}, "exp_last": {0: 0, 1: 0},
                    "nrm_last": 0, "pf_pe": 0}

            for w_ in range(4):
                wpar = w_ % 2
                wprev = (w_ - 1) % 2
                if w_ >= 1:
                    def mk_wtile(vector, w_=w_, gd=prep_dve):
                        S.wait(vector, "PE")
                        for i, cc in enumerate(range(4, 8)):
                            vector.tensor_copy(
                                out=wtile[:, i * 128:i * 128 + 1],
                                in_=stash[:, (w_ - 1) * 4 + i:(w_ - 1) * 4 + i + 1])
                            ins = vector.tensor_copy(
                                out=wtile[:, i * 128 + 1:(i + 1) * 128],
                                in_=xs[:, cc * N + w_ * 512: cc * N + w_ * 512 + 127])
                        S.inc(ins, "DVE")
                    block.vector(mk_wtile)
                mk_dve = S.c["DVE"]

                def shifted_lhs_w(cc, nt, w_=w_):
                    gt = 4 * w_ + nt
                    if cc >= 4 and nt == 0:
                        if w_ == 0:
                            return h0t[:, (cc - 4) * 128:(cc - 4) * 128 + 128]
                        return wtile[:, (cc - 4) * 128:(cc - 4) * 128 + 128]
                    return shifted_lhs(cc, gt)

                for sg in range(3):
                    pidx = w_ * 3 + sg
                    buf = wstrips[pidx % 2]
                    if pidx + 1 < 12:
                        def pf(sync, pidx=pidx, layer=layer):
                            nsg = (pidx + 1) % 3
                            if pidx - 1 >= 0 and (pidx - 1) in gate["strip_pe"]:
                                sync.wait_ge(S.h["PE"], gate["strip_pe"][pidx - 1])
                            ins = sync.dma_start(
                                out=wstrips[(pidx + 1) % 2][:].rearrange("p (cc x) -> p cc x", x=512),
                                in_=din["fwqkv"][layer][:, nsg * 512:(nsg + 1) * 512]
                                    .rearrange("(cc p) x -> p cc x", p=128))
                            strip_wl_f[pidx + 1] = S.inc(ins, "WL", 16)
                        block.sync(pf)

                    for nt in range(4):
                        gt = 4 * w_ + nt
                        def qkvmm(tensor, sg=sg, nt=nt, w_=w_, buf=buf, pidx=pidx,
                                  gmk=mk_dve):
                            tensor.wait_ge(S.h["WL"], strip_wl_f[pidx])
                            tensor.wait_ge(S.h["DVE"], gmk)
                            if gate["qkv_evac"]:
                                tensor.wait_ge(S.h["DVE"], gate["qkv_evac"])
                            if gate["wo_dve"]:
                                tensor.wait_ge(S.h["DVE"], gate["wo_dve"])
                            for cc in range(8):
                                ins = tensor.matmul(
                                    ph_pv[:, :], shifted_lhs_w(cc, nt),
                                    buf[:, cc * 512:(cc + 1) * 512],
                                    start=(cc == 0), stop=(cc == 7))
                            S.inc(ins, "PE")
                        block.tensor(qkvmm)
                        if nt == 3:
                            gate["strip_pe"][pidx] = S.c["PE"]
                        mm_pe = S.c["PE"]

                        if sg == 2:
                            def vev(vector, nt=nt, wpar=wpar, gate_pe=mm_pe):
                                vector.wait_ge(S.h["PE"], gate_pe)
                                ins = vector.tensor_copy(
                                    out=vr[:, wpar * 2048 + nt * 512: wpar * 2048 + (nt + 1) * 512],
                                    in_=ph_pv[:, :])
                                gate["qkv_evac"] = S.inc(ins, "DVE")
                            block.vector(vev)
                            continue

                        def qev(vector, gate_pe=mm_pe):
                            vector.wait_ge(S.h["PE"], gate_pe)
                            ins = vector.tensor_copy(out=qtk[:], in_=ph_pv[:, :])
                            gate["qkv_evac"] = S.inc(ins, "DVE")
                            ins = vector.tensor_tensor(out=sql2[:], in0=qtk[:], in1=qtk[:],
                                                       op=ALU.mult)
                            S.inc(ins, "DVE")
                        block.vector(qev)
                        qe_act = S.c["DVE"]

                        def l2x(vector, sg=sg, gt=gt, gate_a=qe_act):
                            vector.wait_ge(S.h["ACT"], gate_a)
                            ins = vector.tensor_reduce(
                                out=ssq8[:], in_=sql2[:].rearrange("p (h e) -> p h e", e=64),
                                axis=AX.X, op=ALU.add)
                            v = S.inc(ins, "DVE")
                            vector.wait_ge(S.h["DVE"], v)
                            S.inc(ins, "DVE", 0)
                            return
                        def l2a(vector, sg=sg, gt=gt):
                            ins = vector.tensor_reduce(
                                out=ssq8[:], in_=sql2[:].rearrange("p (h e) -> p h e", e=64),
                                axis=AX.X, op=ALU.add)
                            v = S.inc(ins, "DVE")
                            vector.wait_ge(S.h["DVE"], v)
                            ins = vector.tensor_scalar(out=ssq8[:], in0=ssq8[:],
                                                       scalar1=1e-24, scalar2=None,
                                                       op0=ALU.max)
                            S.inc(ins, "DVE")
                        block.vector(l2a)
                        l2_dve = S.c["DVE"]

                        def l2b(scalar, gate_d=l2_dve):
                            scalar.wait_ge(S.h["DVE"], gate_d)
                            ins = scalar.activation(out=invq[:], in_=ssq8[:], func=AF.Ln)
                            v = S.inc(ins, "ACT")
                            scalar.wait_ge(S.h["ACT"], v)
                            ins = scalar.activation(out=invq[:], in_=invq[:], func=AF.Exp,
                                                    scale=-0.5)
                            S.inc(ins, "ACT")
                        block.scalar(l2b)
                        l2b_act = S.c["ACT"]

                        def l2c(vector, sg=sg, gt=gt, gate_a=l2b_act):
                            vector.wait_ge(S.h["ACT"], gate_a)
                            vector.tensor_tensor(
                                out=qtk[:].rearrange("p (h e) -> p h e", e=64),
                                in0=qtk[:].rearrange("p (h e) -> p h e", e=64),
                                in1=invq[:].unsqueeze(2).broadcast_to([128, 8, 64]),
                                op=ALU.mult)
                            sc = fqs_t if sg == 0 else fks_t
                            vector.tensor_tensor(
                                out=qtk[:].rearrange("p (h e) -> p h e", e=64),
                                in0=qtk[:].rearrange("p (h e) -> p h e", e=64),
                                in1=sc[:].unsqueeze(1).broadcast_to([128, 8, 64]),
                                op=ALU.mult)
                            cs = (tcsq if sg == 0 else tcsk)[:, gt * 32:(gt + 1) * 32]
                            ss = (tssq if sg == 0 else tssk)[:, gt * 32:(gt + 1) * 32]
                            csb = cs.unsqueeze(1).broadcast_to([128, 8, 32])
                            ssb = ss.unsqueeze(1).broadcast_to([128, 8, 32])
                            q3 = qtk[:].rearrange("p (h e) -> p h e", e=64)
                            r3 = rtmp[:].rearrange("p (h e) -> p h e", e=64)
                            vector.tensor_tensor(out=r3[:, :, 0:32], in0=q3[:, :, 0:32],
                                                 in1=csb, op=ALU.mult)
                            vector.tensor_tensor(out=sql2[:].rearrange("p (h e) -> p h e", e=64)[:, :, 0:32],
                                                 in0=q3[:, :, 32:64], in1=ssb, op=ALU.mult)
                            vector.tensor_tensor(out=r3[:, :, 32:64], in0=q3[:, :, 32:64],
                                                 in1=csb, op=ALU.mult)
                            vector.tensor_tensor(out=sql2[:].rearrange("p (h e) -> p h e", e=64)[:, :, 32:64],
                                                 in0=q3[:, :, 0:32], in1=ssb, op=ALU.mult)
                            s3 = sql2[:].rearrange("p (h e) -> p h e", e=64)
                            vector.tensor_tensor(out=r3[:, :, 0:32], in0=r3[:, :, 0:32],
                                                 in1=s3[:, :, 0:32], op=ALU.subtract)
                            ins = vector.tensor_tensor(out=r3[:, :, 32:64], in0=r3[:, :, 32:64],
                                                       in1=s3[:, :, 32:64], op=ALU.add)
                            S.inc(ins, "DVE")
                        block.vector(l2c)
                        rot_dve = S.c["DVE"]

                        def qkT(tensor, gate_d=rot_dve):
                            tensor.wait_ge(S.h["DVE"], gate_d)
                            if gate["tev_act"]:
                                tensor.wait_ge(S.h["DVE"], gate["tev_act"])
                            for pr in range(4):
                                ins = tensor.matmul(ph_t[:, pr * 128:(pr + 1) * 128],
                                                    rtmp[:, pr * 128:(pr + 1) * 128],
                                                    ident[:, :], start=(pr == 0),
                                                    stop=(pr == 3), is_transpose=True)
                            S.inc(ins, "PE")
                        block.tensor(qkT)
                        t_pe = S.c["PE"]

                        def qkTev(vector, sg=sg, nt=nt, wpar=wpar, gate_pe=t_pe):
                            vector.wait_ge(S.h["PE"], gate_pe)
                            if sg == 0:
                                dst = qTr[:].rearrange("p (pr x) -> p pr x", x=512)[
                                    :, :, nt * 128:(nt + 1) * 128]
                            else:
                                dst = kTr[:].rearrange("p (pr x) -> p pr x", x=1024)[
                                    :, :, wpar * 512 + nt * 128: wpar * 512 + (nt + 1) * 128]
                            ins = vector.tensor_copy(
                                out=dst, in_=ph_t[:].rearrange("p (pr n) -> p pr n", n=128))
                            gate["tev_act"] = S.inc(ins, "DVE")
                        block.vector(qkTev)

                nks = 1 if w_ == 0 else 2
                for pr in range(4):
                    pv_started = False
                    for qt in range(4):
                        for e_ in range(2):
                            def sc_mm(tensor, pr=pr, qt=qt, e_=e_, w_=w_, nks=nks,
                                      gexp=None):
                                if gate["exp_last"][e_]:
                                    tensor.wait_ge(S.h["ACT"], gate["exp_last"][e_])
                                lo = e_ * 1024
                                for ks in range(nks):
                                    wp = (w_ - 1 + ks) % 2 if w_ >= 1 else w_ % 2
                                    ins = tensor.matmul(
                                        ph_a[:, lo + ks * 512: lo + (ks + 1) * 512],
                                        qTr[0 + e_ * 64: 64 + e_ * 64,
                                            pr * 512 + qt * 128: pr * 512 + (qt + 1) * 128],
                                        kTr[0 + e_ * 64: 64 + e_ * 64,
                                            pr * 1024 + wp * 512: pr * 1024 + (wp + 1) * 512],
                                        start=True, stop=True,
                                        tile_position=(e_ * 64, 0))
                                S.inc(ins, "PE")
                            block.tensor(sc_mm)
                            sc_pe = S.c["PE"]

                            def sc_sm(vector, pr=pr, qt=qt, e_=e_, w_=w_, nks=nks,
                                      gate_pe=sc_pe):
                                vector.wait_ge(S.h["PE"], gate_pe)
                                lo = e_ * 1024
                                nk = nks * 512
                                if w_ == 0:
                                    mk = maskw_t[:, qt * 1024 + 512: qt * 1024 + 1024]
                                else:
                                    mk = maskw_t[:, qt * 1024: qt * 1024 + nk]
                                vector.tensor_tensor(out=ph_a[:, lo:lo + nk],
                                                     in0=ph_a[:, lo:lo + nk],
                                                     in1=mk, op=ALU.add)
                                ins = vector.reduce_max(out=fmx[:], in_=ph_a[:, lo:lo + nk],
                                                        axis=AX.X)
                                v = S.inc(ins, "DVE")
                                vector.wait_ge(S.h["DVE"], v)
                                ins = vector.tensor_scalar(out=frb[:], in0=fmx[:],
                                                           scalar1=-8.0, scalar2=None,
                                                           op0=ALU.mult)
                                S.inc(ins, "DVE")
                            block.vector(sc_sm)
                            sm_dve = S.c["DVE"]

                            def sc_exp(scalar, pr=pr, qt=qt, e_=e_, nks=nks, gate_d=sm_dve,
                                       gnrm=None, gpf=None):
                                scalar.wait_ge(S.h["DVE"], gate_d)
                                if gate["nrm_last"]:
                                    scalar.wait_ge(S.h["DVE"], gate["nrm_last"])
                                if gate["pf_pe"]:
                                    scalar.wait_ge(S.h["PE"], gate["pf_pe"])
                                lo = e_ * 1024
                                nk = nks * 512
                                ins = scalar.activation(out=Pf[:, 0:nk], in_=ph_a[:, lo:lo + nk],
                                                        func=AF.Exp, bias=frb[:], scale=8.0,
                                                        accum_out=fsum[:])
                                gate["exp_last"][e_] = S.inc(ins, "ACT")
                            block.scalar(sc_exp)
                            exp_act = S.c["ACT"]

                            def sc_nrm(vector, nks=nks, gate_a=exp_act):
                                vector.wait_ge(S.h["ACT"], gate_a)
                                ins = vector.reciprocal(out=finv[:], in_=fsum[:])
                                v = S.inc(ins, "DVE")
                                vector.wait_ge(S.h["DVE"], v)
                                ins = vector.tensor_scalar(out=Pf[:, 0:nks * 512],
                                                           in0=Pf[:, 0:nks * 512],
                                                           scalar1=finv[:, 0:1],
                                                           scalar2=None, op0=ALU.mult)
                                gate["nrm_last"] = S.inc(ins, "DVE")
                            block.vector(sc_nrm)
                            nrm_dve = S.c["DVE"]

                            def p_T(tensor, nks=nks, gate_d=nrm_dve):
                                tensor.wait_ge(S.h["DVE"], gate_d)
                                if gate["tev_act"]:
                                    tensor.wait_ge(S.h["ACT"], gate["tev_act"])
                                for j in range(4 * nks):
                                    bank = j % 4
                                    if j == 4:
                                        pass
                                    ins = tensor.matmul(ph_t[:, bank * 128:(bank + 1) * 128],
                                                        Pf[:, j * 128:(j + 1) * 128],
                                                        ident[:, :], start=(bank == 0),
                                                        stop=(bank == 3), is_transpose=True)
                                    if bank == 3:
                                        S.inc(ins, "PE")
                                        gpe = S.c["PE"]
                                        g0 = j - 3
                                        def pTev(scalar, g0=g0, gate_pe=gpe):
                                            scalar.wait_ge(S.h["PE"], gate_pe)
                                            ins = scalar.activation(
                                                out=PTf[:, g0 * 128:(g0 + 4) * 128],
                                                in_=ph_t[:], func=AF.Copy)
                                            gate["tev_act"] = S.inc(ins, "ACT")
                                        block.scalar(pTev)
                            for grp in range(nks):
                                def pT_g(tensor, grp=grp, nks=nks, gate_d=nrm_dve):
                                    tensor.wait_ge(S.h["DVE"], gate_d)
                                    if gate["tev_act"]:
                                        tensor.wait_ge(S.h["DVE"], gate["tev_act"])
                                    for j in range(4):
                                        kc = grp * 4 + j
                                        ins = tensor.matmul(
                                            ph_t[:, j * 128:(j + 1) * 128],
                                            Pf[:, kc * 128:(kc + 1) * 128],
                                            ident[:, :], start=(j == 0), stop=(j == 3),
                                            is_transpose=True)
                                    S.inc(ins, "PE")
                                block.tensor(pT_g)
                                gpe = S.c["PE"]

                                def pTev_g(vector, grp=grp, gate_pe=gpe):
                                    vector.wait_ge(S.h["PE"], gate_pe)
                                    ins = vector.tensor_copy(
                                        out=PTf[:, grp * 512:(grp + 1) * 512],
                                        in_=ph_t[:])
                                    gate["tev_act"] = S.inc(ins, "DVE")
                                block.vector(pTev_g)

                            ptev_act = S.c["DVE"]

                            gate["pf_pe"] = S.c["PE"]
                            def pv_mm(tensor, pr=pr, qt=qt, e_=e_, w_=w_, nks=nks,
                                      gate_a=ptev_act, first=(qt == 0 and e_ == 0)):
                                tensor.wait_ge(S.h["DVE"], gate_a)
                                if first and gate["pvv_act"]:
                                    tensor.wait_ge(S.h["DVE"], gate["pvv_act"])
                                h_ = pr * 2 + e_
                                nkc = 4 * nks
                                for kc in range(nkc):
                                    ks, kt = divmod(kc, 4)
                                    wp = (w_ - 1 + ks) % 2 if w_ >= 1 else w_ % 2
                                    ins = tensor.matmul(
                                        ph_pj[e_ * 64:(e_ + 1) * 64, qt * 128:(qt + 1) * 128],
                                        vr[:, wp * 2048 + kt * 512 + e_ * 64 + (pr * 128 if False else 0) + (h_ // 2) * 0 + (h_ % 2) * 0 + (pr * 2 + e_) * 64 - e_ * 64: wp * 2048 + kt * 512 + (pr * 2 + e_) * 64 - e_ * 64 + 64] if False else
                                        vr[:, wp * 2048 + kt * 512 + h_ * 64: wp * 2048 + kt * 512 + h_ * 64 + 64],
                                        PTf[:, kc * 128:(kc + 1) * 128],
                                        start=(kc == 0 and qt == 0),
                                        stop=(kc == nkc - 1 and qt == 3),
                                        tile_position=(0, e_ * 64))
                                S.inc(ins, "PE")
                            block.tensor(pv_mm)

                    pv_pe = S.c["PE"]
                    def pv_ev(vector, pr=pr, gate_pe=pv_pe):
                        vector.wait_ge(S.h["PE"], gate_pe)
                        ins = vector.tensor_copy(out=oTf[:, pr * 512:(pr + 1) * 512],
                                                 in_=ph_pj[:, :])
                        gate["pvv_act"] = S.inc(ins, "DVE")
                    block.vector(pv_ev)

                wo_act = S.c["DVE"]
                for cc in range(8):
                    def wo_mm(tensor, cc=cc, gate_a=wo_act):
                        tensor.wait_ge(S.h["DVE"], gate_a)
                        if cc >= 1 and gate["wo_dve"]:
                            tensor.wait_ge(S.h["DVE"], gate["wo_dve"] - 1 + 1 - 1) if False else None
                        if cc >= 1:
                            tensor.wait_ge(S.h["DVE"], gate["wo_dve"])
                        for pr in range(4):
                            ins = tensor.matmul(ph_pv[:, :],
                                                fwo_t[:, pr * 1024 + cc * 128: pr * 1024 + (cc + 1) * 128],
                                                oTf[:, pr * 512:(pr + 1) * 512],
                                                start=(pr == 0), stop=(pr == 3))
                        S.inc(ins, "PE")
                    block.tensor(wo_mm)
                    wo_pe = S.c["PE"]

                    def wo_add(vector, cc=cc, w_=w_, gate_pe=wo_pe):
                        vector.wait_ge(S.h["PE"], gate_pe)
                        dst = xs[:, cc * N + w_ * 512: cc * N + (w_ + 1) * 512]
                        ins = vector.scalar_tensor_tensor(out=dst, in0=ph_pv[:, :],
                                                          scalar=1.0, in1=dst,
                                                          op0=ALU.mult, op1=ALU.add)
                        gate["wo_dve"] = S.inc(ins, "DVE")
                    block.vector(wo_add)

            esFa.close()

            phase_barrier(f"mlp{layer}")
            esFm = ExitStack()
            sbM = lambda n_, shp, dt_=F32: esFm.enter_context(
                nc.sbuf_tensor(n_ + f"_l{layer}", shp, dt_))
            flng_t = sbM("flng_t", [128, 1024]); flnb_t = sbM("flnb_t", [128, 1024])
            fb1a_t = sbM("fb1a_t", [128, NJ]); fb1g_t = sbM("fb1g_t", [128, NJ])
            fb2_t = sbM("fb2_t", [128, 8])
            h2nT = sbM("h2nT", [128, 8 * 512])
            actT = sbM("actT", [128, NJ * 512])
            glt = sbM("glt", [128, 512]); att = sbM("att", [128, 512])
            w1b = [sbM(f"w1b{i}", [128, 2048]) for i in range(2)]
            w2b = [sbM(f"w2b{i}", [128, NJ * 128]) for i in range(2)]
            mstash = sbM("mstash", [128, 16])
            mh0 = sbM("mh0", [128, 512])
            mwt = sbM("mwt", [128, 512])

            def mlp_loads(sync, layer=layer):
                for dst, nm in [(flng_t, "flng"), (flnb_t, "flnb"), (fb1a_t, "fb1a"),
                                (fb1g_t, "fb1g"), (fb2_t, "fb2")]:
                    ins = sync.dma_start(out=dst[:], in_=din[nm][layer])
                    S.inc(ins, "WL", 16)
            block.sync(mlp_loads)
            mlp_wl = S.c["WL"]

            def mlp_prep(vector, layer=layer):
                vector.wait_ge(S.h["WL"], mlp_wl)
                vector.memset(mh0[:], 0.0)
                for i, cc in enumerate(range(4, 8)):
                    vector.tensor_copy(out=mh0[:, i * 128 + 1:(i + 1) * 128],
                                       in_=xs[:, cc * N: cc * N + 127])
                for s_ in range(1, 4):
                    for i, cc in enumerate(range(4, 8)):
                        ins = vector.tensor_copy(
                            out=mstash[:, (s_ - 1) * 4 + i:(s_ - 1) * 4 + i + 1],
                            in_=xs[:, cc * N + s_ * 512 - 1: cc * N + s_ * 512])
                S.inc(ins, "DVE")
            block.vector(mlp_prep)
            mprep_dve = S.c["DVE"]

            mgate = {"w1_pe": {}, "w2_pe": {}, "act_j": 0, "w2_stt": 0, "w1_wl": {},
                     "w2_wl": {}}

            for s_ in range(4):
                if s_ >= 1:
                    def mk_mwt(vector, s_=s_):
                        S.wait(vector, "PE")
                        for i, cc in enumerate(range(4, 8)):
                            vector.tensor_copy(
                                out=mwt[:, i * 128:i * 128 + 1],
                                in_=mstash[:, (s_ - 1) * 4 + i:(s_ - 1) * 4 + i + 1])
                            ins = vector.tensor_copy(
                                out=mwt[:, i * 128 + 1:(i + 1) * 128],
                                in_=xs[:, cc * N + s_ * 512: cc * N + s_ * 512 + 127])
                        S.inc(ins, "DVE")
                    block.vector(mk_mwt)
                mwt_dve = S.c["DVE"]

                def msrc(cc, ntr, s_=s_):
                    gt = 4 * s_ + ntr
                    if cc >= 4 and ntr == 0:
                        return mh0[:, (cc - 4) * 128:(cc - 4) * 128 + 128] if s_ == 0 else \
                            mwt[:, (cc - 4) * 128:(cc - 4) * 128 + 128]
                    if cc < 4:
                        return xs[:, cc * N + gt * 128: cc * N + (gt + 1) * 128]
                    return xs[:, cc * N + gt * 128 - 1: cc * N + (gt + 1) * 128 - 1]

                ln_state["first_gate"] = (S.c["DVE"], S.c["ACT"])
                for ntr in range(4):
                    emit_ln_tile(4 * s_ + ntr, gamma_beta=(flng_t[:], flnb_t[:]),
                                 write_back=True,
                                 src_fn=(lambda cc, ntr=ntr: msrc(cc, ntr)),
                                 back_dst_fn=(lambda cc, ntr=ntr:
                                              h2nT[:, cc * 512 + ntr * 128:
                                                   cc * 512 + (ntr + 1) * 128]))
                h2n_act = S.c["ACT"]

                for jj in range(NJ):
                    def w1_load(sync, jj=jj, s_=s_, layer=layer):
                        if jj >= 2 and (s_, jj - 2) in mgate["w1_pe"]:
                            sync.wait_ge(S.h["PE"], mgate["w1_pe"][(s_, jj - 2)])
                        elif jj < 2 and s_ > 0 and (s_ - 1, NJ - 2 + jj) in mgate["w1_pe"]:
                            sync.wait_ge(S.h["PE"], mgate["w1_pe"][(s_ - 1, NJ - 2 + jj)])
                        ins = sync.dma_start(
                            out=w1b[jj % 2][:].rearrange("p (t cc x) -> p t cc x", t=2, x=128),
                            in_=din["fw1"][layer, jj].transpose([2, 0, 1, 3]))
                        mgate["w1_wl"][(s_, jj)] = S.inc(ins, "WL", 16)
                    block.sync(w1_load)

                    def w1_mm(tensor, jj=jj, s_=s_, ga=h2n_act):
                        tensor.wait_ge(S.h["WL"], mgate["w1_wl"][(s_, jj)])
                        tensor.wait_ge(S.h["ACT"], ga)
                        if mgate["act_j"]:
                            tensor.wait_ge(S.h["ACT"], mgate["act_j"])
                        buf = w1b[jj % 2]
                        for t_ in range(2):
                            for cc in range(8):
                                ins = tensor.matmul(
                                    ph_a[:, t_ * 512:(t_ + 1) * 512],
                                    buf[:, t_ * 1024 + cc * 128: t_ * 1024 + (cc + 1) * 128],
                                    h2nT[:, cc * 512:(cc + 1) * 512],
                                    start=(cc == 0), stop=(cc == 7))
                        mgate["w1_pe"][(s_, jj)] = S.inc(ins, "PE")
                    block.tensor(w1_mm)
                    w1_pe = S.c["PE"]

                    def geluact(scalar, jj=jj, gate_pe=w1_pe):
                        scalar.wait_ge(S.h["PE"], gate_pe)
                        S.wait(scalar, "DVE")
                        scalar.activation(out=glt[:], in_=ph_a[:, 512:1024], func=AF.Gelu,
                                          bias=fb1g_t[:, jj:jj + 1])
                        ins = scalar.activation(out=att[:], in_=ph_a[:, 0:512],
                                                func=AF.Identity,
                                                bias=fb1a_t[:, jj:jj + 1])
                        mgate["act_j"] = S.inc(ins, "ACT")
                    block.scalar(geluact)
                    ga_act = S.c["ACT"]

                    def actmul(vector, jj=jj, gate_a=ga_act):
                        vector.wait_ge(S.h["ACT"], gate_a)
                        ins = vector.tensor_tensor(out=actT[:, jj * 512:(jj + 1) * 512],
                                                   in0=att[:], in1=glt[:], op=ALU.mult)
                        S.inc(ins, "DVE")
                    block.vector(actmul)
                act_dve = S.c["DVE"]

                for m_ in range(8):
                    def w2_load(sync, m_=m_, s_=s_, layer=layer):
                        if m_ >= 2 and (s_, m_ - 2) in mgate["w2_pe"]:
                            sync.wait_ge(S.h["PE"], mgate["w2_pe"][(s_, m_ - 2)])
                        elif m_ < 2 and s_ > 0 and (s_ - 1, 6 + m_) in mgate["w2_pe"]:
                            sync.wait_ge(S.h["PE"], mgate["w2_pe"][(s_ - 1, 6 + m_)])
                        ins = sync.dma_start(
                            out=w2b[m_ % 2][:].rearrange("p (j x) -> p j x", x=128),
                            in_=din["fw2"][layer, m_].transpose([1, 0, 2]))
                        mgate["w2_wl"][(s_, m_)] = S.inc(ins, "WL", 16)
                    block.sync(w2_load)

                    def w2_mm(tensor, m_=m_, s_=s_, ga=act_dve):
                        tensor.wait_ge(S.h["WL"], mgate["w2_wl"][(s_, m_)])
                        tensor.wait_ge(S.h["DVE"], ga)
                        if mgate["w2_stt"]:
                            tensor.wait_ge(S.h["DVE"], mgate["w2_stt"])
                        buf = w2b[m_ % 2]
                        for jj in range(NJ):
                            ins = tensor.matmul(ph_a[:, 1024:1536],
                                                buf[:, jj * 128:(jj + 1) * 128],
                                                actT[:, jj * 512:(jj + 1) * 512],
                                                start=(jj == 0), stop=(jj == NJ - 1))
                        mgate["w2_pe"][(s_, m_)] = S.inc(ins, "PE")
                    block.tensor(w2_mm)
                    w2_pe = S.c["PE"]

                    def w2_add(vector, m_=m_, s_=s_, gate_pe=w2_pe):
                        vector.wait_ge(S.h["PE"], gate_pe)
                        dst = xs[:, m_ * N + s_ * 512: m_ * N + (s_ + 1) * 512]
                        ins = vector.scalar_tensor_tensor(out=dst, in0=ph_a[:, 1024:1536],
                                                          scalar=1.0, in1=dst,
                                                          op0=ALU.mult, op1=ALU.add)
                        mgate["w2_stt"] = S.inc(ins, "DVE")
                    block.vector(w2_add)

            def b2add(vector):
                S.wait(vector, "PE")
                for m_ in range(8):
                    ins = vector.tensor_scalar(out=xs[:, m_ * N:(m_ + 1) * N],
                                               in0=xs[:, m_ * N:(m_ + 1) * N],
                                               scalar1=fb2_t[:, m_:m_ + 1],
                                               scalar2=None, op0=ALU.add)
                S.inc(ins, "DVE")
            block.vector(b2add)
            esFm.close()



        phase_barrier("ln2")
        esL2 = ExitStack()
        xinb = esL2.enter_context(nc.sbuf_tensor("xinb", [128, 2048], F32))
        xin_wl = {}
        for cc in range(8):
            def xin_load(sync, cc=cc):
                if cc >= 1:
                    S.wait(sync, "DVE")
                ins = sync.dma_start(out=xinb[:],
                                     in_=xin_sp[:, :].rearrange("(c p) n -> p c n", p=128)[:, cc])
                xin_wl[cc] = S.inc(ins, "WL", 16)
            block.sync(xin_load)

            def xin_add(vector, cc=cc):
                vector.wait_ge(S.h["WL"], xin_wl[cc])
                ins = vector.tensor_tensor(out=xs[:, cc * N:(cc + 1) * N],
                                           in0=xs[:, cc * N:(cc + 1) * N],
                                           in1=xinb[:], op=ALU.add)
                S.inc(ins, "DVE")
            block.vector(xin_add)

        ln_state["first_gate"] = (S.c["DVE"], S.c["ACT"])
        ln_state["prev_back_act"] = 0
        ln_state["prev_apply_pe"] = 0
        for nt in range(NT):
            emit_ln_tile(nt, write_back=False,
                         out_rows=out[nt * 128:(nt + 1) * 128, :])
        esL2.close()

        def final_wait(gpsimd):
            gpsimd.wait_ge(S.h["SP"], S.c["SP"])
        block.gpsimd(final_wait)

    print(f"build done {time.time()-t0:.1f}s n_inst={nc.n_instructions()}")
    return nc, din



def kernel(**inputs):
    shared = prep_shared(inputs)
    x = np.asarray(inputs["L0_attn_x"])
    nc, _ = build()
    in_maps = []
    for b in range(8):
        m = dict(shared)
        m["xT"] = np.ascontiguousarray(x[b].T)
        in_maps.append(m)
    res = run_bass_kernel_spmd(nc, in_maps, core_ids=list(range(8)))
    out = np.stack([np.asarray(res.results[b]["out"]) for b in range(8)], axis=0)
    return out.astype(np.float32)


# version 26
# speedup vs baseline: 1.0223x; 1.0223x over previous
import sys
import time
import numpy as np
from contextlib import ExitStack

import concourse.bass as bass
import concourse.mybir as mybir
from concourse.bass import AP
from concourse.bass_utils import run_bass_kernel_spmd

F32 = mybir.dt.float32
BF16 = mybir.dt.bfloat16
AF = mybir.ActivationFunctionType
ALU = mybir.AluOpType
AX = mybir.AxisListType

N = 2048
C = 1024
NH = 16
NT = N // 128
NC = C // 128

FH = 8
FD = 64
W = 512
HID = 2730
HIDP = 2816
NJ = HIDP // 128
NH_RUN = 16


def host_tables():
    import jax
    import jax.numpy as jnp
    cpu = jax.local_devices(backend="cpu")[0]
    with jax.default_device(cpu):
        inv = 1.0 / 10000.0 ** (jnp.arange(0, 128, 2, dtype=jnp.float32) / 128)
        freqs = jnp.arange(N, dtype=jnp.float32)[:, None] * inv[None, :]
        acos = np.asarray(jnp.cos(freqs), np.float32)
        asin = np.asarray(jnp.sin(freqs), np.float32)

        half_idx = jnp.arange(0, FD, 2, dtype=jnp.float32)
        inv2 = 1.0 / 10000.0 ** (half_idx / FD)
        t = jnp.arange(N, dtype=jnp.float32)
        f2 = t[:, None] * inv2[None, :]
        base = (half_idx + 0.4 * FD) / (1.4 * FD)
        power = (t - N // 2) / float(W // 2)
        scale = base[None, :] ** power[:, None]
        cos2 = np.asarray(jnp.cos(f2), np.float64)
        sin2 = np.asarray(jnp.sin(f2), np.float64)
        sc64 = np.asarray(scale, np.float64)
    csq = (cos2 * sc64).astype(np.float32)
    ssq = (sin2 * sc64).astype(np.float32)
    csk = (cos2 / sc64).astype(np.float32)
    ssk = (sin2 / sc64).astype(np.float32)
    return acos, asin, csq, ssq, csk, ssk


def host_masks():
    MNEG = np.float32(-1.0e30)
    t_idx = np.arange(4)[:, None, None]
    p_idx = np.arange(128)[None, :, None]
    kap = np.arange(1024)[None, None, :]
    rel = 128 * t_idx + p_idx
    allowed = (kap >= rel) & (kap <= rel + 512)
    import ml_dtypes
    maskw = np.where(allowed, np.float32(0), MNEG).astype(ml_dtypes.bfloat16)
    return maskw, None


def prep_shared(inp):
    d = {}
    Wqkv = np.asarray(inp["L0_attn_W_qkv"])
    Wt = np.ascontiguousarray(Wqkv.T)
    vcols = np.concatenate([np.concatenate([4 * 1024 + h * 64 + np.arange(64),
                                            5 * 1024 + h * 64 + np.arange(64)])
                            for h in range(NH)])
    order = np.concatenate([np.arange(0, 2048), np.arange(2048, 4096), vcols])
    Wre = Wt[:, order]
    Wdev = np.empty((3, 2, 8, 128, 1024), np.float32)
    for p in range(3):
        for hf in range(2):
            blk = Wre[:, p * 2048 + hf * 1024: p * 2048 + (hf + 1) * 1024]
            Wdev[p, hf] = blk.reshape(8, 128, 1024)
    d["wqkv"] = Wdev
    d["wproj"] = np.ascontiguousarray(
        np.asarray(inp["L0_attn_W_proj"]).T.reshape(NH, 128, 1024))
    d["lamq1"] = np.asarray(inp["L0_attn_lambda_q1"]).reshape(1, 64)
    d["lamq2"] = np.asarray(inp["L0_attn_lambda_q2"]).reshape(1, 64)
    d["lamk1"] = np.asarray(inp["L0_attn_lambda_k1"]).reshape(1, 64)
    d["lamk2"] = np.asarray(inp["L0_attn_lambda_k2"]).reshape(1, 64)

    d["fwqkv"] = np.ascontiguousarray(np.asarray(inp["L0_ffn_wqkv"]))
    d["fwo"] = np.ascontiguousarray(np.asarray(inp["L0_ffn_wo"]))
    w1 = np.asarray(inp["L0_ffn_w1"])
    w1a = np.zeros((2, 1024, HIDP), np.float32); w1a[:, :, :HID] = w1[:, :, :HID]
    w1g = np.zeros((2, 1024, HIDP), np.float32); w1g[:, :, :HID] = w1[:, :, HID:]
    W1dev = np.empty((2, NJ, 2, 8, 128, 128), np.float32)
    for l in range(2):
        for jj in range(NJ):
            for k in range(8):
                W1dev[l, jj, 0, k] = w1a[l, k * 128:(k + 1) * 128, jj * 128:(jj + 1) * 128]
                W1dev[l, jj, 1, k] = w1g[l, k * 128:(k + 1) * 128, jj * 128:(jj + 1) * 128]
    d["fw1"] = W1dev
    b1 = np.asarray(inp["L0_ffn_b1"])
    b1a = np.zeros((2, HIDP), np.float32); b1a[:, :HID] = b1[:, :HID]
    b1g = np.zeros((2, HIDP), np.float32); b1g[:, :HID] = b1[:, HID:]
    d["fb1a"] = np.ascontiguousarray(b1a.reshape(2, NJ, 128).transpose(0, 2, 1))
    d["fb1g"] = np.ascontiguousarray(b1g.reshape(2, NJ, 128).transpose(0, 2, 1))
    w2 = np.asarray(inp["L0_ffn_w2"])
    w2p = np.zeros((2, HIDP, 1024), np.float32); w2p[:, :HID] = w2
    W2dev = np.empty((2, 8, NJ, 128, 128), np.float32)
    for l in range(2):
        for m in range(8):
            for jj in range(NJ):
                W2dev[l, m, jj] = w2p[l, jj * 128:(jj + 1) * 128, m * 128:(m + 1) * 128]
    d["fw2"] = W2dev
    d["fb2"] = np.ascontiguousarray(
        np.asarray(inp["L0_ffn_b2"]).reshape(2, 8, 128).transpose(0, 2, 1))
    d["flng"] = np.ascontiguousarray(np.broadcast_to(
        np.asarray(inp["L0_ffn_ln_g"])[:, None, :], (2, 128, 1024)))
    d["flnb"] = np.ascontiguousarray(np.broadcast_to(
        np.asarray(inp["L0_ffn_ln_b"])[:, None, :], (2, 128, 1024)))
    d["fqs"] = np.ascontiguousarray(np.broadcast_to(
        np.asarray(inp["L0_ffn_q_scale"])[:, None, :], (2, 128, 64)))
    d["fks"] = np.ascontiguousarray(np.broadcast_to(
        np.asarray(inp["L0_ffn_k_scale"])[:, None, :], (2, 128, 64)))

    acos, asin, csq, ssq, csk, ssk = host_tables()
    d["acos"], d["asin"] = acos, asin
    d["csq"], d["ssq"], d["csk"], d["ssk"] = csq, ssq, csk, ssk
    maskw, _ = host_masks()
    d["maskw"] = maskw
    d["ident"] = np.eye(128, dtype=np.float32)
    d["onesr"] = np.ones((1, 128), np.float32)
    d["onesc"] = np.ones((128, 1), np.float32)
    cvals = np.zeros(8, np.float64)
    cvals[0] = 1e-5
    cvals[1] = 1e-12
    cvals[2] = 0.8 - 0.6 * np.exp(-0.3 * 0)
    cvals[3] = 1.0 / 1024.0
    d["consts"] = np.broadcast_to(cvals.astype(np.float32), (128, 8)).copy()
    return d


class Sems:
    def __init__(self, nc, es, names):
        self.h = {}
        self.c = {}
        for n_ in names:
            self.h[n_] = es.enter_context(nc.semaphore(n_))
            self.c[n_] = 0

    def inc(self, ins, name, n=1):
        self.c[name] += n
        ins.then_inc(self.h[name], n)
        return self.c[name]

    def wait(self, eng, name, val=None):
        v = self.c[name] if val is None else val
        if v > 0:
            eng.wait_ge(self.h[name], v)
        return v


SEM_NAMES = ["WL", "SP", "PE", "ACT", "DVE", "GP"]


def build():
    nc = bass.Bass("TRN2", target_bir_lowering=False)
    t0 = time.time()

    din = {}
    def dri(name, shape, dtype=F32):
        din[name] = nc.dram_tensor(name, list(shape), dtype, kind="ExternalInput").ap()
        return din[name]

    dri("xT", (C, N))
    wqkv = dri("wqkv", (3, 2, 8, 128, 1024))
    dri("wproj", (NH, 128, 1024))
    for nm in ["lamq1", "lamq2", "lamk1", "lamk2"]:
        dri(nm, (1, 64))
    dri("acos", (N, 64)); dri("asin", (N, 64))
    dri("csq", (N, 32)); dri("ssq", (N, 32)); dri("csk", (N, 32)); dri("ssk", (N, 32))
    dri("maskw", (4, 128, 1024), BF16)
    dri("ident", (128, 128)); dri("onesr", (1, 128)); dri("onesc", (128, 1))
    dri("consts", (128, 8))
    dri("fwqkv", (2, C, 1536)); dri("fwo", (2, 512, C))
    dri("fw1", (2, NJ, 2, 8, 128, 128)); dri("fb1a", (2, 128, NJ)); dri("fb1g", (2, 128, NJ))
    dri("fw2", (2, 8, NJ, 128, 128)); dri("fb2", (2, 128, 8))
    dri("flng", (2, 128, C)); dri("flnb", (2, 128, C))
    dri("fqs", (2, 128, 64)); dri("fks", (2, 128, 64))

    out = nc.dram_tensor("out", [N, C], F32, kind="ExternalOutput").ap()

    qT_sp = nc.dram_tensor("qT_sp", [NH, 128, N], F32, kind="Internal").ap()
    kT_sp = nc.dram_tensor("kT_sp", [NH, 128, N], F32, kind="Internal").ap()
    v_sp = nc.dram_tensor("v_sp", [NH, N, 128], F32, kind="Internal").ap()
    xin_sp = nc.dram_tensor("xin_sp", [C, N], F32, kind="Internal").ap()

    es = ExitStack()
    sb = lambda n_, shp, dt_=F32: es.enter_context(nc.sbuf_tensor(n_, shp, dt_))
    S = Sems(nc, es, SEM_NAMES)

    xs = sb("xs", [128, NC * N])
    ident = sb("identt", [128, 128])
    onesr = sb("onesrt", [1, 128])
    onesc = sb("onesct", [128, 1])
    consts = sb("constst", [128, 8])
    lam_b = sb("lam_b", [128, 1])

    with es, nc.Block() as block:
        lam_in = {nm: sb("t_" + nm, [1, 64]) for nm in
                  ["lamq1", "lamk1", "lamq2", "lamk2"]}
        tmp1 = sb("tmp1", [1, 64])
        tmp2 = sb("tmp2", [1, 4])
        ph_s = es.enter_context(nc.psum_tensor("ph_s", [128, 512], F32))

        def ph0(sync):
            ins = sync.dma_start(out=ident[:], in_=din["ident"][:, :]); S.inc(ins, "WL", 16)
            ins = sync.dma_start(out=onesr[:], in_=din["onesr"][:, :]); S.inc(ins, "WL", 16)
            ins = sync.dma_start(out=onesc[:], in_=din["onesc"][:, :]); S.inc(ins, "WL", 16)
            ins = sync.dma_start(out=consts[:], in_=din["consts"][:, :]); S.inc(ins, "WL", 16)
            ins = sync.dma_start(
                out=xs[:].rearrange("p (c n) -> p c n", n=N),
                in_=din["xT"][:, :].rearrange("(c p) n -> p c n", p=128))
            S.inc(ins, "WL", 16)
            for nm in ["lamq1", "lamk1", "lamq2", "lamk2"]:
                ins = sync.dma_start(out=lam_in[nm][:, :], in_=din[nm][:, :])
                S.inc(ins, "WL", 16)
        block.sync(ph0)
        wl_pro = S.c["WL"]

        def lam_dve(vector):
            vector.wait_ge(S.h["WL"], wl_pro)
            vector.tensor_tensor(out=tmp1[:, :], in0=lam_in["lamq1"][:, :],
                                 in1=lam_in["lamk1"][:, :], op=ALU.mult)
            vector.tensor_reduce(out=tmp2[:, 0:1], in_=tmp1[:, :], axis=AX.X, op=ALU.add)
            vector.tensor_tensor(out=tmp1[:, :], in0=lam_in["lamq2"][:, :],
                                 in1=lam_in["lamk2"][:, :], op=ALU.mult)
            ins = vector.tensor_reduce(out=tmp2[:, 1:2], in_=tmp1[:, :], axis=AX.X, op=ALU.add)
            S.inc(ins, "DVE")
        block.vector(lam_dve)

        def lam_act(scalar):
            S.wait(scalar, "DVE")
            scalar.activation(out=tmp2[:, 2:3], in_=tmp2[:, 0:1], func=AF.Exp)
            ins = scalar.activation(out=tmp2[:, 3:4], in_=tmp2[:, 1:2], func=AF.Exp)
            S.inc(ins, "ACT")
        block.scalar(lam_act)

        def lam_dve2(vector):
            S.wait(vector, "ACT")
            ins = vector.tensor_tensor(out=tmp2[:, 0:1], in0=tmp2[:, 2:3], in1=tmp2[:, 3:4],
                                       op=ALU.subtract)
            v = S.inc(ins, "DVE")
            vector.wait_ge(S.h["DVE"], v)
            ins = vector.tensor_tensor(out=tmp2[:, 0:1], in0=tmp2[:, 0:1],
                                       in1=consts[0:1, 2:3], op=ALU.add)
            S.inc(ins, "DVE")
        block.vector(lam_dve2)

        def lam_pe(tensor):
            S.wait(tensor, "DVE")
            ins = tensor.matmul(ph_s[:, 0:1], onesr[:, :], tmp2[0:1, 0:1],
                                start=True, stop=True)
            S.inc(ins, "PE")
        block.tensor(lam_pe)

        def lam_act2(scalar):
            S.wait(scalar, "PE")
            ins = scalar.copy(lam_b[:], ph_s[:, 0:1])
            S.inc(ins, "ACT")
        block.scalar(lam_act2)

        esA = ExitStack()
        sbA = lambda n_, shp, dt_=F32: esA.enter_context(nc.sbuf_tensor(n_, shp, dt_))
        tacos = sbA("tacos", [128, NT * 64])
        tasin = sbA("tasin", [128, NT * 64])
        wstrip = [sbA(f"wstrip{i}", [128, 8 * 1024]) for i in range(2)]
        stg = sbA("stg", [128, 2048])
        rst = sbA("rst", [128, 2048])
        rtm = sbA("rtm", [128, 1024])
        spT = sbA("spT", [128, 512])
        ph_a = es.enter_context(nc.psum_tensor("ph_a", [128, 2048], F32))
        ph_t = es.enter_context(nc.psum_tensor("ph_t", [128, 512], F32))

        def phA_tables(sync):
            for dst, src in [(tacos, din["acos"]), (tasin, din["asin"])]:
                ins = sync.dma_start(
                    out=dst[:].rearrange("p (t e) -> p t e", e=64),
                    in_=src[:, :].rearrange("(t p) e -> p t e", p=128))
                S.inc(ins, "WL", 16)
        block.sync(phA_tables)

        strip_wl = {}
        strip_pe_done = {}
        for i0 in range(2):
            def load_strip(sync, i=i0):
                p, hf = divmod(i, 2)
                ins = sync.dma_start(
                    out=wstrip[i % 2][:].rearrange("p (k x) -> p k x", x=1024),
                    in_=wqkv[p, hf].transpose([1, 0, 2]))
                strip_wl[i] = S.inc(ins, "WL", 16)
            block.sync(load_strip)

        last_evac = {"q0": 0, "q1": 0}
        last_rot = 0
        last_stg_user = 0
        last_tevac = 0
        last_spill = 0
        last_vspill = 0

        for pair in range(3):
            for nt in range(NT):
                def mm_step(tensor, pair=pair, nt=nt,
                            w0=last_evac["q0"], w1=last_evac["q1"],
                            lrot=last_rot):
                    i = pair * 2
                    tensor.wait_ge(S.h["WL"], max(strip_wl[i], strip_wl[i + 1]))
                    if w0:
                        tensor.wait_ge(S.h["ACT"], max(w0, w1))
                    for hf in range(2):
                        buf = wstrip[(pair * 2 + hf) % 2]
                        for cc in range(8):
                            for q5 in range(2):
                                ins = tensor.matmul(
                                    ph_a[:, hf * 1024 + q5 * 512: hf * 1024 + (q5 + 1) * 512],
                                    xs[:, cc * N + nt * 128: cc * N + (nt + 1) * 128],
                                    buf[:, cc * 1024 + q5 * 512: cc * 1024 + (q5 + 1) * 512],
                                    start=(cc == 0), stop=(cc == 7))
                    S.inc(ins, "PE")
                block.tensor(mm_step)
                mm_pe = S.c["PE"]
                if nt == NT - 1:
                    strip_pe_done[pair * 2] = mm_pe
                    strip_pe_done[pair * 2 + 1] = mm_pe
                    if pair < 2:
                        for hf2 in range(2):
                            def load_next(sync, pair=pair, hf2=hf2, gate=mm_pe):
                                i = (pair + 1) * 2 + hf2
                                sync.wait_ge(S.h["PE"], strip_pe_done[i - 2])
                                p2, h2 = divmod(i, 2)
                                ins = sync.dma_start(
                                    out=wstrip[i % 2][:].rearrange("p (k x) -> p k x", x=1024),
                                    in_=wqkv[p2, h2].transpose([1, 0, 2]))
                                strip_wl[i] = S.inc(ins, "WL", 16)
                            block.sync(load_next)

                def evac_step(scalar, pair=pair, nt=nt, gate=mm_pe,
                              lrot=last_rot, lvs=last_vspill):
                    scalar.wait_ge(S.h["PE"], gate)
                    if pair < 2:
                        if lrot:
                            scalar.wait_ge(S.h["DVE"], lrot)
                        dst = stg
                    else:
                        if lvs:
                            scalar.wait_ge(S.h["SP"], lvs)
                        dst = rst
                    scalar.activation(out=dst[:, 0:1024], in_=ph_a[:, 0:1024], func=AF.Copy)
                    ins = scalar.activation(out=dst[:, 1024:2048], in_=ph_a[:, 1024:2048],
                                            func=AF.Copy)
                    S.inc(ins, "ACT")
                block.scalar(evac_step)
                ev_act = S.c["ACT"]
                last_evac["q0"] = ev_act
                last_evac["q1"] = ev_act

                if pair < 2:
                    def rot_step(vector, nt=nt, gate=ev_act, ltev=last_tevac):
                        vector.wait_ge(S.h["ACT"], gate)
                        if ltev:
                            pass
                        cb = tacos[:, nt * 64:(nt + 1) * 64].unsqueeze(1).broadcast_to([128, 16, 64])
                        sbb = tasin[:, nt * 64:(nt + 1) * 64].unsqueeze(1).broadcast_to([128, 16, 64])
                        qa = stg[:, 0:1024].rearrange("p (h e) -> p h e", e=64)
                        qb = stg[:, 1024:2048].rearrange("p (h e) -> p h e", e=64)
                        r3 = rst[:].rearrange("p (h d) -> p h d", d=128)
                        t3 = rtm[:].rearrange("p (h e) -> p h e", e=64)
                        vector.tensor_tensor(out=t3, in0=qa, in1=cb, op=ALU.mult)
                        vector.tensor_tensor(out=r3[:, :, 0:64], in0=qb, in1=sbb, op=ALU.mult)
                        vector.tensor_tensor(out=r3[:, :, 0:64], in0=t3, in1=r3[:, :, 0:64],
                                             op=ALU.subtract)
                        vector.tensor_tensor(out=t3, in0=qb, in1=cb, op=ALU.mult)
                        vector.tensor_tensor(out=r3[:, :, 64:128], in0=qa, in1=sbb, op=ALU.mult)
                        ins = vector.tensor_tensor(out=r3[:, :, 64:128], in0=t3,
                                                   in1=r3[:, :, 64:128], op=ALU.add)
                        S.inc(ins, "DVE")
                    rot_pe_gate = S.c["PE"]

                    def rot_step2(vector, gate_pe=rot_pe_gate):
                        vector.wait_ge(S.h["PE"], gate_pe)
                    block.vector(rot_step2)
                    block.vector(rot_step)
                    last_rot = S.c["DVE"]

                    for g in range(4):
                        def t_step(tensor, g=g, rgate=last_rot, egate=last_tevac):
                            tensor.wait_ge(S.h["DVE"], rgate)
                            if egate:
                                tensor.wait_ge(S.h["ACT"], egate)
                            for j in range(4):
                                hh = g * 4 + j
                                ins = tensor.matmul(
                                    ph_t[:, j * 128:(j + 1) * 128],
                                    rst[:, hh * 128:(hh + 1) * 128],
                                    ident[:, :],
                                    start=(j == 0), stop=(j == 3), is_transpose=True)
                            S.inc(ins, "PE")
                        block.tensor(t_step)
                        t_pe = S.c["PE"]

                        def tevac_step(scalar, gate=t_pe, sgate=last_spill):
                            scalar.wait_ge(S.h["PE"], gate)
                            if sgate:
                                scalar.wait_ge(S.h["SP"], sgate)
                            ins = scalar.activation(out=spT[:], in_=ph_t[:], func=AF.Copy)
                            S.inc(ins, "ACT")
                        block.scalar(tevac_step)
                        last_tevac = S.c["ACT"]

                        def spill_step(gpsimd, pair=pair, nt=nt, g=g, gate=last_tevac):
                            gpsimd.wait_ge(S.h["ACT"], gate)
                            dst_all = qT_sp if pair == 0 else kT_sp
                            dst = dst_all[4 * g:4 * g + 4, :, nt * 128:(nt + 1) * 128]
                            ins = gpsimd.dma_start(
                                out=dst.transpose([1, 0, 2]),
                                in_=spT[:].rearrange("p (j n) -> p j n", n=128))
                            S.inc(ins, "SP", 16)
                        block.gpsimd(spill_step)
                        last_spill = S.c["SP"]
                else:
                    def vspill_step(gpsimd, nt=nt, gate=ev_act):
                        gpsimd.wait_ge(S.h["ACT"], gate)
                        dst = v_sp[:, nt * 128:(nt + 1) * 128, :]
                        ins = gpsimd.dma_start(
                            out=dst.transpose([1, 0, 2]),
                            in_=rst[:].rearrange("p (h d) -> p h d", d=128))
                        S.inc(ins, "SP", 16)
                    block.gpsimd(vspill_step)
                    last_vspill = S.c["SP"]


        phA_end = {k: S.c[k] for k in SEM_NAMES}
        esA.close()

        ph_pv = es.enter_context(nc.psum_tensor("ph_pv", [128, 512], F32))
        ph_pj = es.enter_context(nc.psum_tensor("ph_pj", [128, 512], F32))
        esB = ExitStack()
        sbB = lambda n_, shp, dt_=F32: esB.enter_context(nc.sbuf_tensor(n_, shp, dt_))
        qkb = [sbB(f"qkb{i}", [128, 2048]) for i in range(2)]
        kkb = [sbB(f"kkb{i}", [128, 2048]) for i in range(2)]
        vkb = [sbB(f"vkb{i}", [128, 2048]) for i in range(2)]
        wpb = [sbB("wpb0", [128, 1024])]
        wpb.append(wpb[0])
        P1u = sbB("P1u", [128, 2048])
        P2u = sbB("P2u", [128, 2048])
        Pc = [sbB(f"Pc_{i}", [128, 2048]) for i in range(2)]
        PT = sbB("PT", [128, 8192])
        outT = sbB("outT", [128, 2048])
        sqb = sbB("sqb", [128, 512])
        negc = [sbB(f"negc{i}", [128, 1]) for i in range(2)]
        acc1r = [sbB(f"acc1_{i}", [128, 4]) for i in range(2)]
        acc2r = [sbB(f"acc2_{i}", [128, 4]) for i in range(2)]
        s1t = sbB("s1t_", [128, 1]); s2t = sbB("s2t_", [128, 1])
        inv1 = sbB("inv1", [128, 1]); inv2 = sbB("inv2", [128, 1])
        m2n = sbB("m2n", [128, 1])
        nrm = sbB("nrm", [1, 10])

        head_load_wl = {}
        wpb_wl = {}
        wpb_gate = {}
        wpb_pending = {}
        head_pe_end = {}
        exp_done = {}
        t_pe_done = {}
        comb_done = {}
        tevac_done = {}
        pv_mm_done = {}
        pv_evac_done = {}
        pjadd_done = {}
        negc_done = {}
        cchain = {}

        def emit_head_loads(h):
            def loads(sync, h=h):
                if h >= 2:
                    sync.wait_ge(S.h["PE"], head_pe_end[h - 2])
                else:
                    for k in ["SP", "PE", "ACT", "DVE"]:
                        if phA_end[k]:
                            sync.wait_ge(S.h[k], phA_end[k])
                b = h % 2
                ins = sync.dma_start(out=qkb[b][:], in_=qT_sp[h, :, :]); S.inc(ins, "WL", 16)
                ins = sync.dma_start(out=kkb[b][:], in_=kT_sp[h, :, :]); S.inc(ins, "WL", 16)
                ins = sync.dma_start(
                    out=vkb[b][:].rearrange("p (kc d) -> p kc d", d=128),
                    in_=v_sp[h].rearrange("(kc p) d -> p kc d", p=128))
                S.inc(ins, "WL", 16)
                head_load_wl[h] = S.c["WL"]
            block.sync(loads)

            def wpb_load(sync, h=h, gate=None):
                if h >= 1:
                    sync.wait_ge(S.h["PE"], wpb_gate[h])
                ins = sync.dma_start(
                    out=wpb[0][:].rearrange("p (cc x) -> p cc x", x=128),
                    in_=din["wproj"][h].rearrange("p (cc x) -> p cc x", x=128))
                wpb_wl[h] = S.inc(ins, "WL", 16)
            if h == 0:
                wpb_gate[0] = 0
                block.sync(wpb_load)
            else:
                wpb_pending[h] = wpb_load

        def cchain_steps(h):
            b = h % 2
            last_sq_pe = {"v": 0}

            for side in range(2):
                for g in range(4):
                    def nsq(vector, h=h, b=b, side=side, g=g):
                        if side == 0 and g == 0:
                            vector.wait_ge(S.h["WL"], head_load_wl[h])
                        S.wait(vector, "PE")
                        buf = qkb[b] if side == 0 else kkb[b]
                        ins = vector.tensor_tensor(out=sqb[:],
                                                   in0=buf[:, g * 512:(g + 1) * 512],
                                                   in1=buf[:, g * 512:(g + 1) * 512],
                                                   op=ALU.mult)
                        S.inc(ins, "DVE")
                    yield ("vector", nsq)
                    def nmm(tensor, g=g):
                        S.wait(tensor, "DVE")
                        ins = tensor.matmul(ph_s[0:1, :], onesc[:, :], sqb[:],
                                            start=True, stop=True)
                        S.inc(ins, "PE")
                    yield ("tensor", nmm)
                    def nred(vector, side=side, g=g):
                        S.wait(vector, "PE")
                        ins = vector.reduce_max(out=nrm[:, side * 4 + g:side * 4 + g + 1],
                                                in_=ph_s[0:1, :], axis=AX.X)
                        S.inc(ins, "DVE")
                    yield ("vector", nred)

            def fin(vector):
                vector.reduce_max(out=nrm[:, 8:9], in_=nrm[:, 0:4], axis=AX.X)
                ins = vector.reduce_max(out=nrm[:, 9:10], in_=nrm[:, 4:8], axis=AX.X)
                v = S.inc(ins, "DVE")
                vector.wait_ge(S.h["DVE"], v)
                ins = vector.tensor_tensor(out=nrm[:, 8:9], in0=nrm[:, 8:9],
                                           in1=nrm[:, 9:10], op=ALU.mult)
                S.inc(ins, "DVE")
            yield ("vector", fin)

            def fin2(scalar):
                S.wait(scalar, "DVE")
                ins = scalar.activation(out=nrm[:, 9:10], in_=nrm[:, 8:9], func=AF.Ln)
                v = S.inc(ins, "ACT")
                scalar.wait_ge(S.h["ACT"], v)
                ins = scalar.activation(out=nrm[:, 8:9], in_=nrm[:, 9:10], func=AF.Exp,
                                        scale=0.5)
                S.inc(ins, "ACT")
            yield ("scalar", fin2)

            def fin3(vector):
                S.wait(vector, "ACT")
                ins = vector.tensor_scalar(out=nrm[:, 8:9], in0=nrm[:, 8:9],
                                           scalar1=-0.125, scalar2=None, op0=ALU.mult)
                S.inc(ins, "DVE")
            yield ("vector", fin3)

            def bca(tensor):
                S.wait(tensor, "DVE")
                ins = tensor.matmul(ph_s[:, 0:1], onesr[:, :], nrm[0:1, 8:9],
                                    start=True, stop=True)
                S.inc(ins, "PE")
            yield ("tensor", bca)

            def bcc(vector, h=h):
                S.wait(vector, "PE")
                ins = vector.tensor_copy(out=negc[h % 2][:], in_=ph_s[:, 0:1])
                negc_done[h] = S.inc(ins, "DVE")
            yield ("vector", bcc)

        def run_steps(gen, nsteps):
            n = 0
            for kind, f in gen:
                getattr(block, kind)(f)
                n += 1
                if n >= nsteps:
                    break

        emit_head_loads(0)
        g0 = cchain_steps(0)
        run_steps(g0, 100)
        for h in range(NH_RUN):
            b = h % 2
            if h + 1 < NH_RUN:
                emit_head_loads(h + 1)
                gnext = cchain_steps(h + 1)
            else:
                gnext = None

            def emit_Tpv(qt, h=h, b=b):
                    qg, qr = divmod(qt, 4)
                    for g in range(4):
                        def tmm(tensor, h=h, qt=qt, g=g, qg=qg):
                            tensor.wait_ge(S.h["DVE"], comb_done[(h, qt)])
                            pk = (h, qt, g - 1) if g > 0 else \
                                 ((h, qt - 1, 3) if qt > 0 else
                                  ((h - 1, NT - 1, 3) if h > 0 else None))
                            if pk is not None and pk in tevac_done:
                                tensor.wait_ge(S.h["DVE"], tevac_done[pk])
                            pvk = (h, qg - 1) if qg > 0 else ((h - 1, 3) if h > 0 else None)
                            if pvk is not None and pvk in pv_mm_done and g == 0:
                                tensor.wait_ge(S.h["PE"], pv_mm_done[pvk])
                            pq = qt % 2
                            for j in range(4):
                                kc = g * 4 + j
                                ins = tensor.matmul(ph_t[:, j * 128:(j + 1) * 128],
                                                    Pc[pq][:, kc * 128:(kc + 1) * 128],
                                                    ident[:, :],
                                                    start=(j == 0), stop=(j == 3),
                                                    is_transpose=True)
                            S.inc(ins, "PE")
                        block.tensor(tmm)
                        t_pe = S.c["PE"]
                        t_pe_done[(h, qt, g)] = t_pe

                        def tev(vector, h=h, qt=qt, g=g, qg=qg, qr=qr, gate=t_pe):
                            vector.wait_ge(S.h["PE"], gate)
                            ins = vector.tensor_copy(
                                out=PT[:].rearrange("p (kc q) -> p kc q", q=512)
                                    [:, 4 * g:4 * g + 4, qr * 128:(qr + 1) * 128],
                                in_=ph_t[:].rearrange("p (j n) -> p j n", n=128))
                            tevac_done[(h, qt, g)] = S.inc(ins, "DVE")
                        block.vector(tev)

                    if qr == 3:
                        def pv(tensor, h=h, b=b, qg=qg):
                            for qq in range(4):
                                tensor.wait_ge(S.h["DVE"], tevac_done[(h, qg * 4 + qq, 3)])
                            pk = (h, qg - 1) if qg > 0 else ((h - 1, 3) if h > 0 else None)
                            if pk is not None and pk in pv_evac_done:
                                tensor.wait_ge(S.h["DVE"], pv_evac_done[pk])
                            if pk is not None and (pk[0], pk[1], 7) in pjadd_done:
                                tensor.wait_ge(S.h["DVE"], pjadd_done[(pk[0], pk[1], 7)])
                            for kc in range(16):
                                ins = tensor.matmul(ph_pv[:, :],
                                                    vkb[b][:, kc * 128:(kc + 1) * 128],
                                                    PT[:, kc * 512:(kc + 1) * 512],
                                                    start=(kc == 0), stop=(kc == 15))
                            pv_mm_done[(h, qg)] = S.inc(ins, "PE")
                        block.tensor(pv)
                        pv_pe = S.c["PE"]

                        def pvev(vector, h=h, qg=qg, gate=pv_pe):
                            vector.wait_ge(S.h["PE"], gate)
                            ins = vector.tensor_copy(out=outT[:, qg * 512:(qg + 1) * 512],
                                                     in_=ph_pv[:, :])
                            pv_evac_done[(h, qg)] = S.inc(ins, "DVE")
                        block.vector(pvev)

                        for cc in range(8):
                            def pj(tensor, h=h, b=b, qg=qg, cc=cc):
                                if cc == 0:
                                    tensor.wait_ge(S.h["DVE"], pv_evac_done[(h, qg)])
                                bank = ph_pj if cc % 2 == 0 else ph_pv
                                if cc >= 2:
                                    tensor.wait_ge(S.h["DVE"], pjadd_done[(h, qg, cc - 2)])
                                elif cc == 0:
                                    pk2 = (h, qg - 1) if qg > 0 else ((h - 1, 3) if h > 0 else None)
                                    if pk2 is not None and (pk2[0], pk2[1], 6) in pjadd_done:
                                        tensor.wait_ge(S.h["DVE"], pjadd_done[(pk2[0], pk2[1], 6)])
                                if cc == 0 and qg == 0:
                                    tensor.wait_ge(S.h["WL"], wpb_wl[h])
                                ins = tensor.matmul(bank[:, :],
                                                    wpb[0][:, cc * 128:(cc + 1) * 128],
                                                    outT[:, qg * 512:(qg + 1) * 512],
                                                    start=True, stop=True)
                                S.inc(ins, "PE")
                            block.tensor(pj)
                            pj_pe = S.c["PE"]

                            def pjadd(vector, h=h, qg=qg, cc=cc, gate=pj_pe):
                                vector.wait_ge(S.h["PE"], gate)
                                bank = ph_pj if cc % 2 == 0 else ph_pv
                                dst = xs[:, cc * N + qg * 512: cc * N + (qg + 1) * 512]
                                ins = vector.scalar_tensor_tensor(
                                    out=dst, in0=bank[:, :], scalar=1.0, in1=dst,
                                    op0=ALU.mult, op1=ALU.add)
                                pjadd_done[(h, qg, cc)] = S.inc(ins, "DVE")
                            block.vector(pjadd)

            for qt in range(NT):
                act_base = S.c["ACT"]
                exp_pred = {ks: act_base + ks + 1 for ks in range(4)}
                s_pe_ks = {}
                for ks in range(4):
                    bp = ks % 2
                    def smm(tensor, h=h, b=b, qt=qt, ks=ks, bp=bp, ep=exp_pred):
                        if qt == 0 and ks == 0:
                            tensor.wait_ge(S.h["WL"], head_load_wl[h])
                        key = (h, qt - 1, ks + 2) if qt > 0 else \
                              ((h - 1, NT - 1, ks + 2) if h > 0 else None)
                        if ks >= 2:
                            tensor.wait_ge(S.h["ACT"], ep[ks - 2])
                        elif key is not None and key in exp_done:
                            tensor.wait_ge(S.h["ACT"], exp_done[key])
                        lo = bp * 1024
                        tensor.matmul(ph_a[:, lo:lo + 512],
                                      qkb[b][0:64, qt * 128:(qt + 1) * 128],
                                      kkb[b][0:64, ks * 512:(ks + 1) * 512],
                                      start=True, stop=True)
                        ins = tensor.matmul(ph_a[:, lo + 512:lo + 1024],
                                            qkb[b][64:128, qt * 128:(qt + 1) * 128],
                                            kkb[b][64:128, ks * 512:(ks + 1) * 512],
                                            start=True, stop=True,
                                            tile_position=(64, 0))
                        S.inc(ins, "PE")
                    block.tensor(smm)
                    s_pe_ks[ks] = S.c["PE"]

                assert S.c["ACT"] == act_base, "no ACT emissions between smm and exp body"

                def sexp(scalar, h=h, qt=qt, gates=dict(s_pe_ks)):
                    ckey = (h, qt - 1) if qt >= 1 else \
                        ((h - 1, NT - 1) if h > 0 else None)
                    if ckey is not None and ckey in comb_done:
                        scalar.wait_ge(S.h["DVE"], comb_done[ckey])
                    if qt == 0 and h in negc_done:
                        scalar.wait_ge(S.h["DVE"], negc_done[h])
                    pq = qt % 2
                    for ks in range(4):
                        bp = ks % 2
                        lo = bp * 1024
                        scalar.wait_ge(S.h["PE"], gates[ks])
                        scalar.activation(out=P1u[:, ks * 512:(ks + 1) * 512],
                                          in_=ph_a[:, lo:lo + 512], func=AF.Exp,
                                          bias=negc[h % 2][:], scale=0.125,
                                          accum_out=acc1r[pq][:, ks:ks + 1])
                        ins = scalar.activation(out=P2u[:, ks * 512:(ks + 1) * 512],
                                                in_=ph_a[:, lo + 512:lo + 1024], func=AF.Exp,
                                                bias=negc[h % 2][:], scale=0.125,
                                                accum_out=acc2r[pq][:, ks:ks + 1])
                        exp_done[(h, qt, ks)] = S.inc(ins, "ACT")
                block.scalar(sexp)
                assert exp_done[(h, qt, 3)] == exp_pred[3]

                def comb_a(vector, h=h, qt=qt):
                    pq = qt % 2
                    vector.wait_ge(S.h["ACT"], exp_done[(h, qt, 3)])
                    vector.tensor_reduce(out=s1t[:], in_=acc1r[pq][:], axis=AX.X, op=ALU.add)
                    ins = vector.tensor_reduce(out=s2t[:], in_=acc2r[pq][:], axis=AX.X, op=ALU.add)
                    v = S.inc(ins, "DVE")
                    vector.wait_ge(S.h["DVE"], v)
                    vector.reciprocal(out=inv1[:], in_=s1t[:])
                    ins = vector.reciprocal(out=inv2[:], in_=s2t[:])
                    v = S.inc(ins, "DVE")
                    vector.wait_ge(S.h["DVE"], v)
                    ins = vector.tensor_scalar(out=m2n[:], in0=inv2[:], scalar1=lam_b[:, 0:1],
                                               scalar2=-1.0, op0=ALU.mult, op1=ALU.mult)
                    S.inc(ins, "DVE")
                block.vector(comb_a)
                m2_dve = S.c["DVE"]

                def sscale(scalar, gate=m2_dve):
                    scalar.wait_ge(S.h["DVE"], gate)
                    scalar.activation(out=P1u[:], in_=P1u[:], func=AF.Identity,
                                      scale=inv1[:, 0:1], bias=0.0)
                    ins = scalar.activation(out=P2u[:], in_=P2u[:], func=AF.Identity,
                                            scale=m2n[:, 0:1], bias=0.0)
                    S.inc(ins, "ACT")
                block.scalar(sscale)
                sc_act = S.c["ACT"]

                def comb(vector, h=h, qt=qt, gate=sc_act):
                    pq = qt % 2
                    vector.wait_ge(S.h["ACT"], gate)
                    tk = (h, qt - 2, 3) if qt >= 2 else ((h - 1, NT - 2 + qt, 3) if h > 0 else None)
                    if tk is not None and tk in t_pe_done:
                        vector.wait_ge(S.h["PE"], t_pe_done[tk])
                    ins = vector.tensor_tensor(out=Pc[pq][:], in0=P2u[:], in1=P1u[:],
                                               op=ALU.add)
                    comb_done[(h, qt)] = S.inc(ins, "DVE")
                block.vector(comb)

                if qt >= 1:
                    emit_Tpv(qt - 1)

            emit_Tpv(NT - 1)
            if gnext is not None:
                run_steps(gnext, 100)
            head_pe_end[h] = S.c["PE"]
            if (h + 1) in wpb_pending:
                wpb_gate[h + 1] = S.c["PE"]
                block.sync(wpb_pending.pop(h + 1))




        phB_end = {k: S.c[k] for k in SEM_NAMES}
        esB.close()
        ln_y = sb("ln_y", [128, 1024])
        ln_sq = sb("ln_sq", [128, 1024])
        ln_s = sb("ln_s", [128, 6])
        ln_state = {"prev_back_act": 0, "prev_apply_pe": 0, "first_gate": None}

        def emit_ln_tile(nt, gamma_beta=None, write_back=True, out_rows=None,
                         src_fn=None, back_dst_fn=None):
            def fwd(tensor, nt=nt):
                if ln_state["first_gate"] is not None:
                    g = ln_state["first_gate"]
                    ln_state["first_gate"] = None
                    tensor.wait_ge(S.h["DVE"], g[0])
                    tensor.wait_ge(S.h["ACT"], g[1])
                if ln_state["prev_back_act"]:
                    tensor.wait_ge(S.h["ACT"], ln_state["prev_back_act"])
                if ln_state.get("prev_fwd_evac", 0):
                    tensor.wait_ge(S.h["ACT"], ln_state["prev_fwd_evac"])
                for cc in range(8):
                    src = src_fn(cc) if src_fn is not None else \
                        xs[:, cc * N + nt * 128: cc * N + (nt + 1) * 128]
                    ins = tensor.matmul(ph_a[:, cc * 128:(cc + 1) * 128], src,
                                        ident[:, :],
                                        start=(cc % 4 == 0), stop=(cc % 4 == 3),
                                        is_transpose=True)
                S.inc(ins, "PE")
            block.tensor(fwd)
            fwd_pe = S.c["PE"]

            def evac(scalar, gate=fwd_pe, osp=ln_state.get("out_sp", 0)):
                scalar.wait_ge(S.h["PE"], gate)
                if osp:
                    scalar.wait_ge(S.h["SP"], osp)
                scalar.activation(out=ln_y[:], in_=ph_a[:, 0:1024], func=AF.Copy,
                                  accum_out=ln_s[:, 0:1])
                ins = scalar.activation(out=ln_sq[:], in_=ln_y[:], func=AF.Square,
                                        accum_out=ln_s[:, 1:2])
                S.inc(ins, "ACT")
            block.scalar(evac)
            ev_act = S.c["ACT"]
            ln_state["prev_fwd_evac"] = ev_act

            def stats(vector, gate=ev_act):
                vector.wait_ge(S.h["ACT"], gate)
                vector.tensor_scalar(out=ln_s[:, 2:3], in0=ln_s[:, 0:1],
                                     scalar1=1.0 / 1024.0, scalar2=None, op0=ALU.mult)
                ins = vector.tensor_scalar(out=ln_s[:, 3:4], in0=ln_s[:, 1:2],
                                           scalar1=1.0 / 1024.0, scalar2=None, op0=ALU.mult)
                v = S.inc(ins, "DVE")
                vector.wait_ge(S.h["DVE"], v)
                ins = vector.tensor_tensor(out=ln_s[:, 5:6], in0=ln_s[:, 2:3],
                                           in1=ln_s[:, 2:3], op=ALU.mult)
                v = S.inc(ins, "DVE")
                vector.wait_ge(S.h["DVE"], v)
                ins = vector.tensor_tensor(out=ln_s[:, 3:4], in0=ln_s[:, 3:4],
                                           in1=ln_s[:, 5:6], op=ALU.subtract)
                S.inc(ins, "DVE")
            block.vector(stats)
            st_dve = S.c["DVE"]

            def sqrt_step(scalar, gate=st_dve):
                scalar.wait_ge(S.h["DVE"], gate)
                ins = scalar.activation(out=ln_s[:, 3:4], in_=ln_s[:, 3:4], func=AF.Sqrt,
                                        bias=consts[:, 0:1])
                S.inc(ins, "ACT")
            block.scalar(sqrt_step)
            sq_act = S.c["ACT"]

            def rstep(vector, gate=sq_act):
                vector.wait_ge(S.h["ACT"], gate)
                ins = vector.reciprocal(out=ln_s[:, 4:5], in_=ln_s[:, 3:4])
                v = S.inc(ins, "DVE")
                vector.wait_ge(S.h["DVE"], v)
                ins = vector.tensor_scalar(out=ln_s[:, 5:6], in0=ln_s[:, 2:3],
                                           scalar1=ln_s[:, 4:5], scalar2=-1.0,
                                           op0=ALU.mult, op1=ALU.mult)
                S.inc(ins, "DVE")
            block.vector(rstep)
            r_dve = S.c["DVE"]

            def apply(scalar, gate=r_dve, pgate=ln_state["prev_apply_pe"]):
                scalar.wait_ge(S.h["DVE"], gate)
                if pgate:
                    scalar.wait_ge(S.h["PE"], pgate)
                ins = scalar.activation(out=ln_y[:], in_=ln_y[:], func=AF.Identity,
                                        scale=ln_s[:, 4:5], bias=ln_s[:, 5:6])
                S.inc(ins, "ACT")
            block.scalar(apply)
            if gamma_beta is not None:
                gmt, bt = gamma_beta
                def gb(vector):
                    S.wait(vector, "ACT")
                    vector.tensor_tensor(out=ln_y[:], in0=ln_y[:], in1=gmt, op=ALU.mult)
                    ins = vector.tensor_tensor(out=ln_y[:], in0=ln_y[:], in1=bt, op=ALU.add)
                    S.inc(ins, "DVE")
                block.vector(gb)
            ap_act, ap_dve = S.c["ACT"], S.c["DVE"]

            if write_back:
                def back(tensor, nt=nt, ga=ap_act, gd=ap_dve, gb_=gamma_beta):
                    tensor.wait_ge(S.h["ACT"], ga)
                    if gb_ is not None:
                        tensor.wait_ge(S.h["DVE"], gd)
                    for cc in range(8):
                        ins = tensor.matmul(ph_a[:, 1024 + (cc % 4) * 128:1024 + (cc % 4) * 128 + 128],
                                            ln_y[:, cc * 128:(cc + 1) * 128],
                                            ident[:, :],
                                            start=True, stop=True,
                                            is_transpose=True)
                        S.inc(ins, "PE")
                        g_pe = S.c["PE"]
                        def bevac(scalar, cc=cc, nt=nt, gate=g_pe):
                            scalar.wait_ge(S.h["PE"], gate)
                            ins = scalar.activation(
                                out=xs[:, cc * N + nt * 128: cc * N + (nt + 1) * 128],
                                in_=ph_a[:, 1024 + (cc % 4) * 128:1024 + (cc % 4) * 128 + 128],
                                func=AF.Copy)
                            S.inc(ins, "ACT")
                        return ins, bevac
                bevac_acts = {}
                for cc in range(8):
                    def bt_step(tensor, nt=nt, cc=cc, ga=ap_act, gd=ap_dve, gb_=gamma_beta):
                        if cc == 0:
                            tensor.wait_ge(S.h["ACT"], ga)
                            if gb_ is not None:
                                tensor.wait_ge(S.h["DVE"], gd)
                        if cc >= 4:
                            tensor.wait_ge(S.h["ACT"], bevac_acts[cc - 4])
                        ins = tensor.matmul(
                            ph_a[:, 1024 + (cc % 4) * 128:1024 + (cc % 4) * 128 + 128],
                            ln_y[:, cc * 128:(cc + 1) * 128], ident[:, :],
                            start=True, stop=True, is_transpose=True)
                        S.inc(ins, "PE")
                    block.tensor(bt_step)
                    g_pe = S.c["PE"]
                    def bevac(scalar, cc=cc, nt=nt, gate=g_pe, bdf=back_dst_fn):
                        scalar.wait_ge(S.h["PE"], gate)
                        dst = bdf(cc) if bdf is not None else \
                            xs[:, cc * N + nt * 128: cc * N + (nt + 1) * 128]
                        ins = scalar.activation(
                            out=dst,
                            in_=ph_a[:, 1024 + (cc % 4) * 128:1024 + (cc % 4) * 128 + 128],
                            func=AF.Copy)
                        S.inc(ins, "ACT")
                    block.scalar(bevac)
                    bevac_acts[cc] = S.c["ACT"]
                ln_state["prev_back_act"] = S.c["ACT"]
                ln_state["prev_apply_pe"] = S.c["PE"]
            if out_rows is not None:
                def odma(gpsimd, nt=nt, ga=ap_act, gd=ap_dve, gb_=gamma_beta):
                    gpsimd.wait_ge(S.h["ACT"], ga)
                    if gb_ is not None:
                        gpsimd.wait_ge(S.h["DVE"], gd)
                    ins = gpsimd.dma_start(out=out_rows, in_=ln_y[:])
                    S.inc(ins, "SP", 16)
                block.gpsimd(odma)
                ln_state["prev_apply_pe"] = 0
                ln_state["prev_back_act"] = 0
                ln_state["out_sp"] = S.c["SP"]

        ln_state["first_gate"] = (phB_end["DVE"], phB_end["ACT"])
        for nt in range(NT):
            emit_ln_tile(nt)

        def xin_dma(sync):
            S.wait(sync, "ACT")
            ins = sync.dma_start(
                out=xin_sp[:, :].rearrange("(c p) n -> p c n", p=128),
                in_=xs[:].rearrange("p (c n) -> p c n", n=N))
            S.inc(ins, "WL", 16)
        block.sync(xin_dma)





        def phase_barrier(tag):
            cur = {k: S.c[k] for k in SEM_NAMES}
            def bsync(sync):
                for k in ["PE", "ACT", "DVE", "SP", "GP"]:
                    if cur[k]:
                        sync.wait_ge(S.h[k], cur[k])
            block.sync(bsync)
            def bact(scalar):
                for k in ["PE", "DVE", "WL", "SP"]:
                    if cur[k]:
                        scalar.wait_ge(S.h[k], cur[k])
            block.scalar(bact)
            def bdve(vector):
                for k in ["PE", "ACT", "WL", "SP"]:
                    if cur[k]:
                        vector.wait_ge(S.h[k], cur[k])
            block.vector(bdve)
            def bpe(tensor):
                for k in ["ACT", "DVE", "WL", "SP"]:
                    if cur[k]:
                        tensor.wait_ge(S.h[k], cur[k])
            block.tensor(bpe)
            def bgp(gpsimd):
                for k in ["PE", "ACT", "DVE", "WL"]:
                    if cur[k]:
                        gpsimd.wait_ge(S.h[k], cur[k])
            block.gpsimd(bgp)
            return cur

        for layer in range(2):
            phase_barrier(f"ffa{layer}")
            esFa = ExitStack()
            sbF = lambda n_, shp, dt_=F32: esFa.enter_context(
                nc.sbuf_tensor(n_ + f"_l{layer}", shp, dt_))
            wstrips = [sbF(f"fws{i}", [128, 4096]) for i in range(2)]
            maskw_t = sbF("maskw_t", [128, 4096], BF16)
            tcsq = sbF("tcsq", [128, 512]); tssq = sbF("tssq", [128, 512])
            tcsk = sbF("tcsk", [128, 512]); tssk = sbF("tssk", [128, 512])
            fqs_t = sbF("fqs_t", [128, 64]); fks_t = sbF("fks_t", [128, 64])
            fwo_t = sbF("fwo_t", [128, 4096])
            qTr = sbF("qTr", [128, 2048])
            kTr = sbF("kTr", [128, 4096])
            vr = sbF("vr", [128, 4096])
            qtk = sbF("qtk", [128, 512])
            sql2 = sbF("sql2", [128, 512])
            rtmp = sbF("rtmp", [128, 512])
            ssq8 = sbF("ssq8", [128, 8]); invq = sbF("invq", [128, 8])
            Pf = sbF("Pf", [128, 1024])
            PTf = sbF("PTf", [128, 1024])
            oTf = sbF("oTf", [128, 2048])
            fmx = sbF("fmx", [128, 1]); frb = sbF("frb", [128, 1])
            fsum = sbF("fsum", [128, 1]); finv = sbF("finv", [128, 1])
            h0t = sbF("h0t", [128, 512])
            stash = sbF("stash", [128, 16])

            def ffa_loads(sync, layer=layer):
                ins = sync.dma_start(
                    out=maskw_t[:].rearrange("p (q x) -> p q x", x=1024),
                    in_=din["maskw"][:, :, :].transpose([1, 0, 2]))
                S.inc(ins, "WL", 16)
                for dst, nm in [(tcsq, "csq"), (tssq, "ssq"), (tcsk, "csk"), (tssk, "ssk")]:
                    ins = sync.dma_start(
                        out=dst[:].rearrange("p (t e) -> p t e", e=32),
                        in_=din[nm][:, :].rearrange("(t p) e -> p t e", p=128))
                    S.inc(ins, "WL", 16)
                ins = sync.dma_start(out=fqs_t[:], in_=din["fqs"][layer]); S.inc(ins, "WL", 16)
                ins = sync.dma_start(out=fks_t[:], in_=din["fks"][layer]); S.inc(ins, "WL", 16)
                ins = sync.dma_start(
                    out=fwo_t[:].rearrange("p (pr c) -> p pr c", c=1024),
                    in_=din["fwo"][layer].rearrange("(pr p) c -> p pr c", p=128))
                S.inc(ins, "WL", 16)
                ins = sync.dma_start(
                    out=wstrips[0][:].rearrange("p (cc x) -> p cc x", x=512),
                    in_=din["fwqkv"][layer][:, 0:512].rearrange("(cc p) x -> p cc x", p=128))
                S.inc(ins, "WL", 16)
            block.sync(ffa_loads)
            strip_wl_f = {0: S.c["WL"]}

            def ffa_prep(vector, layer=layer):
                S.wait(vector, "WL")
                vector.memset(h0t[:], 0.0)
                for i, cc in enumerate(range(4, 8)):
                    vector.tensor_copy(out=h0t[:, i * 128 + 1:(i + 1) * 128],
                                       in_=xs[:, cc * N: cc * N + 127])
                for w_ in range(1, 4):
                    for i, cc in enumerate(range(4, 8)):
                        ins = vector.tensor_copy(
                            out=stash[:, (w_ - 1) * 4 + i:(w_ - 1) * 4 + i + 1],
                            in_=xs[:, cc * N + w_ * 512 - 1: cc * N + w_ * 512])
                S.inc(ins, "DVE")
            block.vector(ffa_prep)
            prep_dve = S.c["DVE"]

            def shifted_lhs(cc, gt):
                if cc < 4:
                    return xs[:, cc * N + gt * 128: cc * N + (gt + 1) * 128]
                if gt == 0:
                    return h0t[:, (cc - 4) * 128:(cc - 4) * 128 + 128]
                return xs[:, cc * N + gt * 128 - 1: cc * N + (gt + 1) * 128 - 1]

            wtile = sbF("wtile", [128, 512])

            gate = {"strip_pe": {}, "exp_act": {}, "tev_act": 0, "pvv_act": 0,
                    "qkv_evac": 0, "wo_dve": 0, "sc_exp": {}, "exp_last": {0: 0, 1: 0},
                    "nrm_last": 0, "pf_pe": 0}

            for w_ in range(4):
                wpar = w_ % 2
                wprev = (w_ - 1) % 2
                if w_ >= 1:
                    def mk_wtile(vector, w_=w_, gd=prep_dve):
                        S.wait(vector, "PE")
                        for i, cc in enumerate(range(4, 8)):
                            vector.tensor_copy(
                                out=wtile[:, i * 128:i * 128 + 1],
                                in_=stash[:, (w_ - 1) * 4 + i:(w_ - 1) * 4 + i + 1])
                            ins = vector.tensor_copy(
                                out=wtile[:, i * 128 + 1:(i + 1) * 128],
                                in_=xs[:, cc * N + w_ * 512: cc * N + w_ * 512 + 127])
                        S.inc(ins, "DVE")
                    block.vector(mk_wtile)
                mk_dve = S.c["DVE"]

                def shifted_lhs_w(cc, nt, w_=w_):
                    gt = 4 * w_ + nt
                    if cc >= 4 and nt == 0:
                        if w_ == 0:
                            return h0t[:, (cc - 4) * 128:(cc - 4) * 128 + 128]
                        return wtile[:, (cc - 4) * 128:(cc - 4) * 128 + 128]
                    return shifted_lhs(cc, gt)

                for sg in range(3):
                    pidx = w_ * 3 + sg
                    buf = wstrips[pidx % 2]
                    if pidx + 1 < 12:
                        def pf(sync, pidx=pidx, layer=layer):
                            nsg = (pidx + 1) % 3
                            if pidx - 1 >= 0 and (pidx - 1) in gate["strip_pe"]:
                                sync.wait_ge(S.h["PE"], gate["strip_pe"][pidx - 1])
                            ins = sync.dma_start(
                                out=wstrips[(pidx + 1) % 2][:].rearrange("p (cc x) -> p cc x", x=512),
                                in_=din["fwqkv"][layer][:, nsg * 512:(nsg + 1) * 512]
                                    .rearrange("(cc p) x -> p cc x", p=128))
                            strip_wl_f[pidx + 1] = S.inc(ins, "WL", 16)
                        block.sync(pf)

                    for nt in range(4):
                        gt = 4 * w_ + nt
                        def qkvmm(tensor, sg=sg, nt=nt, w_=w_, buf=buf, pidx=pidx,
                                  gmk=mk_dve):
                            tensor.wait_ge(S.h["WL"], strip_wl_f[pidx])
                            tensor.wait_ge(S.h["DVE"], gmk)
                            if gate["qkv_evac"]:
                                tensor.wait_ge(S.h["DVE"], gate["qkv_evac"])
                            if gate["wo_dve"]:
                                tensor.wait_ge(S.h["DVE"], gate["wo_dve"])
                            for cc in range(8):
                                ins = tensor.matmul(
                                    ph_pv[:, :], shifted_lhs_w(cc, nt),
                                    buf[:, cc * 512:(cc + 1) * 512],
                                    start=(cc == 0), stop=(cc == 7))
                            S.inc(ins, "PE")
                        block.tensor(qkvmm)
                        if nt == 3:
                            gate["strip_pe"][pidx] = S.c["PE"]
                        mm_pe = S.c["PE"]

                        if sg == 2:
                            def vev(vector, nt=nt, wpar=wpar, gate_pe=mm_pe):
                                vector.wait_ge(S.h["PE"], gate_pe)
                                ins = vector.tensor_copy(
                                    out=vr[:, wpar * 2048 + nt * 512: wpar * 2048 + (nt + 1) * 512],
                                    in_=ph_pv[:, :])
                                gate["qkv_evac"] = S.inc(ins, "DVE")
                            block.vector(vev)
                            continue

                        def qev(vector, gate_pe=mm_pe):
                            vector.wait_ge(S.h["PE"], gate_pe)
                            ins = vector.tensor_copy(out=qtk[:], in_=ph_pv[:, :])
                            gate["qkv_evac"] = S.inc(ins, "DVE")
                            ins = vector.tensor_tensor(out=sql2[:], in0=qtk[:], in1=qtk[:],
                                                       op=ALU.mult)
                            S.inc(ins, "DVE")
                        block.vector(qev)
                        qe_act = S.c["DVE"]

                        def l2x(vector, sg=sg, gt=gt, gate_a=qe_act):
                            vector.wait_ge(S.h["ACT"], gate_a)
                            ins = vector.tensor_reduce(
                                out=ssq8[:], in_=sql2[:].rearrange("p (h e) -> p h e", e=64),
                                axis=AX.X, op=ALU.add)
                            v = S.inc(ins, "DVE")
                            vector.wait_ge(S.h["DVE"], v)
                            S.inc(ins, "DVE", 0)
                            return
                        def l2a(vector, sg=sg, gt=gt):
                            ins = vector.tensor_reduce(
                                out=ssq8[:], in_=sql2[:].rearrange("p (h e) -> p h e", e=64),
                                axis=AX.X, op=ALU.add)
                            v = S.inc(ins, "DVE")
                            vector.wait_ge(S.h["DVE"], v)
                            ins = vector.tensor_scalar(out=ssq8[:], in0=ssq8[:],
                                                       scalar1=1e-24, scalar2=None,
                                                       op0=ALU.max)
                            S.inc(ins, "DVE")
                        block.vector(l2a)
                        l2_dve = S.c["DVE"]

                        def l2b(scalar, gate_d=l2_dve):
                            scalar.wait_ge(S.h["DVE"], gate_d)
                            ins = scalar.activation(out=invq[:], in_=ssq8[:], func=AF.Ln)
                            v = S.inc(ins, "ACT")
                            scalar.wait_ge(S.h["ACT"], v)
                            ins = scalar.activation(out=invq[:], in_=invq[:], func=AF.Exp,
                                                    scale=-0.5)
                            S.inc(ins, "ACT")
                        block.scalar(l2b)
                        l2b_act = S.c["ACT"]

                        def l2c(vector, sg=sg, gt=gt, gate_a=l2b_act):
                            vector.wait_ge(S.h["ACT"], gate_a)
                            vector.tensor_tensor(
                                out=qtk[:].rearrange("p (h e) -> p h e", e=64),
                                in0=qtk[:].rearrange("p (h e) -> p h e", e=64),
                                in1=invq[:].unsqueeze(2).broadcast_to([128, 8, 64]),
                                op=ALU.mult)
                            sc = fqs_t if sg == 0 else fks_t
                            vector.tensor_tensor(
                                out=qtk[:].rearrange("p (h e) -> p h e", e=64),
                                in0=qtk[:].rearrange("p (h e) -> p h e", e=64),
                                in1=sc[:].unsqueeze(1).broadcast_to([128, 8, 64]),
                                op=ALU.mult)
                            cs = (tcsq if sg == 0 else tcsk)[:, gt * 32:(gt + 1) * 32]
                            ss = (tssq if sg == 0 else tssk)[:, gt * 32:(gt + 1) * 32]
                            csb = cs.unsqueeze(1).broadcast_to([128, 8, 32])
                            ssb = ss.unsqueeze(1).broadcast_to([128, 8, 32])
                            q3 = qtk[:].rearrange("p (h e) -> p h e", e=64)
                            r3 = rtmp[:].rearrange("p (h e) -> p h e", e=64)
                            vector.tensor_tensor(out=r3[:, :, 0:32], in0=q3[:, :, 0:32],
                                                 in1=csb, op=ALU.mult)
                            vector.tensor_tensor(out=sql2[:].rearrange("p (h e) -> p h e", e=64)[:, :, 0:32],
                                                 in0=q3[:, :, 32:64], in1=ssb, op=ALU.mult)
                            vector.tensor_tensor(out=r3[:, :, 32:64], in0=q3[:, :, 32:64],
                                                 in1=csb, op=ALU.mult)
                            vector.tensor_tensor(out=sql2[:].rearrange("p (h e) -> p h e", e=64)[:, :, 32:64],
                                                 in0=q3[:, :, 0:32], in1=ssb, op=ALU.mult)
                            s3 = sql2[:].rearrange("p (h e) -> p h e", e=64)
                            vector.tensor_tensor(out=r3[:, :, 0:32], in0=r3[:, :, 0:32],
                                                 in1=s3[:, :, 0:32], op=ALU.subtract)
                            ins = vector.tensor_tensor(out=r3[:, :, 32:64], in0=r3[:, :, 32:64],
                                                       in1=s3[:, :, 32:64], op=ALU.add)
                            S.inc(ins, "DVE")
                        block.vector(l2c)
                        rot_dve = S.c["DVE"]

                        def qkT(tensor, gate_d=rot_dve):
                            tensor.wait_ge(S.h["DVE"], gate_d)
                            if gate["tev_act"]:
                                tensor.wait_ge(S.h["DVE"], gate["tev_act"])
                            for pr in range(4):
                                ins = tensor.matmul(ph_t[:, pr * 128:(pr + 1) * 128],
                                                    rtmp[:, pr * 128:(pr + 1) * 128],
                                                    ident[:, :], start=(pr == 0),
                                                    stop=(pr == 3), is_transpose=True)
                            S.inc(ins, "PE")
                        block.tensor(qkT)
                        t_pe = S.c["PE"]

                        def qkTev(vector, sg=sg, nt=nt, wpar=wpar, gate_pe=t_pe):
                            vector.wait_ge(S.h["PE"], gate_pe)
                            if sg == 0:
                                dst = qTr[:].rearrange("p (pr x) -> p pr x", x=512)[
                                    :, :, nt * 128:(nt + 1) * 128]
                            else:
                                dst = kTr[:].rearrange("p (pr x) -> p pr x", x=1024)[
                                    :, :, wpar * 512 + nt * 128: wpar * 512 + (nt + 1) * 128]
                            ins = vector.tensor_copy(
                                out=dst, in_=ph_t[:].rearrange("p (pr n) -> p pr n", n=128))
                            gate["tev_act"] = S.inc(ins, "DVE")
                        block.vector(qkTev)

                nks = 1 if w_ == 0 else 2
                for pr in range(4):
                    pv_started = False
                    for qt in range(4):
                        for e_ in range(2):
                            def sc_mm(tensor, pr=pr, qt=qt, e_=e_, w_=w_, nks=nks,
                                      gexp=None):
                                if gate["exp_last"][e_]:
                                    tensor.wait_ge(S.h["ACT"], gate["exp_last"][e_])
                                lo = e_ * 1024
                                for ks in range(nks):
                                    wp = (w_ - 1 + ks) % 2 if w_ >= 1 else w_ % 2
                                    ins = tensor.matmul(
                                        ph_a[:, lo + ks * 512: lo + (ks + 1) * 512],
                                        qTr[0 + e_ * 64: 64 + e_ * 64,
                                            pr * 512 + qt * 128: pr * 512 + (qt + 1) * 128],
                                        kTr[0 + e_ * 64: 64 + e_ * 64,
                                            pr * 1024 + wp * 512: pr * 1024 + (wp + 1) * 512],
                                        start=True, stop=True,
                                        tile_position=(e_ * 64, 0))
                                S.inc(ins, "PE")
                            block.tensor(sc_mm)
                            sc_pe = S.c["PE"]

                            def sc_sm(vector, pr=pr, qt=qt, e_=e_, w_=w_, nks=nks,
                                      gate_pe=sc_pe):
                                vector.wait_ge(S.h["PE"], gate_pe)
                                lo = e_ * 1024
                                nk = nks * 512
                                if w_ == 0:
                                    mk = maskw_t[:, qt * 1024 + 512: qt * 1024 + 1024]
                                else:
                                    mk = maskw_t[:, qt * 1024: qt * 1024 + nk]
                                vector.tensor_tensor(out=ph_a[:, lo:lo + nk],
                                                     in0=ph_a[:, lo:lo + nk],
                                                     in1=mk, op=ALU.add)
                                ins = vector.reduce_max(out=fmx[:], in_=ph_a[:, lo:lo + nk],
                                                        axis=AX.X)
                                v = S.inc(ins, "DVE")
                                vector.wait_ge(S.h["DVE"], v)
                                ins = vector.tensor_scalar(out=frb[:], in0=fmx[:],
                                                           scalar1=-8.0, scalar2=None,
                                                           op0=ALU.mult)
                                S.inc(ins, "DVE")
                            block.vector(sc_sm)
                            sm_dve = S.c["DVE"]

                            def sc_exp(scalar, pr=pr, qt=qt, e_=e_, nks=nks, gate_d=sm_dve,
                                       gnrm=None, gpf=None):
                                scalar.wait_ge(S.h["DVE"], gate_d)
                                if gate["nrm_last"]:
                                    scalar.wait_ge(S.h["DVE"], gate["nrm_last"])
                                if gate["pf_pe"]:
                                    scalar.wait_ge(S.h["PE"], gate["pf_pe"])
                                lo = e_ * 1024
                                nk = nks * 512
                                ins = scalar.activation(out=Pf[:, 0:nk], in_=ph_a[:, lo:lo + nk],
                                                        func=AF.Exp, bias=frb[:], scale=8.0,
                                                        accum_out=fsum[:])
                                gate["exp_last"][e_] = S.inc(ins, "ACT")
                            block.scalar(sc_exp)
                            exp_act = S.c["ACT"]

                            def sc_nrm(vector, nks=nks, gate_a=exp_act):
                                vector.wait_ge(S.h["ACT"], gate_a)
                                ins = vector.reciprocal(out=finv[:], in_=fsum[:])
                                v = S.inc(ins, "DVE")
                                vector.wait_ge(S.h["DVE"], v)
                                ins = vector.tensor_scalar(out=Pf[:, 0:nks * 512],
                                                           in0=Pf[:, 0:nks * 512],
                                                           scalar1=finv[:, 0:1],
                                                           scalar2=None, op0=ALU.mult)
                                gate["nrm_last"] = S.inc(ins, "DVE")
                            block.vector(sc_nrm)
                            nrm_dve = S.c["DVE"]

                            def p_T(tensor, nks=nks, gate_d=nrm_dve):
                                tensor.wait_ge(S.h["DVE"], gate_d)
                                if gate["tev_act"]:
                                    tensor.wait_ge(S.h["ACT"], gate["tev_act"])
                                for j in range(4 * nks):
                                    bank = j % 4
                                    if j == 4:
                                        pass
                                    ins = tensor.matmul(ph_t[:, bank * 128:(bank + 1) * 128],
                                                        Pf[:, j * 128:(j + 1) * 128],
                                                        ident[:, :], start=(bank == 0),
                                                        stop=(bank == 3), is_transpose=True)
                                    if bank == 3:
                                        S.inc(ins, "PE")
                                        gpe = S.c["PE"]
                                        g0 = j - 3
                                        def pTev(scalar, g0=g0, gate_pe=gpe):
                                            scalar.wait_ge(S.h["PE"], gate_pe)
                                            ins = scalar.activation(
                                                out=PTf[:, g0 * 128:(g0 + 4) * 128],
                                                in_=ph_t[:], func=AF.Copy)
                                            gate["tev_act"] = S.inc(ins, "ACT")
                                        block.scalar(pTev)
                            for grp in range(nks):
                                def pT_g(tensor, grp=grp, nks=nks, gate_d=nrm_dve):
                                    tensor.wait_ge(S.h["DVE"], gate_d)
                                    if gate["tev_act"]:
                                        tensor.wait_ge(S.h["DVE"], gate["tev_act"])
                                    for j in range(4):
                                        kc = grp * 4 + j
                                        ins = tensor.matmul(
                                            ph_t[:, j * 128:(j + 1) * 128],
                                            Pf[:, kc * 128:(kc + 1) * 128],
                                            ident[:, :], start=(j == 0), stop=(j == 3),
                                            is_transpose=True)
                                    S.inc(ins, "PE")
                                block.tensor(pT_g)
                                gpe = S.c["PE"]

                                def pTev_g(vector, grp=grp, gate_pe=gpe):
                                    vector.wait_ge(S.h["PE"], gate_pe)
                                    ins = vector.tensor_copy(
                                        out=PTf[:, grp * 512:(grp + 1) * 512],
                                        in_=ph_t[:])
                                    gate["tev_act"] = S.inc(ins, "DVE")
                                block.vector(pTev_g)

                            ptev_act = S.c["DVE"]

                            gate["pf_pe"] = S.c["PE"]
                            def pv_mm(tensor, pr=pr, qt=qt, e_=e_, w_=w_, nks=nks,
                                      gate_a=ptev_act, first=(qt == 0 and e_ == 0)):
                                tensor.wait_ge(S.h["DVE"], gate_a)
                                if first and gate["pvv_act"]:
                                    tensor.wait_ge(S.h["DVE"], gate["pvv_act"])
                                h_ = pr * 2 + e_
                                nkc = 4 * nks
                                for kc in range(nkc):
                                    ks, kt = divmod(kc, 4)
                                    wp = (w_ - 1 + ks) % 2 if w_ >= 1 else w_ % 2
                                    pvbank = ph_pj if e_ == 0 else ph_s
                                    ins = tensor.matmul(
                                        pvbank[e_ * 64:(e_ + 1) * 64, qt * 128:(qt + 1) * 128],
                                        vr[:, wp * 2048 + kt * 512 + h_ * 64: wp * 2048 + kt * 512 + h_ * 64 + 64],
                                        PTf[:, kc * 128:(kc + 1) * 128],
                                        start=(kc == 0 and qt == 0),
                                        stop=(kc == nkc - 1 and qt == 3),
                                        tile_position=(0, e_ * 64))
                                S.inc(ins, "PE")
                            block.tensor(pv_mm)

                    pv_pe = S.c["PE"]
                    def pv_ev(vector, pr=pr, gate_pe=pv_pe):
                        vector.wait_ge(S.h["PE"], gate_pe)
                        vector.tensor_copy(out=oTf[0:64, pr * 512:(pr + 1) * 512],
                                           in_=ph_pj[0:64, :])
                        ins = vector.tensor_copy(out=oTf[64:128, pr * 512:(pr + 1) * 512],
                                                 in_=ph_s[64:128, :])
                        gate["pvv_act"] = S.inc(ins, "DVE")
                    block.vector(pv_ev)

                wo_act = S.c["DVE"]
                for cc in range(8):
                    def wo_mm(tensor, cc=cc, gate_a=wo_act):
                        tensor.wait_ge(S.h["DVE"], gate_a)
                        if cc >= 1 and gate["wo_dve"]:
                            tensor.wait_ge(S.h["DVE"], gate["wo_dve"] - 1 + 1 - 1) if False else None
                        if cc >= 1:
                            tensor.wait_ge(S.h["DVE"], gate["wo_dve"])
                        for pr in range(4):
                            ins = tensor.matmul(ph_pv[:, :],
                                                fwo_t[:, pr * 1024 + cc * 128: pr * 1024 + (cc + 1) * 128],
                                                oTf[:, pr * 512:(pr + 1) * 512],
                                                start=(pr == 0), stop=(pr == 3))
                        S.inc(ins, "PE")
                    block.tensor(wo_mm)
                    wo_pe = S.c["PE"]

                    def wo_add(vector, cc=cc, w_=w_, gate_pe=wo_pe):
                        vector.wait_ge(S.h["PE"], gate_pe)
                        dst = xs[:, cc * N + w_ * 512: cc * N + (w_ + 1) * 512]
                        ins = vector.scalar_tensor_tensor(out=dst, in0=ph_pv[:, :],
                                                          scalar=1.0, in1=dst,
                                                          op0=ALU.mult, op1=ALU.add)
                        gate["wo_dve"] = S.inc(ins, "DVE")
                    block.vector(wo_add)

            esFa.close()

            phase_barrier(f"mlp{layer}")
            esFm = ExitStack()
            sbM = lambda n_, shp, dt_=F32: esFm.enter_context(
                nc.sbuf_tensor(n_ + f"_l{layer}", shp, dt_))
            flng_t = sbM("flng_t", [128, 1024]); flnb_t = sbM("flnb_t", [128, 1024])
            fb1a_t = sbM("fb1a_t", [128, NJ]); fb1g_t = sbM("fb1g_t", [128, NJ])
            fb2_t = sbM("fb2_t", [128, 8])
            h2nT = sbM("h2nT", [128, 8 * 512])
            actT = sbM("actT", [128, NJ * 512])
            glt = sbM("glt", [128, 512]); att = sbM("att", [128, 512])
            w1b = [sbM(f"w1b{i}", [128, 2048]) for i in range(2)]
            w2b = [sbM(f"w2b{i}", [128, NJ * 128]) for i in range(2)]
            mstash = sbM("mstash", [128, 16])
            mh0 = sbM("mh0", [128, 512])
            mwt = sbM("mwt", [128, 512])

            def mlp_loads(sync, layer=layer):
                for dst, nm in [(flng_t, "flng"), (flnb_t, "flnb"), (fb1a_t, "fb1a"),
                                (fb1g_t, "fb1g"), (fb2_t, "fb2")]:
                    ins = sync.dma_start(out=dst[:], in_=din[nm][layer])
                    S.inc(ins, "WL", 16)
            block.sync(mlp_loads)
            mlp_wl = S.c["WL"]

            def mlp_prep(vector, layer=layer):
                vector.wait_ge(S.h["WL"], mlp_wl)
                vector.memset(mh0[:], 0.0)
                for i, cc in enumerate(range(4, 8)):
                    vector.tensor_copy(out=mh0[:, i * 128 + 1:(i + 1) * 128],
                                       in_=xs[:, cc * N: cc * N + 127])
                for s_ in range(1, 4):
                    for i, cc in enumerate(range(4, 8)):
                        ins = vector.tensor_copy(
                            out=mstash[:, (s_ - 1) * 4 + i:(s_ - 1) * 4 + i + 1],
                            in_=xs[:, cc * N + s_ * 512 - 1: cc * N + s_ * 512])
                S.inc(ins, "DVE")
            block.vector(mlp_prep)
            mprep_dve = S.c["DVE"]

            mgate = {"w1_pe": {}, "w2_pe": {}, "act_j": 0, "w2_stt": 0, "w1_wl": {},
                     "w2_wl": {}}

            for s_ in range(4):
                if s_ >= 1:
                    def mk_mwt(vector, s_=s_):
                        S.wait(vector, "PE")
                        for i, cc in enumerate(range(4, 8)):
                            vector.tensor_copy(
                                out=mwt[:, i * 128:i * 128 + 1],
                                in_=mstash[:, (s_ - 1) * 4 + i:(s_ - 1) * 4 + i + 1])
                            ins = vector.tensor_copy(
                                out=mwt[:, i * 128 + 1:(i + 1) * 128],
                                in_=xs[:, cc * N + s_ * 512: cc * N + s_ * 512 + 127])
                        S.inc(ins, "DVE")
                    block.vector(mk_mwt)
                mwt_dve = S.c["DVE"]

                def msrc(cc, ntr, s_=s_):
                    gt = 4 * s_ + ntr
                    if cc >= 4 and ntr == 0:
                        return mh0[:, (cc - 4) * 128:(cc - 4) * 128 + 128] if s_ == 0 else \
                            mwt[:, (cc - 4) * 128:(cc - 4) * 128 + 128]
                    if cc < 4:
                        return xs[:, cc * N + gt * 128: cc * N + (gt + 1) * 128]
                    return xs[:, cc * N + gt * 128 - 1: cc * N + (gt + 1) * 128 - 1]

                ln_state["first_gate"] = (S.c["DVE"], S.c["ACT"])
                for ntr in range(4):
                    emit_ln_tile(4 * s_ + ntr, gamma_beta=(flng_t[:], flnb_t[:]),
                                 write_back=True,
                                 src_fn=(lambda cc, ntr=ntr: msrc(cc, ntr)),
                                 back_dst_fn=(lambda cc, ntr=ntr:
                                              h2nT[:, cc * 512 + ntr * 128:
                                                   cc * 512 + (ntr + 1) * 128]))
                h2n_act = S.c["ACT"]

                for jj in range(NJ):
                    def w1_load(sync, jj=jj, s_=s_, layer=layer):
                        if jj >= 2 and (s_, jj - 2) in mgate["w1_pe"]:
                            sync.wait_ge(S.h["PE"], mgate["w1_pe"][(s_, jj - 2)])
                        elif jj < 2 and s_ > 0 and (s_ - 1, NJ - 2 + jj) in mgate["w1_pe"]:
                            sync.wait_ge(S.h["PE"], mgate["w1_pe"][(s_ - 1, NJ - 2 + jj)])
                        ins = sync.dma_start(
                            out=w1b[jj % 2][:].rearrange("p (t cc x) -> p t cc x", t=2, x=128),
                            in_=din["fw1"][layer, jj].transpose([2, 0, 1, 3]))
                        mgate["w1_wl"][(s_, jj)] = S.inc(ins, "WL", 16)
                    block.sync(w1_load)

                    def w1_mm(tensor, jj=jj, s_=s_, ga=h2n_act):
                        tensor.wait_ge(S.h["WL"], mgate["w1_wl"][(s_, jj)])
                        tensor.wait_ge(S.h["ACT"], ga)
                        if mgate["act_j"]:
                            tensor.wait_ge(S.h["ACT"], mgate["act_j"])
                        buf = w1b[jj % 2]
                        for t_ in range(2):
                            for cc in range(8):
                                ins = tensor.matmul(
                                    ph_a[:, t_ * 512:(t_ + 1) * 512],
                                    buf[:, t_ * 1024 + cc * 128: t_ * 1024 + (cc + 1) * 128],
                                    h2nT[:, cc * 512:(cc + 1) * 512],
                                    start=(cc == 0), stop=(cc == 7))
                        mgate["w1_pe"][(s_, jj)] = S.inc(ins, "PE")
                    block.tensor(w1_mm)
                    w1_pe = S.c["PE"]

                    def geluact(scalar, jj=jj, gate_pe=w1_pe):
                        scalar.wait_ge(S.h["PE"], gate_pe)
                        S.wait(scalar, "DVE")
                        scalar.activation(out=glt[:], in_=ph_a[:, 512:1024], func=AF.Gelu,
                                          bias=fb1g_t[:, jj:jj + 1])
                        ins = scalar.activation(out=att[:], in_=ph_a[:, 0:512],
                                                func=AF.Identity,
                                                bias=fb1a_t[:, jj:jj + 1])
                        mgate["act_j"] = S.inc(ins, "ACT")
                    block.scalar(geluact)
                    ga_act = S.c["ACT"]

                    def actmul(vector, jj=jj, gate_a=ga_act):
                        vector.wait_ge(S.h["ACT"], gate_a)
                        ins = vector.tensor_tensor(out=actT[:, jj * 512:(jj + 1) * 512],
                                                   in0=att[:], in1=glt[:], op=ALU.mult)
                        S.inc(ins, "DVE")
                    block.vector(actmul)
                act_dve = S.c["DVE"]

                for m_ in range(8):
                    def w2_load(sync, m_=m_, s_=s_, layer=layer):
                        if m_ >= 2 and (s_, m_ - 2) in mgate["w2_pe"]:
                            sync.wait_ge(S.h["PE"], mgate["w2_pe"][(s_, m_ - 2)])
                        elif m_ < 2 and s_ > 0 and (s_ - 1, 6 + m_) in mgate["w2_pe"]:
                            sync.wait_ge(S.h["PE"], mgate["w2_pe"][(s_ - 1, 6 + m_)])
                        ins = sync.dma_start(
                            out=w2b[m_ % 2][:].rearrange("p (j x) -> p j x", x=128),
                            in_=din["fw2"][layer, m_].transpose([1, 0, 2]))
                        mgate["w2_wl"][(s_, m_)] = S.inc(ins, "WL", 16)
                    block.sync(w2_load)

                    def w2_mm(tensor, m_=m_, s_=s_, ga=act_dve):
                        tensor.wait_ge(S.h["WL"], mgate["w2_wl"][(s_, m_)])
                        tensor.wait_ge(S.h["DVE"], ga)
                        if mgate["w2_stt"]:
                            tensor.wait_ge(S.h["DVE"], mgate["w2_stt"])
                        buf = w2b[m_ % 2]
                        for jj in range(NJ):
                            ins = tensor.matmul(ph_a[:, 1024:1536],
                                                buf[:, jj * 128:(jj + 1) * 128],
                                                actT[:, jj * 512:(jj + 1) * 512],
                                                start=(jj == 0), stop=(jj == NJ - 1))
                        mgate["w2_pe"][(s_, m_)] = S.inc(ins, "PE")
                    block.tensor(w2_mm)
                    w2_pe = S.c["PE"]

                    def w2_add(vector, m_=m_, s_=s_, gate_pe=w2_pe):
                        vector.wait_ge(S.h["PE"], gate_pe)
                        dst = xs[:, m_ * N + s_ * 512: m_ * N + (s_ + 1) * 512]
                        ins = vector.scalar_tensor_tensor(out=dst, in0=ph_a[:, 1024:1536],
                                                          scalar=1.0, in1=dst,
                                                          op0=ALU.mult, op1=ALU.add)
                        mgate["w2_stt"] = S.inc(ins, "DVE")
                    block.vector(w2_add)

            def b2add(vector):
                S.wait(vector, "PE")
                for m_ in range(8):
                    ins = vector.tensor_scalar(out=xs[:, m_ * N:(m_ + 1) * N],
                                               in0=xs[:, m_ * N:(m_ + 1) * N],
                                               scalar1=fb2_t[:, m_:m_ + 1],
                                               scalar2=None, op0=ALU.add)
                S.inc(ins, "DVE")
            block.vector(b2add)
            esFm.close()



        phase_barrier("ln2")
        esL2 = ExitStack()
        xinb = esL2.enter_context(nc.sbuf_tensor("xinb", [128, 2048], F32))
        xin_wl = {}
        for cc in range(8):
            def xin_load(sync, cc=cc):
                if cc >= 1:
                    S.wait(sync, "DVE")
                ins = sync.dma_start(out=xinb[:],
                                     in_=xin_sp[:, :].rearrange("(c p) n -> p c n", p=128)[:, cc])
                xin_wl[cc] = S.inc(ins, "WL", 16)
            block.sync(xin_load)

            def xin_add(vector, cc=cc):
                vector.wait_ge(S.h["WL"], xin_wl[cc])
                ins = vector.tensor_tensor(out=xs[:, cc * N:(cc + 1) * N],
                                           in0=xs[:, cc * N:(cc + 1) * N],
                                           in1=xinb[:], op=ALU.add)
                S.inc(ins, "DVE")
            block.vector(xin_add)

        ln_state["first_gate"] = (S.c["DVE"], S.c["ACT"])
        ln_state["prev_back_act"] = 0
        ln_state["prev_apply_pe"] = 0
        for nt in range(NT):
            emit_ln_tile(nt, write_back=False,
                         out_rows=out[nt * 128:(nt + 1) * 128, :])
        esL2.close()

        def final_wait(gpsimd):
            gpsimd.wait_ge(S.h["SP"], S.c["SP"])
        block.gpsimd(final_wait)

    print(f"build done {time.time()-t0:.1f}s n_inst={nc.n_instructions()}")
    return nc, din



def kernel(**inputs):
    shared = prep_shared(inputs)
    x = np.asarray(inputs["L0_attn_x"])
    nc, _ = build()
    in_maps = []
    for b in range(8):
        m = dict(shared)
        m["xT"] = np.ascontiguousarray(x[b].T)
        in_maps.append(m)
    res = run_bass_kernel_spmd(nc, in_maps, core_ids=list(range(8)))
    out = np.stack([np.asarray(res.results[b]["out"]) for b in range(8)], axis=0)
    return out.astype(np.float32)


# version 27
# speedup vs baseline: 1.0751x; 1.0517x over previous
import sys
import time
import numpy as np
from contextlib import ExitStack

import concourse.bass as bass
import concourse.mybir as mybir
from concourse.bass import AP
from concourse.bass_utils import run_bass_kernel_spmd

F32 = mybir.dt.float32
BF16 = mybir.dt.bfloat16
AF = mybir.ActivationFunctionType
ALU = mybir.AluOpType
AX = mybir.AxisListType

N = 2048
C = 1024
NH = 16
NT = N // 128
NC = C // 128

FH = 8
FD = 64
W = 512
HID = 2730
HIDP = 2816
NJ = HIDP // 128
NH_RUN = 16


def host_tables():
    import jax
    import jax.numpy as jnp
    cpu = jax.local_devices(backend="cpu")[0]
    with jax.default_device(cpu):
        inv = 1.0 / 10000.0 ** (jnp.arange(0, 128, 2, dtype=jnp.float32) / 128)
        freqs = jnp.arange(N, dtype=jnp.float32)[:, None] * inv[None, :]
        acos = np.asarray(jnp.cos(freqs), np.float32)
        asin = np.asarray(jnp.sin(freqs), np.float32)

        half_idx = jnp.arange(0, FD, 2, dtype=jnp.float32)
        inv2 = 1.0 / 10000.0 ** (half_idx / FD)
        t = jnp.arange(N, dtype=jnp.float32)
        f2 = t[:, None] * inv2[None, :]
        base = (half_idx + 0.4 * FD) / (1.4 * FD)
        power = (t - N // 2) / float(W // 2)
        scale = base[None, :] ** power[:, None]
        cos2 = np.asarray(jnp.cos(f2), np.float64)
        sin2 = np.asarray(jnp.sin(f2), np.float64)
        sc64 = np.asarray(scale, np.float64)
    csq = (cos2 * sc64).astype(np.float32)
    ssq = (sin2 * sc64).astype(np.float32)
    csk = (cos2 / sc64).astype(np.float32)
    ssk = (sin2 / sc64).astype(np.float32)
    return acos, asin, csq, ssq, csk, ssk


def host_masks():
    MNEG = np.float32(-1.0e30)
    t_idx = np.arange(4)[:, None, None]
    p_idx = np.arange(128)[None, :, None]
    kap = np.arange(1024)[None, None, :]
    rel = 128 * t_idx + p_idx
    allowed = (kap >= rel) & (kap <= rel + 512)
    import ml_dtypes
    maskw = np.where(allowed, np.float32(0), MNEG).astype(ml_dtypes.bfloat16)
    return maskw, None


def prep_shared(inp):
    d = {}
    Wqkv = np.asarray(inp["L0_attn_W_qkv"])
    Wt = np.ascontiguousarray(Wqkv.T)
    vcols = np.concatenate([np.concatenate([4 * 1024 + h * 64 + np.arange(64),
                                            5 * 1024 + h * 64 + np.arange(64)])
                            for h in range(NH)])
    order = np.concatenate([np.arange(0, 2048), np.arange(2048, 4096), vcols])
    Wre = Wt[:, order]
    Wdev = np.empty((3, 2, 8, 128, 1024), np.float32)
    for p in range(3):
        for hf in range(2):
            blk = Wre[:, p * 2048 + hf * 1024: p * 2048 + (hf + 1) * 1024]
            Wdev[p, hf] = blk.reshape(8, 128, 1024)
    d["wqkv"] = Wdev
    d["wproj"] = np.ascontiguousarray(
        np.asarray(inp["L0_attn_W_proj"]).T.reshape(NH, 128, 1024))
    d["lamq1"] = np.asarray(inp["L0_attn_lambda_q1"]).reshape(1, 64)
    d["lamq2"] = np.asarray(inp["L0_attn_lambda_q2"]).reshape(1, 64)
    d["lamk1"] = np.asarray(inp["L0_attn_lambda_k1"]).reshape(1, 64)
    d["lamk2"] = np.asarray(inp["L0_attn_lambda_k2"]).reshape(1, 64)

    d["fwqkv"] = np.ascontiguousarray(np.asarray(inp["L0_ffn_wqkv"]))
    d["fwo"] = np.ascontiguousarray(np.asarray(inp["L0_ffn_wo"]))
    w1 = np.asarray(inp["L0_ffn_w1"])
    w1a = np.zeros((2, 1024, HIDP), np.float32); w1a[:, :, :HID] = w1[:, :, :HID]
    w1g = np.zeros((2, 1024, HIDP), np.float32); w1g[:, :, :HID] = w1[:, :, HID:]
    W1dev = np.empty((2, NJ, 2, 8, 128, 128), np.float32)
    for l in range(2):
        for jj in range(NJ):
            for k in range(8):
                W1dev[l, jj, 0, k] = w1a[l, k * 128:(k + 1) * 128, jj * 128:(jj + 1) * 128]
                W1dev[l, jj, 1, k] = w1g[l, k * 128:(k + 1) * 128, jj * 128:(jj + 1) * 128]
    d["fw1"] = W1dev
    b1 = np.asarray(inp["L0_ffn_b1"])
    b1a = np.zeros((2, HIDP), np.float32); b1a[:, :HID] = b1[:, :HID]
    b1g = np.zeros((2, HIDP), np.float32); b1g[:, :HID] = b1[:, HID:]
    d["fb1a"] = np.ascontiguousarray(b1a.reshape(2, NJ, 128).transpose(0, 2, 1))
    d["fb1g"] = np.ascontiguousarray(b1g.reshape(2, NJ, 128).transpose(0, 2, 1))
    w2 = np.asarray(inp["L0_ffn_w2"])
    w2p = np.zeros((2, HIDP, 1024), np.float32); w2p[:, :HID] = w2
    W2dev = np.empty((2, 8, NJ, 128, 128), np.float32)
    for l in range(2):
        for m in range(8):
            for jj in range(NJ):
                W2dev[l, m, jj] = w2p[l, jj * 128:(jj + 1) * 128, m * 128:(m + 1) * 128]
    d["fw2"] = W2dev
    d["fb2"] = np.ascontiguousarray(
        np.asarray(inp["L0_ffn_b2"]).reshape(2, 8, 128).transpose(0, 2, 1))
    d["flng"] = np.ascontiguousarray(np.broadcast_to(
        np.asarray(inp["L0_ffn_ln_g"])[:, None, :], (2, 128, 1024)))
    d["flnb"] = np.ascontiguousarray(np.broadcast_to(
        np.asarray(inp["L0_ffn_ln_b"])[:, None, :], (2, 128, 1024)))
    d["fqs"] = np.ascontiguousarray(np.broadcast_to(
        np.asarray(inp["L0_ffn_q_scale"])[:, None, :], (2, 128, 64)))
    d["fks"] = np.ascontiguousarray(np.broadcast_to(
        np.asarray(inp["L0_ffn_k_scale"])[:, None, :], (2, 128, 64)))

    acos, asin, csq, ssq, csk, ssk = host_tables()
    d["acos"], d["asin"] = acos, asin
    d["csq"], d["ssq"], d["csk"], d["ssk"] = csq, ssq, csk, ssk
    maskw, _ = host_masks()
    d["maskw"] = maskw
    d["ident"] = np.eye(128, dtype=np.float32)
    d["onesr"] = np.ones((1, 128), np.float32)
    d["onesc"] = np.ones((128, 1), np.float32)
    cvals = np.zeros(8, np.float64)
    cvals[0] = 1e-5
    cvals[1] = 1e-12
    cvals[2] = 0.8 - 0.6 * np.exp(-0.3 * 0)
    cvals[3] = 1.0 / 1024.0
    d["consts"] = np.broadcast_to(cvals.astype(np.float32), (128, 8)).copy()
    return d


class Sems:
    def __init__(self, nc, es, names):
        self.h = {}
        self.c = {}
        for n_ in names:
            self.h[n_] = es.enter_context(nc.semaphore(n_))
            self.c[n_] = 0

    def inc(self, ins, name, n=1):
        self.c[name] += n
        ins.then_inc(self.h[name], n)
        return self.c[name]

    def wait(self, eng, name, val=None):
        v = self.c[name] if val is None else val
        if v > 0:
            eng.wait_ge(self.h[name], v)
        return v


SEM_NAMES = ["WL", "SP", "PE", "ACT", "DVE", "GP"]


def build():
    nc = bass.Bass("TRN2", target_bir_lowering=False)
    t0 = time.time()

    din = {}
    def dri(name, shape, dtype=F32):
        din[name] = nc.dram_tensor(name, list(shape), dtype, kind="ExternalInput").ap()
        return din[name]

    dri("xT", (C, N))
    wqkv = dri("wqkv", (3, 2, 8, 128, 1024))
    dri("wproj", (NH, 128, 1024))
    for nm in ["lamq1", "lamq2", "lamk1", "lamk2"]:
        dri(nm, (1, 64))
    dri("acos", (N, 64)); dri("asin", (N, 64))
    dri("csq", (N, 32)); dri("ssq", (N, 32)); dri("csk", (N, 32)); dri("ssk", (N, 32))
    dri("maskw", (4, 128, 1024), BF16)
    dri("ident", (128, 128)); dri("onesr", (1, 128)); dri("onesc", (128, 1))
    dri("consts", (128, 8))
    dri("fwqkv", (2, C, 1536)); dri("fwo", (2, 512, C))
    dri("fw1", (2, NJ, 2, 8, 128, 128)); dri("fb1a", (2, 128, NJ)); dri("fb1g", (2, 128, NJ))
    dri("fw2", (2, 8, NJ, 128, 128)); dri("fb2", (2, 128, 8))
    dri("flng", (2, 128, C)); dri("flnb", (2, 128, C))
    dri("fqs", (2, 128, 64)); dri("fks", (2, 128, 64))

    out = nc.dram_tensor("out", [N, C], F32, kind="ExternalOutput").ap()

    qT_sp = nc.dram_tensor("qT_sp", [NH, 128, N], F32, kind="Internal").ap()
    kT_sp = nc.dram_tensor("kT_sp", [NH, 128, N], F32, kind="Internal").ap()
    v_sp = nc.dram_tensor("v_sp", [NH, N, 128], F32, kind="Internal").ap()
    xin_sp = nc.dram_tensor("xin_sp", [C, N], F32, kind="Internal").ap()

    es = ExitStack()
    sb = lambda n_, shp, dt_=F32: es.enter_context(nc.sbuf_tensor(n_, shp, dt_))
    S = Sems(nc, es, SEM_NAMES)

    xs = sb("xs", [128, NC * N])
    ident = sb("identt", [128, 128])
    onesr = sb("onesrt", [1, 128])
    onesc = sb("onesct", [128, 1])
    consts = sb("constst", [128, 8])
    lam_b = sb("lam_b", [128, 1])

    with es, nc.Block() as block:
        lam_in = {nm: sb("t_" + nm, [1, 64]) for nm in
                  ["lamq1", "lamk1", "lamq2", "lamk2"]}
        tmp1 = sb("tmp1", [1, 64])
        tmp2 = sb("tmp2", [1, 4])
        ph_s = es.enter_context(nc.psum_tensor("ph_s", [128, 512], F32))

        def ph0(sync):
            ins = sync.dma_start(out=ident[:], in_=din["ident"][:, :]); S.inc(ins, "WL", 16)
            ins = sync.dma_start(out=onesr[:], in_=din["onesr"][:, :]); S.inc(ins, "WL", 16)
            ins = sync.dma_start(out=onesc[:], in_=din["onesc"][:, :]); S.inc(ins, "WL", 16)
            ins = sync.dma_start(out=consts[:], in_=din["consts"][:, :]); S.inc(ins, "WL", 16)
            ins = sync.dma_start(
                out=xs[:].rearrange("p (c n) -> p c n", n=N),
                in_=din["xT"][:, :].rearrange("(c p) n -> p c n", p=128))
            S.inc(ins, "WL", 16)
            for nm in ["lamq1", "lamk1", "lamq2", "lamk2"]:
                ins = sync.dma_start(out=lam_in[nm][:, :], in_=din[nm][:, :])
                S.inc(ins, "WL", 16)
        block.sync(ph0)
        wl_pro = S.c["WL"]

        def lam_dve(vector):
            vector.wait_ge(S.h["WL"], wl_pro)
            vector.tensor_tensor(out=tmp1[:, :], in0=lam_in["lamq1"][:, :],
                                 in1=lam_in["lamk1"][:, :], op=ALU.mult)
            vector.tensor_reduce(out=tmp2[:, 0:1], in_=tmp1[:, :], axis=AX.X, op=ALU.add)
            vector.tensor_tensor(out=tmp1[:, :], in0=lam_in["lamq2"][:, :],
                                 in1=lam_in["lamk2"][:, :], op=ALU.mult)
            ins = vector.tensor_reduce(out=tmp2[:, 1:2], in_=tmp1[:, :], axis=AX.X, op=ALU.add)
            S.inc(ins, "DVE")
        block.vector(lam_dve)

        def lam_act(scalar):
            S.wait(scalar, "DVE")
            scalar.activation(out=tmp2[:, 2:3], in_=tmp2[:, 0:1], func=AF.Exp)
            ins = scalar.activation(out=tmp2[:, 3:4], in_=tmp2[:, 1:2], func=AF.Exp)
            S.inc(ins, "ACT")
        block.scalar(lam_act)

        def lam_dve2(vector):
            S.wait(vector, "ACT")
            ins = vector.tensor_tensor(out=tmp2[:, 0:1], in0=tmp2[:, 2:3], in1=tmp2[:, 3:4],
                                       op=ALU.subtract)
            v = S.inc(ins, "DVE")
            vector.wait_ge(S.h["DVE"], v)
            ins = vector.tensor_tensor(out=tmp2[:, 0:1], in0=tmp2[:, 0:1],
                                       in1=consts[0:1, 2:3], op=ALU.add)
            S.inc(ins, "DVE")
        block.vector(lam_dve2)

        def lam_pe(tensor):
            S.wait(tensor, "DVE")
            ins = tensor.matmul(ph_s[:, 0:1], onesr[:, :], tmp2[0:1, 0:1],
                                start=True, stop=True)
            S.inc(ins, "PE")
        block.tensor(lam_pe)

        def lam_act2(scalar):
            S.wait(scalar, "PE")
            ins = scalar.copy(lam_b[:], ph_s[:, 0:1])
            S.inc(ins, "ACT")
        block.scalar(lam_act2)

        esA = ExitStack()
        sbA = lambda n_, shp, dt_=F32: esA.enter_context(nc.sbuf_tensor(n_, shp, dt_))
        tacos = sbA("tacos", [128, NT * 64])
        tasin = sbA("tasin", [128, NT * 64])
        wstrip = [sbA(f"wstrip{i}", [128, 8 * 1024]) for i in range(2)]
        stg = sbA("stg", [128, 2048])
        rst = sbA("rst", [128, 2048])
        rtm = sbA("rtm", [128, 1024])
        spT = sbA("spT", [128, 512])
        ph_a = es.enter_context(nc.psum_tensor("ph_a", [128, 2048], F32))
        ph_t = es.enter_context(nc.psum_tensor("ph_t", [128, 512], F32))

        def phA_tables(sync):
            for dst, src in [(tacos, din["acos"]), (tasin, din["asin"])]:
                ins = sync.dma_start(
                    out=dst[:].rearrange("p (t e) -> p t e", e=64),
                    in_=src[:, :].rearrange("(t p) e -> p t e", p=128))
                S.inc(ins, "WL", 16)
        block.sync(phA_tables)

        strip_wl = {}
        strip_pe_done = {}
        for i0 in range(2):
            def load_strip(sync, i=i0):
                p, hf = divmod(i, 2)
                ins = sync.dma_start(
                    out=wstrip[i % 2][:].rearrange("p (k x) -> p k x", x=1024),
                    in_=wqkv[p, hf].transpose([1, 0, 2]))
                strip_wl[i] = S.inc(ins, "WL", 16)
            block.sync(load_strip)

        last_evac = {"q0": 0, "q1": 0}
        last_rot = 0
        last_stg_user = 0
        last_tevac = 0
        last_spill = 0
        last_vspill = 0

        for pair in range(3):
            for nt in range(NT):
                def mm_step(tensor, pair=pair, nt=nt,
                            w0=last_evac["q0"], w1=last_evac["q1"],
                            lrot=last_rot):
                    i = pair * 2
                    tensor.wait_ge(S.h["WL"], max(strip_wl[i], strip_wl[i + 1]))
                    if w0:
                        tensor.wait_ge(S.h["ACT"], max(w0, w1))
                    for hf in range(2):
                        buf = wstrip[(pair * 2 + hf) % 2]
                        for cc in range(8):
                            for q5 in range(2):
                                ins = tensor.matmul(
                                    ph_a[:, hf * 1024 + q5 * 512: hf * 1024 + (q5 + 1) * 512],
                                    xs[:, cc * N + nt * 128: cc * N + (nt + 1) * 128],
                                    buf[:, cc * 1024 + q5 * 512: cc * 1024 + (q5 + 1) * 512],
                                    start=(cc == 0), stop=(cc == 7))
                    S.inc(ins, "PE")
                block.tensor(mm_step)
                mm_pe = S.c["PE"]
                if nt == NT - 1:
                    strip_pe_done[pair * 2] = mm_pe
                    strip_pe_done[pair * 2 + 1] = mm_pe
                    if pair < 2:
                        for hf2 in range(2):
                            def load_next(sync, pair=pair, hf2=hf2, gate=mm_pe):
                                i = (pair + 1) * 2 + hf2
                                sync.wait_ge(S.h["PE"], strip_pe_done[i - 2])
                                p2, h2 = divmod(i, 2)
                                ins = sync.dma_start(
                                    out=wstrip[i % 2][:].rearrange("p (k x) -> p k x", x=1024),
                                    in_=wqkv[p2, h2].transpose([1, 0, 2]))
                                strip_wl[i] = S.inc(ins, "WL", 16)
                            block.sync(load_next)

                def evac_step(scalar, pair=pair, nt=nt, gate=mm_pe,
                              lrot=last_rot, lvs=last_vspill):
                    scalar.wait_ge(S.h["PE"], gate)
                    if pair < 2:
                        if lrot:
                            scalar.wait_ge(S.h["DVE"], lrot)
                        dst = stg
                    else:
                        if lvs:
                            scalar.wait_ge(S.h["SP"], lvs)
                        dst = rst
                    scalar.activation(out=dst[:, 0:1024], in_=ph_a[:, 0:1024], func=AF.Copy)
                    ins = scalar.activation(out=dst[:, 1024:2048], in_=ph_a[:, 1024:2048],
                                            func=AF.Copy)
                    S.inc(ins, "ACT")
                block.scalar(evac_step)
                ev_act = S.c["ACT"]
                last_evac["q0"] = ev_act
                last_evac["q1"] = ev_act

                if pair < 2:
                    def rot_step(vector, nt=nt, gate=ev_act, ltev=last_tevac):
                        vector.wait_ge(S.h["ACT"], gate)
                        if ltev:
                            pass
                        cb = tacos[:, nt * 64:(nt + 1) * 64].unsqueeze(1).broadcast_to([128, 16, 64])
                        sbb = tasin[:, nt * 64:(nt + 1) * 64].unsqueeze(1).broadcast_to([128, 16, 64])
                        qa = stg[:, 0:1024].rearrange("p (h e) -> p h e", e=64)
                        qb = stg[:, 1024:2048].rearrange("p (h e) -> p h e", e=64)
                        r3 = rst[:].rearrange("p (h d) -> p h d", d=128)
                        t3 = rtm[:].rearrange("p (h e) -> p h e", e=64)
                        vector.tensor_tensor(out=t3, in0=qa, in1=cb, op=ALU.mult)
                        vector.tensor_tensor(out=r3[:, :, 0:64], in0=qb, in1=sbb, op=ALU.mult)
                        vector.tensor_tensor(out=r3[:, :, 0:64], in0=t3, in1=r3[:, :, 0:64],
                                             op=ALU.subtract)
                        vector.tensor_tensor(out=t3, in0=qb, in1=cb, op=ALU.mult)
                        vector.tensor_tensor(out=r3[:, :, 64:128], in0=qa, in1=sbb, op=ALU.mult)
                        ins = vector.tensor_tensor(out=r3[:, :, 64:128], in0=t3,
                                                   in1=r3[:, :, 64:128], op=ALU.add)
                        S.inc(ins, "DVE")
                    rot_pe_gate = S.c["PE"]

                    def rot_step2(vector, gate_pe=rot_pe_gate):
                        vector.wait_ge(S.h["PE"], gate_pe)
                    block.vector(rot_step2)
                    block.vector(rot_step)
                    last_rot = S.c["DVE"]

                    for g in range(4):
                        def t_step(tensor, g=g, rgate=last_rot, egate=last_tevac):
                            tensor.wait_ge(S.h["DVE"], rgate)
                            if egate:
                                tensor.wait_ge(S.h["ACT"], egate)
                            for j in range(4):
                                hh = g * 4 + j
                                ins = tensor.matmul(
                                    ph_t[:, j * 128:(j + 1) * 128],
                                    rst[:, hh * 128:(hh + 1) * 128],
                                    ident[:, :],
                                    start=(j == 0), stop=(j == 3), is_transpose=True)
                            S.inc(ins, "PE")
                        block.tensor(t_step)
                        t_pe = S.c["PE"]

                        def tevac_step(scalar, gate=t_pe, sgate=last_spill):
                            scalar.wait_ge(S.h["PE"], gate)
                            if sgate:
                                scalar.wait_ge(S.h["SP"], sgate)
                            ins = scalar.activation(out=spT[:], in_=ph_t[:], func=AF.Copy)
                            S.inc(ins, "ACT")
                        block.scalar(tevac_step)
                        last_tevac = S.c["ACT"]

                        def spill_step(gpsimd, pair=pair, nt=nt, g=g, gate=last_tevac):
                            gpsimd.wait_ge(S.h["ACT"], gate)
                            dst_all = qT_sp if pair == 0 else kT_sp
                            dst = dst_all[4 * g:4 * g + 4, :, nt * 128:(nt + 1) * 128]
                            ins = gpsimd.dma_start(
                                out=dst.transpose([1, 0, 2]),
                                in_=spT[:].rearrange("p (j n) -> p j n", n=128))
                            S.inc(ins, "SP", 16)
                        block.gpsimd(spill_step)
                        last_spill = S.c["SP"]
                else:
                    def vspill_step(gpsimd, nt=nt, gate=ev_act):
                        gpsimd.wait_ge(S.h["ACT"], gate)
                        dst = v_sp[:, nt * 128:(nt + 1) * 128, :]
                        ins = gpsimd.dma_start(
                            out=dst.transpose([1, 0, 2]),
                            in_=rst[:].rearrange("p (h d) -> p h d", d=128))
                        S.inc(ins, "SP", 16)
                    block.gpsimd(vspill_step)
                    last_vspill = S.c["SP"]


        phA_end = {k: S.c[k] for k in SEM_NAMES}
        esA.close()

        ph_pv = es.enter_context(nc.psum_tensor("ph_pv", [128, 512], F32))
        ph_pj = es.enter_context(nc.psum_tensor("ph_pj", [128, 512], F32))
        esB = ExitStack()
        sbB = lambda n_, shp, dt_=F32: esB.enter_context(nc.sbuf_tensor(n_, shp, dt_))
        qkb = [sbB(f"qkb{i}", [128, 2048]) for i in range(2)]
        kkb = [sbB(f"kkb{i}", [128, 2048]) for i in range(2)]
        vkb = [sbB(f"vkb{i}", [128, 2048]) for i in range(2)]
        wpb = [sbB("wpb0", [128, 1024])]
        wpb.append(wpb[0])
        P1 = [sbB(f"P1_{i}", [128, 2048]) for i in range(2)]
        P2 = [sbB(f"P2_{i}", [128, 2048]) for i in range(2)]
        Pc = P2
        PT = sbB("PT", [128, 8192])
        outT = sbB("outT", [128, 2048])
        sqb = sbB("sqb", [128, 512])
        negc = [sbB(f"negc{i}", [128, 1]) for i in range(2)]
        acc1r = [sbB(f"acc1_{i}", [128, 4]) for i in range(2)]
        acc2r = [sbB(f"acc2_{i}", [128, 4]) for i in range(2)]
        s1t = sbB("s1t_", [128, 1]); s2t = sbB("s2t_", [128, 1])
        inv1 = sbB("inv1", [128, 1]); inv2 = sbB("inv2", [128, 1])
        m2n = sbB("m2n", [128, 1])
        nrm = sbB("nrm", [1, 10])

        head_load_wl = {}
        wpb_wl = {}
        wpb_gate = {}
        wpb_pending = {}
        head_pe_end = {}
        exp_done = {}
        t_pe_done = {}
        comb_done = {}
        tevac_done = {}
        pv_mm_done = {}
        pv_evac_done = {}
        pjadd_done = {}
        negc_done = {}
        cchain = {}

        def emit_head_loads(h):
            def loads(sync, h=h):
                if h >= 2:
                    sync.wait_ge(S.h["PE"], head_pe_end[h - 2])
                else:
                    for k in ["SP", "PE", "ACT", "DVE"]:
                        if phA_end[k]:
                            sync.wait_ge(S.h[k], phA_end[k])
                b = h % 2
                ins = sync.dma_start(out=qkb[b][:], in_=qT_sp[h, :, :]); S.inc(ins, "WL", 16)
                ins = sync.dma_start(out=kkb[b][:], in_=kT_sp[h, :, :]); S.inc(ins, "WL", 16)
                ins = sync.dma_start(
                    out=vkb[b][:].rearrange("p (kc d) -> p kc d", d=128),
                    in_=v_sp[h].rearrange("(kc p) d -> p kc d", p=128))
                S.inc(ins, "WL", 16)
                head_load_wl[h] = S.c["WL"]
            block.sync(loads)

            def wpb_load(sync, h=h, gate=None):
                if h >= 1:
                    sync.wait_ge(S.h["PE"], wpb_gate[h])
                ins = sync.dma_start(
                    out=wpb[0][:].rearrange("p (cc x) -> p cc x", x=128),
                    in_=din["wproj"][h].rearrange("p (cc x) -> p cc x", x=128))
                wpb_wl[h] = S.inc(ins, "WL", 16)
            if h == 0:
                wpb_gate[0] = 0
                block.sync(wpb_load)
            else:
                wpb_pending[h] = wpb_load

        def cchain_steps(h):
            b = h % 2
            last_sq_pe = {"v": 0}

            for side in range(2):
                for g in range(4):
                    def nsq(vector, h=h, b=b, side=side, g=g):
                        if side == 0 and g == 0:
                            vector.wait_ge(S.h["WL"], head_load_wl[h])
                        S.wait(vector, "PE")
                        buf = qkb[b] if side == 0 else kkb[b]
                        ins = vector.tensor_tensor(out=sqb[:],
                                                   in0=buf[:, g * 512:(g + 1) * 512],
                                                   in1=buf[:, g * 512:(g + 1) * 512],
                                                   op=ALU.mult)
                        S.inc(ins, "DVE")
                    yield ("vector", nsq)
                    def nmm(tensor, g=g):
                        S.wait(tensor, "DVE")
                        ins = tensor.matmul(ph_s[0:1, :], onesc[:, :], sqb[:],
                                            start=True, stop=True)
                        S.inc(ins, "PE")
                    yield ("tensor", nmm)
                    def nred(vector, side=side, g=g):
                        S.wait(vector, "PE")
                        ins = vector.reduce_max(out=nrm[:, side * 4 + g:side * 4 + g + 1],
                                                in_=ph_s[0:1, :], axis=AX.X)
                        S.inc(ins, "DVE")
                    yield ("vector", nred)

            def fin(vector):
                vector.reduce_max(out=nrm[:, 8:9], in_=nrm[:, 0:4], axis=AX.X)
                ins = vector.reduce_max(out=nrm[:, 9:10], in_=nrm[:, 4:8], axis=AX.X)
                v = S.inc(ins, "DVE")
                vector.wait_ge(S.h["DVE"], v)
                ins = vector.tensor_tensor(out=nrm[:, 8:9], in0=nrm[:, 8:9],
                                           in1=nrm[:, 9:10], op=ALU.mult)
                S.inc(ins, "DVE")
            yield ("vector", fin)

            def fin2(scalar):
                S.wait(scalar, "DVE")
                ins = scalar.activation(out=nrm[:, 9:10], in_=nrm[:, 8:9], func=AF.Ln)
                v = S.inc(ins, "ACT")
                scalar.wait_ge(S.h["ACT"], v)
                ins = scalar.activation(out=nrm[:, 8:9], in_=nrm[:, 9:10], func=AF.Exp,
                                        scale=0.5)
                S.inc(ins, "ACT")
            yield ("scalar", fin2)

            def fin3(vector):
                S.wait(vector, "ACT")
                ins = vector.tensor_scalar(out=nrm[:, 8:9], in0=nrm[:, 8:9],
                                           scalar1=-0.125, scalar2=None, op0=ALU.mult)
                S.inc(ins, "DVE")
            yield ("vector", fin3)

            def bca(tensor):
                S.wait(tensor, "DVE")
                ins = tensor.matmul(ph_s[:, 0:1], onesr[:, :], nrm[0:1, 8:9],
                                    start=True, stop=True)
                S.inc(ins, "PE")
            yield ("tensor", bca)

            def bcc(vector, h=h):
                S.wait(vector, "PE")
                ins = vector.tensor_copy(out=negc[h % 2][:], in_=ph_s[:, 0:1])
                negc_done[h] = S.inc(ins, "DVE")
            yield ("vector", bcc)

        def run_steps(gen, nsteps):
            n = 0
            for kind, f in gen:
                getattr(block, kind)(f)
                n += 1
                if n >= nsteps:
                    break

        emit_head_loads(0)
        g0 = cchain_steps(0)
        run_steps(g0, 100)
        for h in range(NH_RUN):
            b = h % 2
            if h + 1 < NH_RUN:
                emit_head_loads(h + 1)
                gnext = cchain_steps(h + 1)
            else:
                gnext = None

            def emit_Tpv(qt, h=h, b=b):
                    qg, qr = divmod(qt, 4)
                    for g in range(4):
                        def tmm(tensor, h=h, qt=qt, g=g, qg=qg):
                            tensor.wait_ge(S.h["DVE"], comb_done[(h, qt)])
                            pk = (h, qt, g - 1) if g > 0 else \
                                 ((h, qt - 1, 3) if qt > 0 else
                                  ((h - 1, NT - 1, 3) if h > 0 else None))
                            if pk is not None and pk in tevac_done:
                                tensor.wait_ge(S.h["DVE"], tevac_done[pk])
                            pvk = (h, qg - 1) if qg > 0 else ((h - 1, 3) if h > 0 else None)
                            if pvk is not None and pvk in pv_mm_done and g == 0:
                                tensor.wait_ge(S.h["PE"], pv_mm_done[pvk])
                            pq = qt % 2
                            for j in range(4):
                                kc = g * 4 + j
                                ins = tensor.matmul(ph_t[:, j * 128:(j + 1) * 128],
                                                    Pc[pq][:, kc * 128:(kc + 1) * 128],
                                                    ident[:, :],
                                                    start=(j == 0), stop=(j == 3),
                                                    is_transpose=True)
                            S.inc(ins, "PE")
                        block.tensor(tmm)
                        t_pe = S.c["PE"]
                        t_pe_done[(h, qt, g)] = t_pe

                        def tev(vector, h=h, qt=qt, g=g, qg=qg, qr=qr, gate=t_pe):
                            vector.wait_ge(S.h["PE"], gate)
                            ins = vector.tensor_copy(
                                out=PT[:].rearrange("p (kc q) -> p kc q", q=512)
                                    [:, 4 * g:4 * g + 4, qr * 128:(qr + 1) * 128],
                                in_=ph_t[:].rearrange("p (j n) -> p j n", n=128))
                            tevac_done[(h, qt, g)] = S.inc(ins, "DVE")
                        block.vector(tev)

                    if qr == 3:
                        def pv(tensor, h=h, b=b, qg=qg):
                            for qq in range(4):
                                tensor.wait_ge(S.h["DVE"], tevac_done[(h, qg * 4 + qq, 3)])
                            pk = (h, qg - 1) if qg > 0 else ((h - 1, 3) if h > 0 else None)
                            if pk is not None and pk in pv_evac_done:
                                tensor.wait_ge(S.h["DVE"], pv_evac_done[pk])
                            if pk is not None and (pk[0], pk[1], 7) in pjadd_done:
                                tensor.wait_ge(S.h["DVE"], pjadd_done[(pk[0], pk[1], 7)])
                            for kc in range(16):
                                ins = tensor.matmul(ph_pv[:, :],
                                                    vkb[b][:, kc * 128:(kc + 1) * 128],
                                                    PT[:, kc * 512:(kc + 1) * 512],
                                                    start=(kc == 0), stop=(kc == 15))
                            pv_mm_done[(h, qg)] = S.inc(ins, "PE")
                        block.tensor(pv)
                        pv_pe = S.c["PE"]

                        def pvev(vector, h=h, qg=qg, gate=pv_pe):
                            vector.wait_ge(S.h["PE"], gate)
                            ins = vector.tensor_copy(out=outT[:, qg * 512:(qg + 1) * 512],
                                                     in_=ph_pv[:, :])
                            pv_evac_done[(h, qg)] = S.inc(ins, "DVE")
                        block.vector(pvev)

                        for cc in range(8):
                            def pj(tensor, h=h, b=b, qg=qg, cc=cc):
                                if cc == 0:
                                    tensor.wait_ge(S.h["DVE"], pv_evac_done[(h, qg)])
                                bank = ph_pj if cc % 2 == 0 else ph_pv
                                if cc >= 2:
                                    tensor.wait_ge(S.h["DVE"], pjadd_done[(h, qg, cc - 2)])
                                elif cc == 0:
                                    pk2 = (h, qg - 1) if qg > 0 else ((h - 1, 3) if h > 0 else None)
                                    if pk2 is not None and (pk2[0], pk2[1], 6) in pjadd_done:
                                        tensor.wait_ge(S.h["DVE"], pjadd_done[(pk2[0], pk2[1], 6)])
                                if cc == 0 and qg == 0:
                                    tensor.wait_ge(S.h["WL"], wpb_wl[h])
                                ins = tensor.matmul(bank[:, :],
                                                    wpb[0][:, cc * 128:(cc + 1) * 128],
                                                    outT[:, qg * 512:(qg + 1) * 512],
                                                    start=True, stop=True)
                                S.inc(ins, "PE")
                            block.tensor(pj)
                            pj_pe = S.c["PE"]

                            def pjadd(vector, h=h, qg=qg, cc=cc, gate=pj_pe):
                                vector.wait_ge(S.h["PE"], gate)
                                bank = ph_pj if cc % 2 == 0 else ph_pv
                                dst = xs[:, cc * N + qg * 512: cc * N + (qg + 1) * 512]
                                ins = vector.scalar_tensor_tensor(
                                    out=dst, in0=bank[:, :], scalar=1.0, in1=dst,
                                    op0=ALU.mult, op1=ALU.add)
                                pjadd_done[(h, qg, cc)] = S.inc(ins, "DVE")
                            block.vector(pjadd)

            for qt in range(NT):
                act_base = S.c["ACT"]
                exp_pred = {ks: act_base + ks + 1 for ks in range(4)}
                s_pe_ks = {}
                for ks in range(4):
                    bp = ks % 2
                    def smm(tensor, h=h, b=b, qt=qt, ks=ks, bp=bp, ep=exp_pred):
                        if qt == 0 and ks == 0:
                            tensor.wait_ge(S.h["WL"], head_load_wl[h])
                        key = (h, qt - 1, ks + 2) if qt > 0 else \
                              ((h - 1, NT - 1, ks + 2) if h > 0 else None)
                        if ks >= 2:
                            tensor.wait_ge(S.h["ACT"], ep[ks - 2])
                        elif key is not None and key in exp_done:
                            tensor.wait_ge(S.h["ACT"], exp_done[key])
                        lo = bp * 1024
                        tensor.matmul(ph_a[:, lo:lo + 512],
                                      qkb[b][0:64, qt * 128:(qt + 1) * 128],
                                      kkb[b][0:64, ks * 512:(ks + 1) * 512],
                                      start=True, stop=True)
                        ins = tensor.matmul(ph_a[:, lo + 512:lo + 1024],
                                            qkb[b][64:128, qt * 128:(qt + 1) * 128],
                                            kkb[b][64:128, ks * 512:(ks + 1) * 512],
                                            start=True, stop=True,
                                            tile_position=(64, 0))
                        S.inc(ins, "PE")
                    block.tensor(smm)
                    s_pe_ks[ks] = S.c["PE"]

                assert S.c["ACT"] == act_base, "no ACT emissions between smm and exp body"

                def sexp(scalar, h=h, qt=qt, gates=dict(s_pe_ks)):
                    pkey = (h, qt - 2) if qt >= 2 else ((h - 1, NT - 2 + qt) if h > 0 else None)
                    if pkey is not None and pkey in comb_done:
                        scalar.wait_ge(S.h["DVE"], comb_done[pkey])
                    if pkey is not None and (pkey[0], pkey[1], 3) in t_pe_done:
                        scalar.wait_ge(S.h["PE"], t_pe_done[(pkey[0], pkey[1], 3)])
                    if qt == 0 and h in negc_done:
                        scalar.wait_ge(S.h["DVE"], negc_done[h])
                    pq = qt % 2
                    for ks in range(4):
                        bp = ks % 2
                        lo = bp * 1024
                        scalar.wait_ge(S.h["PE"], gates[ks])
                        scalar.activation(out=P1[pq][:, ks * 512:(ks + 1) * 512],
                                          in_=ph_a[:, lo:lo + 512], func=AF.Exp,
                                          bias=negc[h % 2][:], scale=0.125,
                                          accum_out=acc1r[pq][:, ks:ks + 1])
                        ins = scalar.activation(out=P2[pq][:, ks * 512:(ks + 1) * 512],
                                                in_=ph_a[:, lo + 512:lo + 1024], func=AF.Exp,
                                                bias=negc[h % 2][:], scale=0.125,
                                                accum_out=acc2r[pq][:, ks:ks + 1])
                        exp_done[(h, qt, ks)] = S.inc(ins, "ACT")
                block.scalar(sexp)
                assert exp_done[(h, qt, 3)] == exp_pred[3]

                if gnext is not None and 1 <= qt <= 13:
                    run_steps(gnext, 2)

                def comb(vector, h=h, qt=qt):
                    pq = qt % 2
                    vector.wait_ge(S.h["ACT"], exp_done[(h, qt, 3)])
                    tk = (h, qt - 2, 3) if qt >= 2 else ((h - 1, NT - 2 + qt, 3) if h > 0 else None)
                    if tk is not None and tk in t_pe_done:
                        vector.wait_ge(S.h["PE"], t_pe_done[tk])
                    vector.tensor_reduce(out=s1t[:], in_=acc1r[pq][:], axis=AX.X, op=ALU.add)
                    ins = vector.tensor_reduce(out=s2t[:], in_=acc2r[pq][:], axis=AX.X, op=ALU.add)
                    v = S.inc(ins, "DVE")
                    vector.wait_ge(S.h["DVE"], v)
                    vector.reciprocal(out=inv1[:], in_=s1t[:])
                    ins = vector.reciprocal(out=inv2[:], in_=s2t[:])
                    v = S.inc(ins, "DVE")
                    vector.wait_ge(S.h["DVE"], v)
                    ins = vector.tensor_scalar(out=m2n[:], in0=inv2[:], scalar1=lam_b[:, 0:1],
                                               scalar2=-1.0, op0=ALU.mult, op1=ALU.mult)
                    v = S.inc(ins, "DVE")
                    vector.wait_ge(S.h["DVE"], v)
                    vector.tensor_scalar(out=P1[pq][:], in0=P1[pq][:], scalar1=inv1[:, 0:1],
                                         scalar2=None, op0=ALU.mult)
                    vector.tensor_scalar(out=P2[pq][:], in0=P2[pq][:], scalar1=m2n[:, 0:1],
                                         scalar2=None, op0=ALU.mult)
                    ins = vector.tensor_tensor(out=P2[pq][:], in0=P2[pq][:], in1=P1[pq][:],
                                               op=ALU.add)
                    comb_done[(h, qt)] = S.inc(ins, "DVE")
                block.vector(comb)

                if qt >= 1:
                    emit_Tpv(qt - 1)

            emit_Tpv(NT - 1)
            if gnext is not None:
                run_steps(gnext, 100)
            head_pe_end[h] = S.c["PE"]
            if (h + 1) in wpb_pending:
                wpb_gate[h + 1] = S.c["PE"]
                block.sync(wpb_pending.pop(h + 1))




        phB_end = {k: S.c[k] for k in SEM_NAMES}
        esB.close()
        ln_y = sb("ln_y", [128, 1024])
        ln_sq = sb("ln_sq", [128, 1024])
        ln_s = sb("ln_s", [128, 6])
        ln_state = {"prev_back_act": 0, "prev_apply_pe": 0, "first_gate": None}

        def emit_ln_tile(nt, gamma_beta=None, write_back=True, out_rows=None,
                         src_fn=None, back_dst_fn=None):
            def fwd(tensor, nt=nt):
                if ln_state["first_gate"] is not None:
                    g = ln_state["first_gate"]
                    ln_state["first_gate"] = None
                    tensor.wait_ge(S.h["DVE"], g[0])
                    tensor.wait_ge(S.h["ACT"], g[1])
                if ln_state["prev_back_act"]:
                    tensor.wait_ge(S.h["ACT"], ln_state["prev_back_act"])
                if ln_state.get("prev_fwd_evac", 0):
                    tensor.wait_ge(S.h["ACT"], ln_state["prev_fwd_evac"])
                for cc in range(8):
                    src = src_fn(cc) if src_fn is not None else \
                        xs[:, cc * N + nt * 128: cc * N + (nt + 1) * 128]
                    ins = tensor.matmul(ph_a[:, cc * 128:(cc + 1) * 128], src,
                                        ident[:, :],
                                        start=(cc % 4 == 0), stop=(cc % 4 == 3),
                                        is_transpose=True)
                S.inc(ins, "PE")
            block.tensor(fwd)
            fwd_pe = S.c["PE"]

            def evac(scalar, gate=fwd_pe, osp=ln_state.get("out_sp", 0)):
                scalar.wait_ge(S.h["PE"], gate)
                if osp:
                    scalar.wait_ge(S.h["SP"], osp)
                scalar.activation(out=ln_y[:], in_=ph_a[:, 0:1024], func=AF.Copy,
                                  accum_out=ln_s[:, 0:1])
                ins = scalar.activation(out=ln_sq[:], in_=ln_y[:], func=AF.Square,
                                        accum_out=ln_s[:, 1:2])
                S.inc(ins, "ACT")
            block.scalar(evac)
            ev_act = S.c["ACT"]
            ln_state["prev_fwd_evac"] = ev_act

            def stats(vector, gate=ev_act):
                vector.wait_ge(S.h["ACT"], gate)
                vector.tensor_scalar(out=ln_s[:, 2:3], in0=ln_s[:, 0:1],
                                     scalar1=1.0 / 1024.0, scalar2=None, op0=ALU.mult)
                ins = vector.tensor_scalar(out=ln_s[:, 3:4], in0=ln_s[:, 1:2],
                                           scalar1=1.0 / 1024.0, scalar2=None, op0=ALU.mult)
                v = S.inc(ins, "DVE")
                vector.wait_ge(S.h["DVE"], v)
                ins = vector.tensor_tensor(out=ln_s[:, 5:6], in0=ln_s[:, 2:3],
                                           in1=ln_s[:, 2:3], op=ALU.mult)
                v = S.inc(ins, "DVE")
                vector.wait_ge(S.h["DVE"], v)
                ins = vector.tensor_tensor(out=ln_s[:, 3:4], in0=ln_s[:, 3:4],
                                           in1=ln_s[:, 5:6], op=ALU.subtract)
                S.inc(ins, "DVE")
            block.vector(stats)
            st_dve = S.c["DVE"]

            def sqrt_step(scalar, gate=st_dve):
                scalar.wait_ge(S.h["DVE"], gate)
                ins = scalar.activation(out=ln_s[:, 3:4], in_=ln_s[:, 3:4], func=AF.Sqrt,
                                        bias=consts[:, 0:1])
                S.inc(ins, "ACT")
            block.scalar(sqrt_step)
            sq_act = S.c["ACT"]

            def rstep(vector, gate=sq_act):
                vector.wait_ge(S.h["ACT"], gate)
                ins = vector.reciprocal(out=ln_s[:, 4:5], in_=ln_s[:, 3:4])
                v = S.inc(ins, "DVE")
                vector.wait_ge(S.h["DVE"], v)
                ins = vector.tensor_scalar(out=ln_s[:, 5:6], in0=ln_s[:, 2:3],
                                           scalar1=ln_s[:, 4:5], scalar2=-1.0,
                                           op0=ALU.mult, op1=ALU.mult)
                S.inc(ins, "DVE")
            block.vector(rstep)
            r_dve = S.c["DVE"]

            def apply(scalar, gate=r_dve, pgate=ln_state["prev_apply_pe"]):
                scalar.wait_ge(S.h["DVE"], gate)
                if pgate:
                    scalar.wait_ge(S.h["PE"], pgate)
                ins = scalar.activation(out=ln_y[:], in_=ln_y[:], func=AF.Identity,
                                        scale=ln_s[:, 4:5], bias=ln_s[:, 5:6])
                S.inc(ins, "ACT")
            block.scalar(apply)
            if gamma_beta is not None:
                gmt, bt = gamma_beta
                def gb(vector):
                    S.wait(vector, "ACT")
                    vector.tensor_tensor(out=ln_y[:], in0=ln_y[:], in1=gmt, op=ALU.mult)
                    ins = vector.tensor_tensor(out=ln_y[:], in0=ln_y[:], in1=bt, op=ALU.add)
                    S.inc(ins, "DVE")
                block.vector(gb)
            ap_act, ap_dve = S.c["ACT"], S.c["DVE"]

            if write_back:
                def back(tensor, nt=nt, ga=ap_act, gd=ap_dve, gb_=gamma_beta):
                    tensor.wait_ge(S.h["ACT"], ga)
                    if gb_ is not None:
                        tensor.wait_ge(S.h["DVE"], gd)
                    for cc in range(8):
                        ins = tensor.matmul(ph_a[:, 1024 + (cc % 4) * 128:1024 + (cc % 4) * 128 + 128],
                                            ln_y[:, cc * 128:(cc + 1) * 128],
                                            ident[:, :],
                                            start=True, stop=True,
                                            is_transpose=True)
                        S.inc(ins, "PE")
                        g_pe = S.c["PE"]
                        def bevac(scalar, cc=cc, nt=nt, gate=g_pe):
                            scalar.wait_ge(S.h["PE"], gate)
                            ins = scalar.activation(
                                out=xs[:, cc * N + nt * 128: cc * N + (nt + 1) * 128],
                                in_=ph_a[:, 1024 + (cc % 4) * 128:1024 + (cc % 4) * 128 + 128],
                                func=AF.Copy)
                            S.inc(ins, "ACT")
                        return ins, bevac
                bevac_acts = {}
                for cc in range(8):
                    def bt_step(tensor, nt=nt, cc=cc, ga=ap_act, gd=ap_dve, gb_=gamma_beta):
                        if cc == 0:
                            tensor.wait_ge(S.h["ACT"], ga)
                            if gb_ is not None:
                                tensor.wait_ge(S.h["DVE"], gd)
                        if cc >= 4:
                            tensor.wait_ge(S.h["ACT"], bevac_acts[cc - 4])
                        ins = tensor.matmul(
                            ph_a[:, 1024 + (cc % 4) * 128:1024 + (cc % 4) * 128 + 128],
                            ln_y[:, cc * 128:(cc + 1) * 128], ident[:, :],
                            start=True, stop=True, is_transpose=True)
                        S.inc(ins, "PE")
                    block.tensor(bt_step)
                    g_pe = S.c["PE"]
                    def bevac(scalar, cc=cc, nt=nt, gate=g_pe, bdf=back_dst_fn):
                        scalar.wait_ge(S.h["PE"], gate)
                        dst = bdf(cc) if bdf is not None else \
                            xs[:, cc * N + nt * 128: cc * N + (nt + 1) * 128]
                        ins = scalar.activation(
                            out=dst,
                            in_=ph_a[:, 1024 + (cc % 4) * 128:1024 + (cc % 4) * 128 + 128],
                            func=AF.Copy)
                        S.inc(ins, "ACT")
                    block.scalar(bevac)
                    bevac_acts[cc] = S.c["ACT"]
                ln_state["prev_back_act"] = S.c["ACT"]
                ln_state["prev_apply_pe"] = S.c["PE"]
            if out_rows is not None:
                def odma(gpsimd, nt=nt, ga=ap_act, gd=ap_dve, gb_=gamma_beta):
                    gpsimd.wait_ge(S.h["ACT"], ga)
                    if gb_ is not None:
                        gpsimd.wait_ge(S.h["DVE"], gd)
                    ins = gpsimd.dma_start(out=out_rows, in_=ln_y[:])
                    S.inc(ins, "SP", 16)
                block.gpsimd(odma)
                ln_state["prev_apply_pe"] = 0
                ln_state["prev_back_act"] = 0
                ln_state["out_sp"] = S.c["SP"]

        ln_state["first_gate"] = (phB_end["DVE"], phB_end["ACT"])
        for nt in range(NT):
            emit_ln_tile(nt)

        def xin_dma(sync):
            S.wait(sync, "ACT")
            ins = sync.dma_start(
                out=xin_sp[:, :].rearrange("(c p) n -> p c n", p=128),
                in_=xs[:].rearrange("p (c n) -> p c n", n=N))
            S.inc(ins, "WL", 16)
        block.sync(xin_dma)





        def phase_barrier(tag):
            cur = {k: S.c[k] for k in SEM_NAMES}
            def bsync(sync):
                for k in ["PE", "ACT", "DVE", "SP", "GP"]:
                    if cur[k]:
                        sync.wait_ge(S.h[k], cur[k])
            block.sync(bsync)
            def bact(scalar):
                for k in ["PE", "DVE", "WL", "SP"]:
                    if cur[k]:
                        scalar.wait_ge(S.h[k], cur[k])
            block.scalar(bact)
            def bdve(vector):
                for k in ["PE", "ACT", "WL", "SP"]:
                    if cur[k]:
                        vector.wait_ge(S.h[k], cur[k])
            block.vector(bdve)
            def bpe(tensor):
                for k in ["ACT", "DVE", "WL", "SP"]:
                    if cur[k]:
                        tensor.wait_ge(S.h[k], cur[k])
            block.tensor(bpe)
            def bgp(gpsimd):
                for k in ["PE", "ACT", "DVE", "WL"]:
                    if cur[k]:
                        gpsimd.wait_ge(S.h[k], cur[k])
            block.gpsimd(bgp)
            return cur

        for layer in range(2):
            phase_barrier(f"ffa{layer}")
            esFa = ExitStack()
            sbF = lambda n_, shp, dt_=F32: esFa.enter_context(
                nc.sbuf_tensor(n_ + f"_l{layer}", shp, dt_))
            wstrips = [sbF(f"fws{i}", [128, 4096]) for i in range(2)]
            maskw_t = sbF("maskw_t", [128, 4096], BF16)
            tcsq = sbF("tcsq", [128, 512]); tssq = sbF("tssq", [128, 512])
            tcsk = sbF("tcsk", [128, 512]); tssk = sbF("tssk", [128, 512])
            fqs_t = sbF("fqs_t", [128, 64]); fks_t = sbF("fks_t", [128, 64])
            fwo_t = sbF("fwo_t", [128, 4096])
            qTr = sbF("qTr", [128, 2048])
            kTr = sbF("kTr", [128, 4096])
            vr = sbF("vr", [128, 4096])
            qtk = sbF("qtk", [128, 512])
            sql2 = sbF("sql2", [128, 512])
            rtmp = sbF("rtmp", [128, 512])
            ssq8 = sbF("ssq8", [128, 8]); invq = sbF("invq", [128, 8])
            Pf = sbF("Pf", [128, 1024])
            PTf = sbF("PTf", [128, 1024])
            oTf = sbF("oTf", [128, 2048])
            fmx = sbF("fmx", [128, 1]); frb = sbF("frb", [128, 1])
            fsum = sbF("fsum", [128, 1]); finv = sbF("finv", [128, 1])
            h0t = sbF("h0t", [128, 512])
            stash = sbF("stash", [128, 16])

            def ffa_loads(sync, layer=layer):
                ins = sync.dma_start(
                    out=maskw_t[:].rearrange("p (q x) -> p q x", x=1024),
                    in_=din["maskw"][:, :, :].transpose([1, 0, 2]))
                S.inc(ins, "WL", 16)
                for dst, nm in [(tcsq, "csq"), (tssq, "ssq"), (tcsk, "csk"), (tssk, "ssk")]:
                    ins = sync.dma_start(
                        out=dst[:].rearrange("p (t e) -> p t e", e=32),
                        in_=din[nm][:, :].rearrange("(t p) e -> p t e", p=128))
                    S.inc(ins, "WL", 16)
                ins = sync.dma_start(out=fqs_t[:], in_=din["fqs"][layer]); S.inc(ins, "WL", 16)
                ins = sync.dma_start(out=fks_t[:], in_=din["fks"][layer]); S.inc(ins, "WL", 16)
                ins = sync.dma_start(
                    out=fwo_t[:].rearrange("p (pr c) -> p pr c", c=1024),
                    in_=din["fwo"][layer].rearrange("(pr p) c -> p pr c", p=128))
                S.inc(ins, "WL", 16)
                ins = sync.dma_start(
                    out=wstrips[0][:].rearrange("p (cc x) -> p cc x", x=512),
                    in_=din["fwqkv"][layer][:, 0:512].rearrange("(cc p) x -> p cc x", p=128))
                S.inc(ins, "WL", 16)
            block.sync(ffa_loads)
            strip_wl_f = {0: S.c["WL"]}

            def ffa_prep(vector, layer=layer):
                S.wait(vector, "WL")
                vector.memset(h0t[:], 0.0)
                for i, cc in enumerate(range(4, 8)):
                    vector.tensor_copy(out=h0t[:, i * 128 + 1:(i + 1) * 128],
                                       in_=xs[:, cc * N: cc * N + 127])
                for w_ in range(1, 4):
                    for i, cc in enumerate(range(4, 8)):
                        ins = vector.tensor_copy(
                            out=stash[:, (w_ - 1) * 4 + i:(w_ - 1) * 4 + i + 1],
                            in_=xs[:, cc * N + w_ * 512 - 1: cc * N + w_ * 512])
                S.inc(ins, "DVE")
            block.vector(ffa_prep)
            prep_dve = S.c["DVE"]

            def shifted_lhs(cc, gt):
                if cc < 4:
                    return xs[:, cc * N + gt * 128: cc * N + (gt + 1) * 128]
                if gt == 0:
                    return h0t[:, (cc - 4) * 128:(cc - 4) * 128 + 128]
                return xs[:, cc * N + gt * 128 - 1: cc * N + (gt + 1) * 128 - 1]

            wtile = sbF("wtile", [128, 512])

            gate = {"strip_pe": {}, "exp_act": {}, "tev_act": 0, "pvv_act": 0,
                    "qkv_evac": 0, "wo_dve": 0, "sc_exp": {}, "exp_last": {0: 0, 1: 0},
                    "nrm_last": 0, "pf_pe": 0}

            for w_ in range(4):
                wpar = w_ % 2
                wprev = (w_ - 1) % 2
                if w_ >= 1:
                    def mk_wtile(vector, w_=w_, gd=prep_dve):
                        S.wait(vector, "PE")
                        for i, cc in enumerate(range(4, 8)):
                            vector.tensor_copy(
                                out=wtile[:, i * 128:i * 128 + 1],
                                in_=stash[:, (w_ - 1) * 4 + i:(w_ - 1) * 4 + i + 1])
                            ins = vector.tensor_copy(
                                out=wtile[:, i * 128 + 1:(i + 1) * 128],
                                in_=xs[:, cc * N + w_ * 512: cc * N + w_ * 512 + 127])
                        S.inc(ins, "DVE")
                    block.vector(mk_wtile)
                mk_dve = S.c["DVE"]

                def shifted_lhs_w(cc, nt, w_=w_):
                    gt = 4 * w_ + nt
                    if cc >= 4 and nt == 0:
                        if w_ == 0:
                            return h0t[:, (cc - 4) * 128:(cc - 4) * 128 + 128]
                        return wtile[:, (cc - 4) * 128:(cc - 4) * 128 + 128]
                    return shifted_lhs(cc, gt)

                for sg in range(3):
                    pidx = w_ * 3 + sg
                    buf = wstrips[pidx % 2]
                    if pidx + 1 < 12:
                        def pf(sync, pidx=pidx, layer=layer):
                            nsg = (pidx + 1) % 3
                            if pidx - 1 >= 0 and (pidx - 1) in gate["strip_pe"]:
                                sync.wait_ge(S.h["PE"], gate["strip_pe"][pidx - 1])
                            ins = sync.dma_start(
                                out=wstrips[(pidx + 1) % 2][:].rearrange("p (cc x) -> p cc x", x=512),
                                in_=din["fwqkv"][layer][:, nsg * 512:(nsg + 1) * 512]
                                    .rearrange("(cc p) x -> p cc x", p=128))
                            strip_wl_f[pidx + 1] = S.inc(ins, "WL", 16)
                        block.sync(pf)

                    for nt in range(4):
                        gt = 4 * w_ + nt
                        def qkvmm(tensor, sg=sg, nt=nt, w_=w_, buf=buf, pidx=pidx,
                                  gmk=mk_dve):
                            tensor.wait_ge(S.h["WL"], strip_wl_f[pidx])
                            tensor.wait_ge(S.h["DVE"], gmk)
                            if gate["qkv_evac"]:
                                tensor.wait_ge(S.h["DVE"], gate["qkv_evac"])
                            if gate["wo_dve"]:
                                tensor.wait_ge(S.h["DVE"], gate["wo_dve"])
                            for cc in range(8):
                                ins = tensor.matmul(
                                    ph_pv[:, :], shifted_lhs_w(cc, nt),
                                    buf[:, cc * 512:(cc + 1) * 512],
                                    start=(cc == 0), stop=(cc == 7))
                            S.inc(ins, "PE")
                        block.tensor(qkvmm)
                        if nt == 3:
                            gate["strip_pe"][pidx] = S.c["PE"]
                        mm_pe = S.c["PE"]

                        if sg == 2:
                            def vev(vector, nt=nt, wpar=wpar, gate_pe=mm_pe):
                                vector.wait_ge(S.h["PE"], gate_pe)
                                ins = vector.tensor_copy(
                                    out=vr[:, wpar * 2048 + nt * 512: wpar * 2048 + (nt + 1) * 512],
                                    in_=ph_pv[:, :])
                                gate["qkv_evac"] = S.inc(ins, "DVE")
                            block.vector(vev)
                            continue

                        def qev(vector, gate_pe=mm_pe):
                            vector.wait_ge(S.h["PE"], gate_pe)
                            ins = vector.tensor_copy(out=qtk[:], in_=ph_pv[:, :])
                            gate["qkv_evac"] = S.inc(ins, "DVE")
                            ins = vector.tensor_tensor(out=sql2[:], in0=qtk[:], in1=qtk[:],
                                                       op=ALU.mult)
                            S.inc(ins, "DVE")
                        block.vector(qev)
                        qe_act = S.c["DVE"]

                        def l2x(vector, sg=sg, gt=gt, gate_a=qe_act):
                            vector.wait_ge(S.h["ACT"], gate_a)
                            ins = vector.tensor_reduce(
                                out=ssq8[:], in_=sql2[:].rearrange("p (h e) -> p h e", e=64),
                                axis=AX.X, op=ALU.add)
                            v = S.inc(ins, "DVE")
                            vector.wait_ge(S.h["DVE"], v)
                            S.inc(ins, "DVE", 0)
                            return
                        def l2a(vector, sg=sg, gt=gt):
                            ins = vector.tensor_reduce(
                                out=ssq8[:], in_=sql2[:].rearrange("p (h e) -> p h e", e=64),
                                axis=AX.X, op=ALU.add)
                            v = S.inc(ins, "DVE")
                            vector.wait_ge(S.h["DVE"], v)
                            ins = vector.tensor_scalar(out=ssq8[:], in0=ssq8[:],
                                                       scalar1=1e-24, scalar2=None,
                                                       op0=ALU.max)
                            S.inc(ins, "DVE")
                        block.vector(l2a)
                        l2_dve = S.c["DVE"]

                        def l2b(scalar, gate_d=l2_dve):
                            scalar.wait_ge(S.h["DVE"], gate_d)
                            ins = scalar.activation(out=invq[:], in_=ssq8[:], func=AF.Ln)
                            v = S.inc(ins, "ACT")
                            scalar.wait_ge(S.h["ACT"], v)
                            ins = scalar.activation(out=invq[:], in_=invq[:], func=AF.Exp,
                                                    scale=-0.5)
                            S.inc(ins, "ACT")
                        block.scalar(l2b)
                        l2b_act = S.c["ACT"]

                        def l2c(vector, sg=sg, gt=gt, gate_a=l2b_act):
                            vector.wait_ge(S.h["ACT"], gate_a)
                            vector.tensor_tensor(
                                out=qtk[:].rearrange("p (h e) -> p h e", e=64),
                                in0=qtk[:].rearrange("p (h e) -> p h e", e=64),
                                in1=invq[:].unsqueeze(2).broadcast_to([128, 8, 64]),
                                op=ALU.mult)
                            sc = fqs_t if sg == 0 else fks_t
                            vector.tensor_tensor(
                                out=qtk[:].rearrange("p (h e) -> p h e", e=64),
                                in0=qtk[:].rearrange("p (h e) -> p h e", e=64),
                                in1=sc[:].unsqueeze(1).broadcast_to([128, 8, 64]),
                                op=ALU.mult)
                            cs = (tcsq if sg == 0 else tcsk)[:, gt * 32:(gt + 1) * 32]
                            ss = (tssq if sg == 0 else tssk)[:, gt * 32:(gt + 1) * 32]
                            csb = cs.unsqueeze(1).broadcast_to([128, 8, 32])
                            ssb = ss.unsqueeze(1).broadcast_to([128, 8, 32])
                            q3 = qtk[:].rearrange("p (h e) -> p h e", e=64)
                            r3 = rtmp[:].rearrange("p (h e) -> p h e", e=64)
                            vector.tensor_tensor(out=r3[:, :, 0:32], in0=q3[:, :, 0:32],
                                                 in1=csb, op=ALU.mult)
                            vector.tensor_tensor(out=sql2[:].rearrange("p (h e) -> p h e", e=64)[:, :, 0:32],
                                                 in0=q3[:, :, 32:64], in1=ssb, op=ALU.mult)
                            vector.tensor_tensor(out=r3[:, :, 32:64], in0=q3[:, :, 32:64],
                                                 in1=csb, op=ALU.mult)
                            vector.tensor_tensor(out=sql2[:].rearrange("p (h e) -> p h e", e=64)[:, :, 32:64],
                                                 in0=q3[:, :, 0:32], in1=ssb, op=ALU.mult)
                            s3 = sql2[:].rearrange("p (h e) -> p h e", e=64)
                            vector.tensor_tensor(out=r3[:, :, 0:32], in0=r3[:, :, 0:32],
                                                 in1=s3[:, :, 0:32], op=ALU.subtract)
                            ins = vector.tensor_tensor(out=r3[:, :, 32:64], in0=r3[:, :, 32:64],
                                                       in1=s3[:, :, 32:64], op=ALU.add)
                            S.inc(ins, "DVE")
                        block.vector(l2c)
                        rot_dve = S.c["DVE"]

                        def qkT(tensor, gate_d=rot_dve):
                            tensor.wait_ge(S.h["DVE"], gate_d)
                            if gate["tev_act"]:
                                tensor.wait_ge(S.h["DVE"], gate["tev_act"])
                            for pr in range(4):
                                ins = tensor.matmul(ph_t[:, pr * 128:(pr + 1) * 128],
                                                    rtmp[:, pr * 128:(pr + 1) * 128],
                                                    ident[:, :], start=(pr == 0),
                                                    stop=(pr == 3), is_transpose=True)
                            S.inc(ins, "PE")
                        block.tensor(qkT)
                        t_pe = S.c["PE"]

                        def qkTev(vector, sg=sg, nt=nt, wpar=wpar, gate_pe=t_pe):
                            vector.wait_ge(S.h["PE"], gate_pe)
                            if sg == 0:
                                dst = qTr[:].rearrange("p (pr x) -> p pr x", x=512)[
                                    :, :, nt * 128:(nt + 1) * 128]
                            else:
                                dst = kTr[:].rearrange("p (pr x) -> p pr x", x=1024)[
                                    :, :, wpar * 512 + nt * 128: wpar * 512 + (nt + 1) * 128]
                            ins = vector.tensor_copy(
                                out=dst, in_=ph_t[:].rearrange("p (pr n) -> p pr n", n=128))
                            gate["tev_act"] = S.inc(ins, "DVE")
                        block.vector(qkTev)

                nks = 1 if w_ == 0 else 2
                for pr in range(4):
                    pv_started = False
                    for qt in range(4):
                        for e_ in range(2):
                            def sc_mm(tensor, pr=pr, qt=qt, e_=e_, w_=w_, nks=nks,
                                      gexp=None):
                                if gate["exp_last"][e_]:
                                    tensor.wait_ge(S.h["ACT"], gate["exp_last"][e_])
                                lo = e_ * 1024
                                for ks in range(nks):
                                    wp = (w_ - 1 + ks) % 2 if w_ >= 1 else w_ % 2
                                    ins = tensor.matmul(
                                        ph_a[:, lo + ks * 512: lo + (ks + 1) * 512],
                                        qTr[0 + e_ * 64: 64 + e_ * 64,
                                            pr * 512 + qt * 128: pr * 512 + (qt + 1) * 128],
                                        kTr[0 + e_ * 64: 64 + e_ * 64,
                                            pr * 1024 + wp * 512: pr * 1024 + (wp + 1) * 512],
                                        start=True, stop=True,
                                        tile_position=(e_ * 64, 0))
                                S.inc(ins, "PE")
                            block.tensor(sc_mm)
                            sc_pe = S.c["PE"]

                            def sc_sm(vector, pr=pr, qt=qt, e_=e_, w_=w_, nks=nks,
                                      gate_pe=sc_pe):
                                vector.wait_ge(S.h["PE"], gate_pe)
                                lo = e_ * 1024
                                nk = nks * 512
                                if w_ == 0:
                                    mk = maskw_t[:, qt * 1024 + 512: qt * 1024 + 1024]
                                else:
                                    mk = maskw_t[:, qt * 1024: qt * 1024 + nk]
                                vector.tensor_tensor(out=ph_a[:, lo:lo + nk],
                                                     in0=ph_a[:, lo:lo + nk],
                                                     in1=mk, op=ALU.add)
                                ins = vector.reduce_max(out=fmx[:], in_=ph_a[:, lo:lo + nk],
                                                        axis=AX.X)
                                v = S.inc(ins, "DVE")
                                vector.wait_ge(S.h["DVE"], v)
                                ins = vector.tensor_scalar(out=frb[:], in0=fmx[:],
                                                           scalar1=-8.0, scalar2=None,
                                                           op0=ALU.mult)
                                S.inc(ins, "DVE")
                            block.vector(sc_sm)
                            sm_dve = S.c["DVE"]

                            def sc_exp(scalar, pr=pr, qt=qt, e_=e_, nks=nks, gate_d=sm_dve,
                                       gnrm=None, gpf=None):
                                scalar.wait_ge(S.h["DVE"], gate_d)
                                if gate["nrm_last"]:
                                    scalar.wait_ge(S.h["DVE"], gate["nrm_last"])
                                if gate["pf_pe"]:
                                    scalar.wait_ge(S.h["PE"], gate["pf_pe"])
                                lo = e_ * 1024
                                nk = nks * 512
                                ins = scalar.activation(out=Pf[:, 0:nk], in_=ph_a[:, lo:lo + nk],
                                                        func=AF.Exp, bias=frb[:], scale=8.0,
                                                        accum_out=fsum[:])
                                gate["exp_last"][e_] = S.inc(ins, "ACT")
                            block.scalar(sc_exp)
                            exp_act = S.c["ACT"]

                            def sc_nrm(vector, nks=nks, gate_a=exp_act):
                                vector.wait_ge(S.h["ACT"], gate_a)
                                ins = vector.reciprocal(out=finv[:], in_=fsum[:])
                                v = S.inc(ins, "DVE")
                                vector.wait_ge(S.h["DVE"], v)
                                ins = vector.tensor_scalar(out=Pf[:, 0:nks * 512],
                                                           in0=Pf[:, 0:nks * 512],
                                                           scalar1=finv[:, 0:1],
                                                           scalar2=None, op0=ALU.mult)
                                gate["nrm_last"] = S.inc(ins, "DVE")
                            block.vector(sc_nrm)
                            nrm_dve = S.c["DVE"]

                            def p_T(tensor, nks=nks, gate_d=nrm_dve):
                                tensor.wait_ge(S.h["DVE"], gate_d)
                                if gate["tev_act"]:
                                    tensor.wait_ge(S.h["ACT"], gate["tev_act"])
                                for j in range(4 * nks):
                                    bank = j % 4
                                    if j == 4:
                                        pass
                                    ins = tensor.matmul(ph_t[:, bank * 128:(bank + 1) * 128],
                                                        Pf[:, j * 128:(j + 1) * 128],
                                                        ident[:, :], start=(bank == 0),
                                                        stop=(bank == 3), is_transpose=True)
                                    if bank == 3:
                                        S.inc(ins, "PE")
                                        gpe = S.c["PE"]
                                        g0 = j - 3
                                        def pTev(scalar, g0=g0, gate_pe=gpe):
                                            scalar.wait_ge(S.h["PE"], gate_pe)
                                            ins = scalar.activation(
                                                out=PTf[:, g0 * 128:(g0 + 4) * 128],
                                                in_=ph_t[:], func=AF.Copy)
                                            gate["tev_act"] = S.inc(ins, "ACT")
                                        block.scalar(pTev)
                            for grp in range(nks):
                                def pT_g(tensor, grp=grp, nks=nks, gate_d=nrm_dve):
                                    tensor.wait_ge(S.h["DVE"], gate_d)
                                    if gate["tev_act"]:
                                        tensor.wait_ge(S.h["DVE"], gate["tev_act"])
                                    for j in range(4):
                                        kc = grp * 4 + j
                                        ins = tensor.matmul(
                                            ph_t[:, j * 128:(j + 1) * 128],
                                            Pf[:, kc * 128:(kc + 1) * 128],
                                            ident[:, :], start=(j == 0), stop=(j == 3),
                                            is_transpose=True)
                                    S.inc(ins, "PE")
                                block.tensor(pT_g)
                                gpe = S.c["PE"]

                                def pTev_g(vector, grp=grp, gate_pe=gpe):
                                    vector.wait_ge(S.h["PE"], gate_pe)
                                    ins = vector.tensor_copy(
                                        out=PTf[:, grp * 512:(grp + 1) * 512],
                                        in_=ph_t[:])
                                    gate["tev_act"] = S.inc(ins, "DVE")
                                block.vector(pTev_g)

                            ptev_act = S.c["DVE"]

                            gate["pf_pe"] = S.c["PE"]
                            def pv_mm(tensor, pr=pr, qt=qt, e_=e_, w_=w_, nks=nks,
                                      gate_a=ptev_act, first=(qt == 0 and e_ == 0)):
                                tensor.wait_ge(S.h["DVE"], gate_a)
                                if first and gate["pvv_act"]:
                                    tensor.wait_ge(S.h["DVE"], gate["pvv_act"])
                                h_ = pr * 2 + e_
                                nkc = 4 * nks
                                for kc in range(nkc):
                                    ks, kt = divmod(kc, 4)
                                    wp = (w_ - 1 + ks) % 2 if w_ >= 1 else w_ % 2
                                    pvbank = ph_pj if e_ == 0 else ph_s
                                    ins = tensor.matmul(
                                        pvbank[e_ * 64:(e_ + 1) * 64, qt * 128:(qt + 1) * 128],
                                        vr[:, wp * 2048 + kt * 512 + h_ * 64: wp * 2048 + kt * 512 + h_ * 64 + 64],
                                        PTf[:, kc * 128:(kc + 1) * 128],
                                        start=(kc == 0 and qt == 0),
                                        stop=(kc == nkc - 1 and qt == 3),
                                        tile_position=(0, e_ * 64))
                                S.inc(ins, "PE")
                            block.tensor(pv_mm)

                    pv_pe = S.c["PE"]
                    def pv_ev(vector, pr=pr, gate_pe=pv_pe):
                        vector.wait_ge(S.h["PE"], gate_pe)
                        vector.tensor_copy(out=oTf[0:64, pr * 512:(pr + 1) * 512],
                                           in_=ph_pj[0:64, :])
                        ins = vector.tensor_copy(out=oTf[64:128, pr * 512:(pr + 1) * 512],
                                                 in_=ph_s[64:128, :])
                        gate["pvv_act"] = S.inc(ins, "DVE")
                    block.vector(pv_ev)

                wo_act = S.c["DVE"]
                for cc in range(8):
                    def wo_mm(tensor, cc=cc, gate_a=wo_act):
                        tensor.wait_ge(S.h["DVE"], gate_a)
                        if cc >= 1 and gate["wo_dve"]:
                            tensor.wait_ge(S.h["DVE"], gate["wo_dve"] - 1 + 1 - 1) if False else None
                        if cc >= 1:
                            tensor.wait_ge(S.h["DVE"], gate["wo_dve"])
                        for pr in range(4):
                            ins = tensor.matmul(ph_pv[:, :],
                                                fwo_t[:, pr * 1024 + cc * 128: pr * 1024 + (cc + 1) * 128],
                                                oTf[:, pr * 512:(pr + 1) * 512],
                                                start=(pr == 0), stop=(pr == 3))
                        S.inc(ins, "PE")
                    block.tensor(wo_mm)
                    wo_pe = S.c["PE"]

                    def wo_add(vector, cc=cc, w_=w_, gate_pe=wo_pe):
                        vector.wait_ge(S.h["PE"], gate_pe)
                        dst = xs[:, cc * N + w_ * 512: cc * N + (w_ + 1) * 512]
                        ins = vector.scalar_tensor_tensor(out=dst, in0=ph_pv[:, :],
                                                          scalar=1.0, in1=dst,
                                                          op0=ALU.mult, op1=ALU.add)
                        gate["wo_dve"] = S.inc(ins, "DVE")
                    block.vector(wo_add)

            esFa.close()

            phase_barrier(f"mlp{layer}")
            esFm = ExitStack()
            sbM = lambda n_, shp, dt_=F32: esFm.enter_context(
                nc.sbuf_tensor(n_ + f"_l{layer}", shp, dt_))
            flng_t = sbM("flng_t", [128, 1024]); flnb_t = sbM("flnb_t", [128, 1024])
            fb1a_t = sbM("fb1a_t", [128, NJ]); fb1g_t = sbM("fb1g_t", [128, NJ])
            fb2_t = sbM("fb2_t", [128, 8])
            h2nT = sbM("h2nT", [128, 8 * 512])
            actT = sbM("actT", [128, NJ * 512])
            glt = sbM("glt", [128, 512]); att = sbM("att", [128, 512])
            w1b = [sbM(f"w1b{i}", [128, 2048]) for i in range(2)]
            w2b = [sbM(f"w2b{i}", [128, NJ * 128]) for i in range(2)]
            mstash = sbM("mstash", [128, 16])
            mh0 = sbM("mh0", [128, 512])
            mwt = sbM("mwt", [128, 512])

            def mlp_loads(sync, layer=layer):
                for dst, nm in [(flng_t, "flng"), (flnb_t, "flnb"), (fb1a_t, "fb1a"),
                                (fb1g_t, "fb1g"), (fb2_t, "fb2")]:
                    ins = sync.dma_start(out=dst[:], in_=din[nm][layer])
                    S.inc(ins, "WL", 16)
            block.sync(mlp_loads)
            mlp_wl = S.c["WL"]

            def mlp_prep(vector, layer=layer):
                vector.wait_ge(S.h["WL"], mlp_wl)
                vector.memset(mh0[:], 0.0)
                for i, cc in enumerate(range(4, 8)):
                    vector.tensor_copy(out=mh0[:, i * 128 + 1:(i + 1) * 128],
                                       in_=xs[:, cc * N: cc * N + 127])
                for s_ in range(1, 4):
                    for i, cc in enumerate(range(4, 8)):
                        ins = vector.tensor_copy(
                            out=mstash[:, (s_ - 1) * 4 + i:(s_ - 1) * 4 + i + 1],
                            in_=xs[:, cc * N + s_ * 512 - 1: cc * N + s_ * 512])
                S.inc(ins, "DVE")
            block.vector(mlp_prep)
            mprep_dve = S.c["DVE"]

            mgate = {"w1_pe": {}, "w2_pe": {}, "act_j": 0, "w2_stt": 0, "w1_wl": {},
                     "w2_wl": {}}

            for s_ in range(4):
                if s_ >= 1:
                    def mk_mwt(vector, s_=s_):
                        S.wait(vector, "PE")
                        for i, cc in enumerate(range(4, 8)):
                            vector.tensor_copy(
                                out=mwt[:, i * 128:i * 128 + 1],
                                in_=mstash[:, (s_ - 1) * 4 + i:(s_ - 1) * 4 + i + 1])
                            ins = vector.tensor_copy(
                                out=mwt[:, i * 128 + 1:(i + 1) * 128],
                                in_=xs[:, cc * N + s_ * 512: cc * N + s_ * 512 + 127])
                        S.inc(ins, "DVE")
                    block.vector(mk_mwt)
                mwt_dve = S.c["DVE"]

                def msrc(cc, ntr, s_=s_):
                    gt = 4 * s_ + ntr
                    if cc >= 4 and ntr == 0:
                        return mh0[:, (cc - 4) * 128:(cc - 4) * 128 + 128] if s_ == 0 else \
                            mwt[:, (cc - 4) * 128:(cc - 4) * 128 + 128]
                    if cc < 4:
                        return xs[:, cc * N + gt * 128: cc * N + (gt + 1) * 128]
                    return xs[:, cc * N + gt * 128 - 1: cc * N + (gt + 1) * 128 - 1]

                ln_state["first_gate"] = (S.c["DVE"], S.c["ACT"])
                for ntr in range(4):
                    emit_ln_tile(4 * s_ + ntr, gamma_beta=(flng_t[:], flnb_t[:]),
                                 write_back=True,
                                 src_fn=(lambda cc, ntr=ntr: msrc(cc, ntr)),
                                 back_dst_fn=(lambda cc, ntr=ntr:
                                              h2nT[:, cc * 512 + ntr * 128:
                                                   cc * 512 + (ntr + 1) * 128]))
                h2n_act = S.c["ACT"]

                for jj in range(NJ):
                    def w1_load(sync, jj=jj, s_=s_, layer=layer):
                        if jj >= 2 and (s_, jj - 2) in mgate["w1_pe"]:
                            sync.wait_ge(S.h["PE"], mgate["w1_pe"][(s_, jj - 2)])
                        elif jj < 2 and s_ > 0 and (s_ - 1, NJ - 2 + jj) in mgate["w1_pe"]:
                            sync.wait_ge(S.h["PE"], mgate["w1_pe"][(s_ - 1, NJ - 2 + jj)])
                        ins = sync.dma_start(
                            out=w1b[jj % 2][:].rearrange("p (t cc x) -> p t cc x", t=2, x=128),
                            in_=din["fw1"][layer, jj].transpose([2, 0, 1, 3]))
                        mgate["w1_wl"][(s_, jj)] = S.inc(ins, "WL", 16)
                    block.sync(w1_load)

                    def w1_mm(tensor, jj=jj, s_=s_, ga=h2n_act):
                        tensor.wait_ge(S.h["WL"], mgate["w1_wl"][(s_, jj)])
                        tensor.wait_ge(S.h["ACT"], ga)
                        if mgate["act_j"]:
                            tensor.wait_ge(S.h["ACT"], mgate["act_j"])
                        buf = w1b[jj % 2]
                        for t_ in range(2):
                            for cc in range(8):
                                ins = tensor.matmul(
                                    ph_a[:, t_ * 512:(t_ + 1) * 512],
                                    buf[:, t_ * 1024 + cc * 128: t_ * 1024 + (cc + 1) * 128],
                                    h2nT[:, cc * 512:(cc + 1) * 512],
                                    start=(cc == 0), stop=(cc == 7))
                        mgate["w1_pe"][(s_, jj)] = S.inc(ins, "PE")
                    block.tensor(w1_mm)
                    w1_pe = S.c["PE"]

                    def geluact(scalar, jj=jj, gate_pe=w1_pe):
                        scalar.wait_ge(S.h["PE"], gate_pe)
                        S.wait(scalar, "DVE")
                        scalar.activation(out=glt[:], in_=ph_a[:, 512:1024], func=AF.Gelu,
                                          bias=fb1g_t[:, jj:jj + 1])
                        ins = scalar.activation(out=att[:], in_=ph_a[:, 0:512],
                                                func=AF.Identity,
                                                bias=fb1a_t[:, jj:jj + 1])
                        mgate["act_j"] = S.inc(ins, "ACT")
                    block.scalar(geluact)
                    ga_act = S.c["ACT"]

                    def actmul(vector, jj=jj, gate_a=ga_act):
                        vector.wait_ge(S.h["ACT"], gate_a)
                        ins = vector.tensor_tensor(out=actT[:, jj * 512:(jj + 1) * 512],
                                                   in0=att[:], in1=glt[:], op=ALU.mult)
                        S.inc(ins, "DVE")
                    block.vector(actmul)
                act_dve = S.c["DVE"]

                for m_ in range(8):
                    def w2_load(sync, m_=m_, s_=s_, layer=layer):
                        if m_ >= 2 and (s_, m_ - 2) in mgate["w2_pe"]:
                            sync.wait_ge(S.h["PE"], mgate["w2_pe"][(s_, m_ - 2)])
                        elif m_ < 2 and s_ > 0 and (s_ - 1, 6 + m_) in mgate["w2_pe"]:
                            sync.wait_ge(S.h["PE"], mgate["w2_pe"][(s_ - 1, 6 + m_)])
                        ins = sync.dma_start(
                            out=w2b[m_ % 2][:].rearrange("p (j x) -> p j x", x=128),
                            in_=din["fw2"][layer, m_].transpose([1, 0, 2]))
                        mgate["w2_wl"][(s_, m_)] = S.inc(ins, "WL", 16)
                    block.sync(w2_load)

                    def w2_mm(tensor, m_=m_, s_=s_, ga=act_dve):
                        tensor.wait_ge(S.h["WL"], mgate["w2_wl"][(s_, m_)])
                        tensor.wait_ge(S.h["DVE"], ga)
                        if mgate["w2_stt"]:
                            tensor.wait_ge(S.h["DVE"], mgate["w2_stt"])
                        buf = w2b[m_ % 2]
                        for jj in range(NJ):
                            ins = tensor.matmul(ph_a[:, 1024:1536],
                                                buf[:, jj * 128:(jj + 1) * 128],
                                                actT[:, jj * 512:(jj + 1) * 512],
                                                start=(jj == 0), stop=(jj == NJ - 1))
                        mgate["w2_pe"][(s_, m_)] = S.inc(ins, "PE")
                    block.tensor(w2_mm)
                    w2_pe = S.c["PE"]

                    def w2_add(vector, m_=m_, s_=s_, gate_pe=w2_pe):
                        vector.wait_ge(S.h["PE"], gate_pe)
                        dst = xs[:, m_ * N + s_ * 512: m_ * N + (s_ + 1) * 512]
                        ins = vector.scalar_tensor_tensor(out=dst, in0=ph_a[:, 1024:1536],
                                                          scalar=1.0, in1=dst,
                                                          op0=ALU.mult, op1=ALU.add)
                        mgate["w2_stt"] = S.inc(ins, "DVE")
                    block.vector(w2_add)

            def b2add(vector):
                S.wait(vector, "PE")
                for m_ in range(8):
                    ins = vector.tensor_scalar(out=xs[:, m_ * N:(m_ + 1) * N],
                                               in0=xs[:, m_ * N:(m_ + 1) * N],
                                               scalar1=fb2_t[:, m_:m_ + 1],
                                               scalar2=None, op0=ALU.add)
                S.inc(ins, "DVE")
            block.vector(b2add)
            esFm.close()



        phase_barrier("ln2")
        esL2 = ExitStack()
        xinb = esL2.enter_context(nc.sbuf_tensor("xinb", [128, 2048], F32))
        xin_wl = {}
        for cc in range(8):
            def xin_load(sync, cc=cc):
                if cc >= 1:
                    S.wait(sync, "DVE")
                ins = sync.dma_start(out=xinb[:],
                                     in_=xin_sp[:, :].rearrange("(c p) n -> p c n", p=128)[:, cc])
                xin_wl[cc] = S.inc(ins, "WL", 16)
            block.sync(xin_load)

            def xin_add(vector, cc=cc):
                vector.wait_ge(S.h["WL"], xin_wl[cc])
                ins = vector.tensor_tensor(out=xs[:, cc * N:(cc + 1) * N],
                                           in0=xs[:, cc * N:(cc + 1) * N],
                                           in1=xinb[:], op=ALU.add)
                S.inc(ins, "DVE")
            block.vector(xin_add)

        ln_state["first_gate"] = (S.c["DVE"], S.c["ACT"])
        ln_state["prev_back_act"] = 0
        ln_state["prev_apply_pe"] = 0
        for nt in range(NT):
            emit_ln_tile(nt, write_back=False,
                         out_rows=out[nt * 128:(nt + 1) * 128, :])
        esL2.close()

        def final_wait(gpsimd):
            gpsimd.wait_ge(S.h["SP"], S.c["SP"])
        block.gpsimd(final_wait)

    print(f"build done {time.time()-t0:.1f}s n_inst={nc.n_instructions()}")
    return nc, din



def kernel(**inputs):
    shared = prep_shared(inputs)
    x = np.asarray(inputs["L0_attn_x"])
    nc, _ = build()
    in_maps = []
    for b in range(8):
        m = dict(shared)
        m["xT"] = np.ascontiguousarray(x[b].T)
        in_maps.append(m)
    res = run_bass_kernel_spmd(nc, in_maps, core_ids=list(range(8)))
    out = np.stack([np.asarray(res.results[b]["out"]) for b in range(8)], axis=0)
    return out.astype(np.float32)
